# version 07
# baseline (speedup 1.0000x reference)
import numpy as np
from contextlib import ExitStack

import concourse.bass as bass
import concourse.mybir as mybir
from concourse.bass_utils import run_bass_kernel_spmd

F32 = mybir.dt.float32
BF16 = mybir.dt.bfloat16
I32 = mybir.dt.int32
AF = mybir.ActivationFunctionType
ALU = mybir.AluOpType
AX = mybir.AxisListType

NT = 16
NA = 8
D = 1024
DC = 8
H = 16
FF = 4096
E = 8
EPS_LN = 1e-5
EPS_RMS = 1e-6
EPS_DEN = 1e-6

ENGINES = ("tensor", "vector", "scalar", "gpsimd", "sync")


class Prog:

    DMA_SEMS = ("dx0", "dx1", "dW", "dW2", "dF0", "dF1", "dA0", "dA1", "dO", "dCC")

    def __init__(self):
        self.stream = {e: [] for e in ENGINES}
        self.cnt = {}

    def do(self, eng, fn, waits=(), sem=None, inc=1, mode=None):
        for (ws, wv) in waits:
            if ws in self.DMA_SEMS:
                assert wv == self.cnt.get(ws, 0), \
                    f"prefix wait on dma sem {ws}: {wv} != {self.cnt.get(ws, 0)}"
        if sem is None:
            sem = eng
            inc = 1
        c = self.cnt.get(sem, 0) + inc
        self.cnt[sem] = c
        self.stream[eng].append((tuple(waits), fn, sem, inc, mode))
        return (sem, c)

    def drain(self):
        self.stream["tensor"].append(((), lambda e: e.drain(), None, 0, None))

    def wait_only(self, eng, waits):
        self.stream[eng].append((tuple(waits), None, None, 0, None))

    def tok(self, sem):
        return (sem, self.cnt.get(sem, 0))


def build_nc(debug=False):
    nc = bass.Bass("TRN2", target_bir_lowering=False)

    d_x = nc.dram_tensor("x", [2048, 1024], F32, kind="ExternalInput").ap()
    d_wqkv = nc.dram_tensor("wqkv", [1024, 3072], F32, kind="ExternalInput").ap()
    d_qn = nc.dram_tensor("qn", [1024], F32, kind="ExternalInput").ap()
    d_kn = nc.dram_tensor("kn", [1024], F32, kind="ExternalInput").ap()
    d_wout = nc.dram_tensor("wout", [1024, 1024], F32, kind="ExternalInput").ap()
    d_bo = nc.dram_tensor("bo", [1024], F32, kind="ExternalInput").ap()
    d_ew = nc.dram_tensor("ew", [1024, 2], F32, kind="ExternalInput").ap()
    d_ei = nc.dram_tensor("ei", [1024, 2], I32, kind="ExternalInput").ap()
    d_w1a = nc.dram_tensor("w1a", [512, 1024], F32, kind="ExternalInput").ap()
    d_w1b = nc.dram_tensor("w1b", [512, 4096], F32, kind="ExternalInput").ap()
    d_w2a = nc.dram_tensor("w2a", [512, 4096], F32, kind="ExternalInput").ap()
    d_w2b = nc.dram_tensor("w2b", [512, 1024], F32, kind="ExternalInput").ap()
    d_out = nc.dram_tensor("out", [2048, 1024], F32, kind="ExternalOutput").ap()

    d_ccin = nc.dram_tensor("cc_in", [64, 2080], F32)
    d_ccout = nc.dram_tensor("cc_out", [64, 2080], F32)
    d_gate = nc.dram_tensor("gate_scr", [8, 1024], F32)

    dbg = {}
    if debug:
        for name, shape in [
            ("ln1_0", [128, 1024]), ("q_str", [128, 16384]), ("k_0", [128, 1024]),
            ("v_0", [128, 1024]), ("cc_sb", [64, 2080]), ("kv_bd", [128, 2048]),
            ("sk", [128, 16]), ("recip0", [128, 16]), ("attn0", [128, 1024]),
            ("y1_0", [128, 1024]), ("xr_mid", [128, 16384]), ("t_sb", [128, 4096]),
            ("u_sb", [128, 4096]), ("wrep", [128, 4096]), ("wexp", [8, 1024]),
            ("qfm", [128, 1024]), ("ksbd", [128, 256]),
        ]:
            dbg[name] = nc.dram_tensor("dbg_" + name, shape, F32, kind="ExternalOutput").ap()

    with ExitStack() as ctx:
        sb = lambda name, shape, dt: ctx.enter_context(nc.sbuf_tensor(name, shape, dt))

        x_res = sb("x_res", [128, NT, 1024], F32)
        arena1 = sb("arena1", [128, 12288], F32)
        arena2 = sb("arena2", [128, 8192], F32)
        wout_sb = sb("wout_sb", [128, DC, 1024], BF16)
        w1aT = sb("w1aT", [128, DC, 512], BF16)
        h_sb = sb("h_sb", [128, 4, 1024], BF16)
        arena4 = sb("arena4", [128, 4096], BF16)
        arena3 = sb("arena3", [128, 2080], F32)
        ksum_bd = sb("ksum_bd", [128, 2, DC, 16], BF16)
        sk_sum = sb("sk_sum", [128, 16], F32)
        ident = sb("ident", [128, 128], BF16)
        ln1_tm = sb("ln1_tm", [128, 1024], BF16)
        ln1_fm = sb("ln1_fm", [128, 2, DC, 128], BF16)
        k_tile = sb("k_tile", [128, 1024], BF16)
        v_tile = sb("v_tile", [128, 1024], BF16)
        kn_rep = sb("kn_rep", [128, 1024], BF16)
        qn_fm = sb("qn_fm", [128, 8], F32)
        bo_bf = sb("bo_bf", [1, 1024], BF16)
        ones_r = sb("ones_r", [1, 128], BF16)
        ones_c = sb("ones_c", [128, 1], BF16)
        recips = sb("recips", [128, 32], F32)
        stats = sb("stats", [128, 16], F32)
        pc8 = sb("pc8", [8, 1], F32)
        cst = sb("cst", [128, 4], F32)

        ps = ctx.enter_context(nc.psum_tensor("ps", [128, 8, 512], F32))

        a1_bf = arena1.ap().bitcast(BF16)
        vw_qkv = a1_bf.rearrange("p (k f) -> p k f", k=DC)
        vw_w1b = a1_bf[:, 0:16384].rearrange("p (c f) -> p c f", c=4)
        vw_w2b = a1_bf[:, 16384:20480].rearrange("p (c f) -> p c f", c=4)
        vw_t = a1_bf[:, 20480:24576].rearrange("p (c f) -> p c f", c=4)

        a2_bf = arena2.ap().bitcast(BF16)
        vw_qstr = a2_bf.rearrange("p (t f) -> p t f", t=NT)
        vw_u = a2_bf[:, 0:4096].rearrange("p (c f) -> p c f", c=4)
        vw_sA = a2_bf[:, 4096:8192].rearrange("p (s c f) -> p s c f", s=2, c=4)

        vw_w1a_stage = h_sb.ap()

        vw_kvbd = arena4.ap()[:, 0:2048].rearrange("p (b f) -> p b f", b=2)
        vw_attn_tm = arena4.ap()[:, 2048:3072]
        vw_attn_fm = arena4.ap()[:, 3072:4096].rearrange("p (c f) -> p c f", c=DC)
        vw_wrep = arena4.ap().rearrange("p (c f) -> p c f", c=4)

        vw_cc = arena3.ap()[0:64, 0:2080]
        vw_y1 = arena3.ap()[:, 0:1024]
        vw_tmp = arena3.ap()[:, 1024:2048]
        vw_ewb = arena2.ap()[0:8, 2048:4096]
        vw_eib = arena2.ap()[0:8, 4096:6144].bitcast(I32)
        vw_eif = arena2.ap()[0:8, 6144:8192]
        vw_wexp = arena2.ap()[0:8, 4096:6144]

        q_fm = ln1_fm.ap()[:, 1, :, :]

        vw_w2aT = (x_res.ap()[:, 8:16, :].rearrange("p t f -> p (t f)")
                   .bitcast(BF16).rearrange("p (c f) -> p c f", c=32))

        def bank(i):
            return ps[:, i, :]

        def bankpair(i):
            return ps[:, i:i + 2, :].rearrange("p b f -> p (b f)")

        b0_bf = ps[:, 0, :].bitcast(BF16)

        semnames = ["tensor", "vector", "scalar", "gpsimd", "cc",
                    "dx0", "dx1", "dW", "dW2", "dF0", "dF1", "dA0", "dA1", "dO", "dCC"]
        sems = {n: ctx.enter_context(nc.semaphore(n)) for n in semnames}

        P = Prog()
        do = P.do

        out_t = d_out.rearrange("(t p) f -> t p f", p=128)
        x_t = d_x.rearrange("(t p) f -> p t f", p=128)

        def emit_ln_stats2(x_ap, w_sq=(), s0=0, scratch=None):
            if scratch is None:
                scratch = vw_tmp
            t_sq = do("scalar", lambda e: e.activation(
                scratch, x_ap, AF.Square, accum_out=stats[:, s0:s0 + 1]), waits=list(w_sq))
            t3 = do("scalar", lambda e: e.activation(stats[:, s0 + 2:s0 + 3], stats[:, s0 + 1:s0 + 2],
                                                     AF.Copy, scale=cst[:, 2:3]))
            t4 = do("scalar", lambda e: e.activation(stats[:, s0 + 3:s0 + 4], stats[:, s0 + 2:s0 + 3], AF.Square))
            t5 = do("vector", lambda e: e.tensor_scalar(
                stats[:, s0 + 4:s0 + 5], stats[:, s0:s0 + 1], cst[:, 2:3], stats[:, s0 + 3:s0 + 4],
                ALU.mult, ALU.subtract), waits=[t4])
            t6 = do("scalar", lambda e: e.activation(stats[:, s0 + 5:s0 + 6], stats[:, s0 + 4:s0 + 5],
                                                     AF.Sqrt, bias=cst[:, 0:1]), waits=[t5])
            t7 = do("vector", lambda e: e.reciprocal(stats[:, s0 + 6:s0 + 7], stats[:, s0 + 5:s0 + 6]),
                    waits=[t6])
            return t7

        def emit_ln_stats(x_ap, scratch, w_first=()):
            t1 = do("vector", lambda e: e.scalar_tensor_tensor(
                scratch, x_ap, 1.0, x_ap, ALU.mult, ALU.mult,
                accum_out=stats[:, 0:1]), waits=list(w_first))
            t2 = do("vector", lambda e: e.tensor_reduce(stats[:, 1:2], x_ap, AX.X, ALU.add))
            t3 = do("scalar", lambda e: e.activation(stats[:, 2:3], stats[:, 1:2], AF.Copy,
                                                     scale=cst[:, 2:3]), waits=[t2])
            t4 = do("scalar", lambda e: e.activation(stats[:, 3:4], stats[:, 2:3], AF.Square))
            t5 = do("vector", lambda e: e.tensor_scalar(
                stats[:, 4:5], stats[:, 0:1], cst[:, 2:3], stats[:, 3:4],
                ALU.mult, ALU.subtract), waits=[t4, t1])
            t6 = do("scalar", lambda e: e.activation(stats[:, 5:6], stats[:, 4:5], AF.Sqrt,
                                                     bias=cst[:, 0:1]), waits=[t5])
            t7 = do("vector", lambda e: e.reciprocal(stats[:, 6:7], stats[:, 5:6]), waits=[t6])
            return t7

        t_x0 = do("sync", lambda e: e.dma_start(out=x_res[:, 0:8, :], in_=x_t[:, 0:8, :]),
                  sem="dx0", inc=16)
        t_x1 = do("sync", lambda e: e.dma_start(out=x_res[:, 8:16, :], in_=x_t[:, 8:16, :]),
                  sem="dx1", inc=16)

        t_w1a_l = do("gpsimd", lambda e: e.dma_start(out=vw_w1a_stage, in_=d_w1a.rearrange("(c p) f -> p c f", p=128)),
                     sem="dA0", inc=16)
        wq_src = d_wqkv.rearrange("(k p) f -> p k f", p=128)
        t_wqkv1 = do("gpsimd", lambda e: e.dma_start(out=vw_qkv[:, :, 1024:2048], in_=wq_src[:, :, 1024:2048]),
                     sem="dW", inc=16)
        do("gpsimd", lambda e: e.dma_start(out=vw_qkv[:, :, 0:1024], in_=wq_src[:, :, 0:1024]),
           sem="dW2", inc=16)
        t_wqkv2 = do("gpsimd", lambda e: e.dma_start(out=vw_qkv[:, :, 2048:3072], in_=wq_src[:, :, 2048:3072]),
                     sem="dW2", inc=16)
        do("gpsimd", lambda e: e.dma_start(out=wout_sb[:, :, :], in_=d_wout.rearrange("(k p) f -> p k f", p=128)),
           sem="dF0", inc=16)
        do("gpsimd", lambda e: e.dma_start(out=kn_rep[:, :], in_=d_kn.unsqueeze(0).broadcast_to([128, 1024])),
           sem="dF0", inc=16)
        do("gpsimd", lambda e: e.dma_start(out=bo_bf[0:1, :], in_=d_bo.unsqueeze(0)),
           sem="dF0", inc=16)
        do("gpsimd", lambda e: e.dma_start(out=vw_eib, in_=d_ei.rearrange("n k -> (n k)").unsqueeze(0).broadcast_to([8, 2048])),
           sem="dF0", inc=16)
        do("gpsimd", lambda e: e.dma_start(out=qn_fm[:, :], in_=d_qn.rearrange("(c p) -> p c", p=128), allow_slow_non_contiguous=True),
           sem="dF0", inc=16)
        do("gpsimd", lambda e: e.dma_start(out=vw_ewb, in_=d_ew.rearrange("n k -> (n k)").unsqueeze(0).broadcast_to([8, 2048])),
           sem="dF0", inc=16)
        t_f0_all = P.tok("dF0")
        t_qn = t_f0_all
        t_kn = t_f0_all

        do("gpsimd", lambda e: e.memset(ident[:, :], 0.0))
        t_ident = do("gpsimd", lambda e: e.affine_select(
            out=ident[:, :], in_=ident[:, :], compare_op=ALU.not_equal, fill=1.0,
            base=0, pattern=[[-1, 128]], channel_multiplier=1))
        do("gpsimd", lambda e: e.memset(ones_r[0:1, :], 1.0))
        t_ones = do("gpsimd", lambda e: e.memset(ones_c[:, :], 1.0))
        t_pc = do("gpsimd", lambda e: e.iota(pc8[:, :], [[0, 1]], channel_multiplier=1,
                                             allow_small_or_imprecise_dtypes=True))
        t_ksbd0 = do("gpsimd", lambda e: e.memset(ksum_bd.ap().rearrange("p a b c -> p (a b c)"), 0.0))
        t_kvbd0 = do("gpsimd", lambda e: e.memset(arena4.ap()[:, 0:2048], 0.0))
        do("gpsimd", lambda e: e.memset(cst[:, 0:1], EPS_LN))
        do("gpsimd", lambda e: e.memset(cst[:, 1:2], EPS_RMS))
        t_cst = do("gpsimd", lambda e: e.memset(cst[:, 2:3], 1.0 / 1024.0))
        t_cc0 = do("vector", lambda e: e.memset(vw_cc, 0.0))

        t_eif = do("vector", lambda e: e.tensor_copy(vw_eif, vw_eib), waits=[t_f0_all])
        t_m = do("vector", lambda e: e.tensor_single_scalar(vw_eif, vw_eif, pc8[:, 0:1], ALU.is_equal),
                 waits=[t_eif, t_pc])
        t_mw = do("vector", lambda e: e.tensor_tensor(vw_wexp, vw_eif, vw_ewb, ALU.mult),
                  waits=[t_m])
        t_wexp = do("vector", lambda e: e.tensor_reduce(
            vw_wexp[:, 0:1024], vw_wexp.rearrange("p (n k) -> p n k", k=2), AX.X, ALU.add),
            waits=[t_mw])
        t_gate_st = do("gpsimd", lambda e: e.dma_start(out=d_gate.ap(), in_=vw_wexp[:, 0:1024]),
                       sem="dF1", inc=16, waits=[t_wexp])
        if debug:
            do("gpsimd", lambda e: e.dma_start(out=dbg["wexp"], in_=vw_wexp[:, 0:1024]),
               sem="dO", inc=16, waits=[t_wexp])

        blocks = [(rc, fc) for rc in range(4) for fc in range(8)]
        t_w1aT = None
        for grp in range(4):
            chunk = blocks[grp * 8:(grp + 1) * 8]
            for idx, (rc, fc) in enumerate(chunk):
                w = [t_w1a_l, t_ident]
                if t_w1aT is not None:
                    w.append(t_w1aT)
                do("tensor", lambda e, rc=rc, fc=fc, idx=idx: e.transpose(
                    b0_bf[:, idx * 128:(idx + 1) * 128],
                    vw_w1a_stage[:, rc, fc * 128:(fc + 1) * 128], ident[:, :]),
                    waits=w if idx == 0 else (), mode="T")
            t_tp = P.tok("tensor")
            for idx, (rc, fc) in enumerate(chunk):
                do("vector", lambda e, rc=rc, fc=fc, idx=idx: e.tensor_copy(
                    w1aT[:, fc, rc * 128:(rc + 1) * 128], b0_bf[:, idx * 128:(idx + 1) * 128]),
                    waits=[t_tp] if idx == 0 else ())
            t_w1aT = P.tok("vector")

        GRP_COL = [1024, 1536, 0, 512, 2048, 2560]
        evac_qkv = {}
        tT_evac = {}
        last_qkv = {}
        kv_consumed = None
        kv_add = {}
        sq_done = None
        q_scaled = {}
        ln1_used = {}
        apply_tok = {}

        def emit_ln_statsA(i):
            t_xl = t_x0 if i < 8 else t_x1
            t1 = do("vector", lambda e, i=i: e.scalar_tensor_tensor(
                vw_attn_tm, x_res[:, i, :], 1.0, x_res[:, i, :], ALU.mult, ALU.mult,
                accum_out=stats[:, 0:1]), waits=[t_xl, t_cst])
            t2 = do("vector", lambda e, i=i: e.tensor_reduce(
                stats[:, 1:2], x_res[:, i, :], AX.X, ALU.add))
            t3 = do("scalar", lambda e: e.activation(stats[:, 2:3], stats[:, 1:2], AF.Copy,
                                                     scale=cst[:, 2:3]), waits=[t2])
            t4 = do("scalar", lambda e: e.activation(stats[:, 3:4], stats[:, 2:3], AF.Square))
            return t1, t4

        def emit_ln_statsB(t1, t4):
            t5 = do("vector", lambda e: e.tensor_scalar(
                stats[:, 4:5], stats[:, 0:1], cst[:, 2:3], stats[:, 3:4],
                ALU.mult, ALU.subtract), waits=[t4, t1])
            t6 = do("scalar", lambda e: e.activation(stats[:, 5:6], stats[:, 4:5], AF.Sqrt,
                                                     bias=cst[:, 0:1]), waits=[t5])
            return t6

        def emit_ln_apply(i, t6):
            t7 = do("vector", lambda e: e.reciprocal(stats[:, 6:7], stats[:, 5:6]), waits=[t6])
            w_ap = [t7]
            if i >= 1:
                w_ap.append(ln1_used[i - 1])
            return do("vector", lambda e, i=i: e.tensor_scalar(
                ln1_tm[:, :], x_res[:, i, :], stats[:, 2:3], stats[:, 6:7],
                ALU.subtract, ALU.mult), waits=w_ap)

        _t1, _t4 = emit_ln_statsA(0)
        _t6 = emit_ln_statsB(_t1, _t4)
        apply_tok[0] = emit_ln_apply(0, _t6)

        for i in range(NT):
            wT = [apply_tok[i], t_ident]
            if i == 0:
                wT.append(t_w1aT)
            else:
                wT.append(tT_evac[i - 1])
            for j in range(DC):
                do("tensor", lambda e, j=j: e.transpose(
                    b0_bf[:, j * 128:(j + 1) * 128], ln1_tm[:, j * 128:(j + 1) * 128], ident[:, :]),
                    waits=wT if j == 0 else (), mode="T")
            t_tp = P.tok("tensor")
            ln1_used[i] = t_tp
            wev = [t_tp]
            if i >= 2:
                wev.append(last_qkv[i - 2])
            t_ev = do("vector", lambda e, i=i: e.tensor_copy(
                ln1_fm[:, i % 2, :, :].rearrange("p c f -> p (c f)"), b0_bf[:, :]), waits=wev)
            tT_evac[i] = t_ev
            if debug and i == NT - 1:
                do("gpsimd", lambda e: e.dma_start(out=dbg["ln1_0"], in_=ln1_tm[:, :]),
                   sem="dO", inc=16, waits=[apply_tok[i]])

            ln_state = None
            for g in range(6):
                bk = 1 + (g % 2)
                col = GRP_COL[g]
                wq = [t_ev, t_wqkv1 if g < 2 else t_wqkv2]
                prev = (i, g - 2) if g >= 2 else ((i - 1, g + 4) if i >= 1 else None)
                if prev is not None:
                    wq.append(evac_qkv[prev])
                for k in range(DC):
                    do("tensor", lambda e, k=k, col=col, bk=bk, i=i: e.matmul(
                        bank(bk), ln1_fm[:, i % 2, k, :], vw_qkv[:, k, col:col + 512],
                        start=(k == 0), stop=(k == DC - 1)),
                        waits=wq if k == 0 else (), mode="P")
                t_mm = P.tok("tensor")
                if g == 5:
                    last_qkv[i] = t_mm
                if g < 2:
                    dest = k_tile[:, (g % 2) * 512:(g % 2) * 512 + 512]
                    wv = [t_mm]
                    if kv_consumed is not None and g == 0:
                        wv.append(kv_consumed)
                elif g < 4:
                    dest = vw_qstr[:, i, (g % 2) * 512:(g % 2) * 512 + 512]
                    wv = [t_mm]
                    if i in (8, 9) and g == 2:
                        wv.append(t_gate_st)
                    if i >= 1 and g == 2:
                        wv.append(q_scaled[i - 1])
                else:
                    dest = v_tile[:, (g % 2) * 512:(g % 2) * 512 + 512]
                    wv = [t_mm]
                    if kv_consumed is not None and g == 4:
                        wv.append(kv_consumed)
                t_e = do("vector", lambda e, dest=dest, bk=bk: e.tensor_copy(dest, bank(bk)), waits=wv)
                evac_qkv[(i, g)] = t_e

                if g == 1:
                    t_sqk = do("scalar", lambda e: e.activation(
                        vw_attn_fm.rearrange("p c f -> p (c f)"), k_tile[:, :], AF.Square,
                        accum_out=recips[:, 19:20]),
                        waits=[evac_qkv[(i, 0)], evac_qkv[(i, 1)]])
                    sq_done = t_sqk
                    t_rk0 = do("scalar", lambda e: e.activation(
                        recips[:, 20:21], recips[:, 19:20], AF.Sqrt,
                        scale=cst[:, 2:3], bias=cst[:, 1:2]))
                    t_kkn = do("gpsimd", lambda e: e.tensor_tensor(
                        k_tile[:, :], k_tile[:, :], kn_rep[:, :], ALU.mult),
                        waits=[t_sqk, t_kn])
                    t_rk = do("vector", lambda e: e.reciprocal(recips[:, 21:22], recips[:, 20:21]),
                              waits=[t_rk0])
                    t_krelu = do("scalar", lambda e: e.activation(
                        k_tile[:, :], k_tile[:, :], AF.Relu, scale=recips[:, 21:22]),
                        waits=[t_rk, t_kkn])
                if g == 2 and i < NT - 1:
                    ln_state = emit_ln_statsA(i + 1)
                if g == 3:
                    do("scalar", lambda e, i=i: e.activation(
                        vw_attn_fm.rearrange("p c f -> p (c f)"), vw_qstr[:, i, :], AF.Square,
                        accum_out=recips[:, 16:17]),
                        waits=[evac_qkv[(i, 2)], evac_qkv[(i, 3)]])
                    t_rq0 = do("scalar", lambda e: e.activation(
                        recips[:, 17:18], recips[:, 16:17], AF.Sqrt,
                        scale=cst[:, 2:3], bias=cst[:, 1:2]))
                    t_rq = do("vector", lambda e: e.reciprocal(recips[:, 18:19], recips[:, 17:18]),
                              waits=[t_rq0])
                    q_scaled[i] = do("scalar", lambda e, i=i: e.activation(
                        vw_qstr[:, i, :], vw_qstr[:, i, :], AF.Copy, scale=recips[:, 18:19]),
                        waits=[t_rq])
                if g == 4 and ln_state is not None:
                    ln_state = ln_state + (emit_ln_statsB(*ln_state),)

            if debug and i == NT - 1:
                do("gpsimd", lambda e: e.dma_start(out=dbg["k_0"], in_=k_tile[:, :]),
                   sem="dO", inc=16, waits=[t_krelu])
                do("gpsimd", lambda e: e.dma_start(out=dbg["v_0"], in_=v_tile[:, :]),
                   sem="dO", inc=16, waits=[evac_qkv[(i, 5)]])

            b = i // 8
            kvp = 3 if i % 2 == 0 else 5
            ksc = 16 * (i % 2)
            wkv = [t_krelu, evac_qkv[(i, 4)], evac_qkv[(i, 5)], t_ones]
            if i >= 1:
                wkv.append(kv_add[i - 1])
            for h in range(H):
                do("tensor", lambda e, h=h, kvp=kvp: e.matmul(
                    ps[0:64, kvp + h // 8, (h % 8) * 64:(h % 8) * 64 + 64],
                    k_tile[:, h * 64:(h + 1) * 64], v_tile[:, h * 64:(h + 1) * 64],
                    start=True, stop=True),
                    waits=wkv if h == 0 else (), mode="P")
            for h in range(H):
                do("tensor", lambda e, h=h, ksc=ksc: e.matmul(
                    ps[0:64, 7, ksc + h:ksc + h + 1],
                    k_tile[:, h * 64:(h + 1) * 64], ones_c[:, 0:1],
                    start=True, stop=True), mode="P")
            kv_consumed = P.tok("tensor")
            if ln_state is not None:
                apply_tok[i + 1] = emit_ln_apply(i + 1, ln_state[2])
            do("vector", lambda e, b=b, kvp=kvp: e.tensor_tensor(
                vw_cc[:, b * 1024:(b + 1) * 1024],
                ps[0:64, kvp:kvp + 2, :].rearrange("p b2 f -> p (b2 f)"),
                vw_cc[:, b * 1024:(b + 1) * 1024], ALU.add),
                waits=[kv_consumed, t_cc0])
            t_a2 = do("vector", lambda e, b=b, ksc=ksc: e.tensor_tensor(
                vw_cc[:, 2048 + b * 16: 2048 + b * 16 + 16],
                ps[0:64, 7, ksc:ksc + 16],
                vw_cc[:, 2048 + b * 16: 2048 + b * 16 + 16], ALU.add))
            kv_add[i] = t_a2

        qkv_done = last_qkv[NT - 1]

        t_ccready = kv_add[NT - 1]
        if debug:
            do("gpsimd", lambda e: e.dma_start(out=dbg["cc_sb"], in_=vw_cc),
               sem="dO", inc=16, waits=[t_ccready])
            do("gpsimd", lambda e: e.dma_start(out=dbg["q_str"], in_=vw_qstr.rearrange("p t f -> p (t f)")),
               sem="dO", inc=16, waits=[q_scaled[NT - 1]])

        t_ccst = do("sync", lambda e: e.dma_start(out=d_ccin.ap(), in_=vw_cc),
                    sem="dCC", inc=16, waits=[t_ccready])
        t_cc = do("gpsimd", lambda e: e.collective_compute(
            "AllReduce", ALU.add,
            replica_groups=[[0, 1], [2, 3], [4, 5], [6, 7]],
            ins=[d_ccin.ap().opt()], outs=[d_ccout.ap().opt()]),
            sem="cc", inc=1, waits=[t_ccst])
        kvbd_even = arena4.ap()[0:64, 0:2048].rearrange("p (b2 j f) -> p b2 j f", b2=2, j=8)
        kvbd_odd = arena4.ap()[64:128, 0:2048].rearrange("p (b2 j f) -> p b2 j f", b2=2, j=8)
        cc_kv = d_ccout.ap()[0:64, 0:2048].rearrange("p (b2 j two f) -> p b2 j two f", b2=2, j=8, two=2)
        do("gpsimd", lambda e: e.dma_start(out=kvbd_even[:, :, :, 0:64], in_=cc_kv[:, :, :, 0, :]),
           sem="dCC", inc=16, waits=[t_cc, t_kvbd0])
        do("gpsimd", lambda e: e.dma_start(out=kvbd_odd[:, :, :, 64:128], in_=cc_kv[:, :, :, 1, :]),
           sem="dCC", inc=16)
        cc_ks = d_ccout.ap()[0:64, 2048:2080].rearrange("p (b2 j two) -> p b2 j two", b2=2, two=2)
        do("sync", lambda e: e.dma_start(
            out=sk_sum[0:64, :].rearrange("p (b2 j) -> p b2 j", b2=2),
            in_=cc_ks[:, :, :, 0], allow_slow_non_contiguous=True), sem="dCC", inc=16, waits=[t_cc])
        do("sync", lambda e: e.dma_start(
            out=sk_sum[64:128, :].rearrange("p (b2 j) -> p b2 j", b2=2),
            in_=cc_ks[:, :, :, 1], allow_slow_non_contiguous=True), sem="dCC", inc=16)
        t_cc_all = P.tok("dCC")
        t_kvbd = t_cc_all
        t_sk = t_cc_all
        for b in range(2):
            for j in range(8):
                do("vector", lambda e, b=b, j=j: e.tensor_copy(
                    ksum_bd[0:64, b, j, 2 * j: 2 * j + 1], sk_sum[0:64, 8 * b + j: 8 * b + j + 1]),
                    waits=[t_sk, t_ksbd0] if (b == 0 and j == 0) else ())
                do("vector", lambda e, b=b, j=j: e.tensor_copy(
                    ksum_bd[64:128, b, j, 2 * j + 1: 2 * j + 2], sk_sum[64:128, 8 * b + j: 8 * b + j + 1]))
        t_ksbd = P.tok("vector")
        if debug:
            do("gpsimd", lambda e: e.dma_start(out=dbg["kv_bd"], in_=vw_kvbd.rearrange("p b f -> p (b f)")),
               sem="dO", inc=16, waits=[t_kvbd, t_ksbd])
            do("gpsimd", lambda e: e.dma_start(out=dbg["sk"], in_=sk_sum[:, :]),
               sem="dO", inc=16, waits=[t_sk])
            do("gpsimd", lambda e: e.dma_start(out=dbg["ksbd"], in_=ksum_bd.ap().rearrange("p a b c -> p (a b c)")),
               sem="dO", inc=16, waits=[t_ksbd])

        b0_free = tT_evac[NT - 1]
        qfm_used = None
        attn_evac = None
        t_evac_t = None
        wout_evac = {}
        norm_read = None
        t_store = {}
        attn_div_prev = None
        last_wout = None
        attn_all_done = None
        pend = None

        for i in range(NT):
            b = i // 8
            wT = [q_scaled[i], b0_free, t_kvbd, t_ksbd]
            if i == 0:
                wT.append(qkv_done)
            if qfm_used is not None:
                wT.append(qfm_used)
            for j in range(DC):
                do("tensor", lambda e, i=i, j=j: e.transpose(
                    b0_bf[:, j * 128:(j + 1) * 128], vw_qstr[:, i, j * 128:(j + 1) * 128], ident[:, :]),
                    waits=wT if j == 0 else (), mode="T")
            t_qT = P.tok("tensor")
            for j in range(DC):
                do("vector", lambda e, j=j: e.tensor_scalar(
                    q_fm[:, j, :], b0_bf[:, j * 128:(j + 1) * 128],
                    qn_fm[:, j:j + 1], 0.0, ALU.mult, ALU.max),
                    waits=[t_qT, t_qn] if j == 0 else ())
            t_qfm = P.tok("vector")
            b0_free = t_qfm
            if debug and i == NT - 1:
                do("gpsimd", lambda e: e.dma_start(out=dbg["qfm"], in_=q_fm.rearrange("p c f -> p (c f)")),
                   sem="dO", inc=16, waits=[t_qfm])

            wN = [t_qfm, t_ksbd]
            if norm_read is not None:
                wN.append(norm_read)
            if i == 0:
                wN.append(kv_add[NT - 1])
            for j in range(DC):
                do("tensor", lambda e, j=j, b=b: e.matmul(
                    bank(7)[:, 0:16], q_fm[:, j, :], ksum_bd[:, b, j, :],
                    start=(j == 0), stop=(j == DC - 1)),
                    waits=wN if j == 0 else (), mode="P")
            t_norm = P.tok("tensor")
            wA = [t_kvbd]
            if t_evac_t is not None:
                wA.append(t_evac_t)
            elif attn_evac is not None:
                wA.append(attn_evac)
            if i == 0:
                wA += [evac_qkv[(NT - 1, 4)], evac_qkv[(NT - 1, 5)]]
            for j in range(DC):
                do("tensor", lambda e, j=j, b=b: e.matmul(
                    bankpair(1)[:, j * 128:(j + 1) * 128], q_fm[:, j, :], vw_kvbd[:, b, j * 128:(j + 1) * 128],
                    start=True, stop=True),
                    waits=wA if j == 0 else (), mode="P")
            t_attnmm = P.tok("tensor")
            qfm_used = t_attnmm
            if i == NT - 1:
                attn_all_done = t_attnmm

            t_eps = do("scalar", lambda e: e.activation(
                recips[:, 16:32], bank(7)[:, 0:16], AF.Identity, bias=cst[:, 1:2]),
                waits=[t_norm])
            norm_read = t_eps
            t_rec = do("vector", lambda e: e.reciprocal(recips[:, 0:16], recips[:, 16:32]),
                       waits=[t_eps])
            if debug and i == NT - 1:
                do("gpsimd", lambda e: e.dma_start(out=dbg["recip0"], in_=recips[:, 0:16]),
                   sem="dO", inc=16, waits=[t_rec])

            wD = [t_attnmm, t_rec]
            if i == 0:
                wD.append(sq_done)
            if attn_div_prev is not None:
                wD.append(attn_div_prev)
            t_div = do("vector", lambda e: e.tensor_tensor(
                vw_attn_tm.rearrange("p (h d) -> p h d", d=64),
                bankpair(1).rearrange("p (h d) -> p h d", d=64),
                recips[:, 0:16].unsqueeze(-1).broadcast_to([128, 16, 64]),
                ALU.mult), waits=wD)
            attn_evac = t_div
            if debug and i == NT - 1:
                do("gpsimd", lambda e: e.dma_start(out=dbg["attn0"], in_=vw_attn_tm),
                   sem="dO", inc=16, waits=[t_div])

            for j in range(DC):
                do("tensor", lambda e, j=j: e.transpose(
                    b0_bf[:, j * 128:(j + 1) * 128], vw_attn_tm[:, j * 128:(j + 1) * 128], ident[:, :]),
                    waits=[t_div, b0_free] if j == 0 else (), mode="T")
            t_aT = P.tok("tensor")
            t_afm = do("vector", lambda e: e.tensor_copy(
                vw_attn_fm.rearrange("p c f -> p (c f)"), b0_bf[:, :]),
                waits=[t_aT] + ([sq_done] if i == 0 else []))
            attn_div_prev = t_afm
            b0_free = t_afm

            wb = 3 if i % 2 == 0 else 5
            wW = [t_afm, t_f0_all, t_ones]
            if i < 2:
                wW.append(kv_add[NT - 1])
            elif (i - 2) in wout_evac:
                wW.append(wout_evac[i - 2])
            for half in range(2):
                for j in range(DC):
                    do("tensor", lambda e, j=j, half=half, wb=wb: e.matmul(
                        bank(wb + half), vw_attn_fm[:, j, :], wout_sb[:, j, half * 512:(half + 1) * 512],
                        start=(j == 0), stop=False),
                        waits=wW if (j == 0 and half == 0) else (), mode="P")
                do("tensor", lambda e, half=half, wb=wb: e.matmul(
                    bank(wb + half), ones_r[0:1, :], bo_bf[0:1, half * 512:(half + 1) * 512],
                    start=False, stop=True), mode="P")
            t_wout = P.tok("tensor")
            last_wout = t_wout

            if pend is not None:
                pi, t_ln2p = pend
                for j in range(DC):
                    do("tensor", lambda e, j=j: e.transpose(
                        b0_bf[:, j * 128:(j + 1) * 128], ln1_tm[:, j * 128:(j + 1) * 128], ident[:, :]),
                        waits=[t_ln2p, b0_free] if j == 0 else (), mode="T")
                t_l2T = P.tok("tensor")
                l2T_done = t_l2T
                t_l2fm = do("vector", lambda e: e.tensor_copy(
                    ln1_fm[:, 0, :, :].rearrange("p c f -> p (c f)"), b0_bf[:, :]), waits=[t_l2T])
                b0_free = t_l2fm
                wt_ = [t_l2fm, t_div, t_w1aT]
                for m in range(4):
                    for k in range(DC):
                        do("tensor", lambda e, m=m, k=k: e.matmul(
                            bankpair(1)[:, m * 128:(m + 1) * 128], w1aT[:, k, m * 128:(m + 1) * 128],
                            ln1_fm[:, 0, k, :], start=(k == 0), stop=(k == DC - 1)),
                            waits=wt_ if (m == 0 and k == 0) else (), mode="P")
                t_tmm = P.tok("tensor")
                wev = [t_tmm]
                if pi == 0:
                    wev.append(qkv_done)
                t_evac_t = do("vector", lambda e, pi=pi: e.tensor_copy(
                    vw_t[:, :, pi * 128:(pi + 1) * 128],
                    bankpair(1)[:, 0:512].rearrange("p (m n) -> p m n", n=128)), waits=wev)
                pend = None

            wY = [t_wout]
            if i == 0:
                wY.append(t_cc_all)
            t_y1 = do("vector", lambda e, wb=wb: e.tensor_scalar(
                vw_y1, bankpair(wb), 1.0, 0.0, ALU.mult, ALU.add,
                accum_out=stats[:, 1:2]), waits=wY)
            wout_evac[i] = t_y1
            if debug and i == NT - 1:
                do("gpsimd", lambda e: e.dma_start(out=dbg["y1_0"], in_=vw_y1),
                   sem="dO", inc=16, waits=[t_y1])
            t_rs = emit_ln_stats2(vw_y1, [t_y1])
            t_lnA = do("vector", lambda e: e.tensor_scalar(
                vw_tmp, vw_y1, stats[:, 2:3], stats[:, 6:7], ALU.subtract, ALU.mult),
                waits=[t_rs])
            t_res = do("vector", lambda e, i=i: e.scalar_tensor_tensor(
                x_res[:, i, :], x_res[:, i, :], 1.0, vw_tmp, ALU.mult, ALU.add,
                accum_out=stats[:, 9:10]), waits=[t_lnA])

            if i >= 8:
                t_store[i] = do("sync", lambda e, i=i: e.dma_start(out=out_t[i], in_=x_res[:, i, :]),
                                sem="dO", inc=16, waits=[t_res])
            else:
                t_rs2 = emit_ln_stats2(x_res[:, i, :], [t_res], s0=8)
                w_l2 = [t_rs2]
                if i == 0:
                    w_l2.append(ln1_used[NT - 1])
                else:
                    w_l2.append(l2T_done)
                t_ln2 = do("vector", lambda e, i=i: e.tensor_scalar(
                    ln1_tm[:, :], x_res[:, i, :], stats[:, 10:11], stats[:, 14:15], ALU.subtract, ALU.mult),
                    waits=w_l2)
                pend = (i, t_ln2)

            if i == NT - 1 and pend is not None:
                pass

        t_t_all = t_evac_t
        if debug:
            do("gpsimd", lambda e: e.dma_start(
                out=dbg["xr_mid"], in_=x_res.ap().rearrange("p t f -> p (t f)")),
                sem="dO", inc=16, waits=[P.tok("vector")])

        do("gpsimd", lambda e: e.dma_start(
            out=vw_w1b, in_=d_w1b.rearrange("(c p) f -> p c f", p=128)),
            sem="dF1", inc=16, waits=[qkv_done])
        do("gpsimd", lambda e: e.dma_start(
            out=vw_w2b, in_=d_w2b.rearrange("(c p) f -> p c f", p=128)),
            sem="dF1", inc=16)
        for cpair in range(4):
            for e2 in range(2):
                do("gpsimd", lambda e, cpair=cpair, e2=e2: e.dma_start(
                    out=vw_wrep[e2 * 64:(e2 + 1) * 64, cpair, :],
                    in_=d_gate.ap()[2 * cpair + e2: 2 * cpair + e2 + 1, :].broadcast_to([64, 1024])),
                    sem="dF1", inc=16,
                    waits=([P.tok("dF1"), last_wout, attn_all_done]
                           + ([P.tok("dO")] if debug else [])) if (cpair == 0 and e2 == 0) else ())
        t_f1_all = P.tok("dF1")
        if debug:
            do("gpsimd", lambda e: e.dma_start(out=dbg["wrep"], in_=vw_wrep.rearrange("p c f -> p (c f)")),
               sem="dO", inc=16, waits=[t_f1_all])
            do("gpsimd", lambda e: e.dma_start(out=dbg["t_sb"], in_=vw_t.rearrange("p c f -> p (c f)")),
               sem="dO", inc=16, waits=[t_t_all])

        stores_done = P.tok("dO")
        t_sA = {}
        sA_used = {}
        b0f = b0_free
        for c8 in range(8):
            slot = c8 % 2
            wL = [qfm_used]
            if c8 >= 2:
                wL.append(sA_used[c8 - 2])
            t_sA[c8] = do("gpsimd", lambda e, c8=c8, slot=slot: e.dma_start(
                out=vw_sA[:, slot, :, :].rearrange("p c f -> p (c f)"),
                in_=d_w2a.rearrange("(c p) f -> p c f", p=128)[:, :, c8 * 512:(c8 + 1) * 512]),
                sem="dA0" if slot == 0 else "dA1", inc=16, waits=wL)
            blocks = [(rc, fc) for rc in range(4) for fc in range(4)]
            for halfg in range(2):
                chunk = blocks[halfg * 8:(halfg + 1) * 8]
                for idx, (rc, fc) in enumerate(chunk):
                    w = [t_sA[c8], b0f] if idx == 0 else ()
                    do("tensor", lambda e, slot=slot, rc=rc, fc=fc, idx=idx: e.transpose(
                        b0_bf[:, idx * 128:(idx + 1) * 128],
                        vw_sA[:, slot, rc, fc * 128:(fc + 1) * 128], ident[:, :]),
                        waits=w, mode="T")
                t_tp = P.tok("tensor")
                for idx, (rc, fc) in enumerate(chunk):
                    do("vector", lambda e, c8=c8, rc=rc, fc=fc, idx=idx: e.tensor_copy(
                        vw_w2aT[:, c8 * 4 + fc, rc * 128:(rc + 1) * 128],
                        b0_bf[:, idx * 128:(idx + 1) * 128]),
                        waits=[t_tp, stores_done] if idx == 0 else ())
                b0f = P.tok("vector")
            sA_used[c8] = b0f
        t_w2aT = b0f

        u_evac = {}
        gel = {}
        umm = {}
        u_free = None
        first_h = True
        seq = [(c, nb, Fc) for c in range(4) for nb in range(2) for Fc in range(32)]

        def emit_u(c, nb, Fc):
            w = [gel[(c, nb, Fc)]]
            if Fc == 0:
                w.append(u_free if u_free is not None else wout_evac[15])
            u_start = (Fc == 0)
            u_stop = (Fc == 31)
            do("tensor", lambda e, c=c, Fc=Fc, s=u_start, st=u_stop: e.matmul(
                ps[0:64, 5, :], vw_w2aT[:, Fc, (2 * c) * 64:(2 * c) * 64 + 64],
                h_sb[:, Fc % 4, 0:512],
                start=s, stop=st, tile_position=(0, 0)),
                waits=w, mode="C")
            do("tensor", lambda e, c=c, Fc=Fc, s=u_start, st=u_stop: e.matmul(
                ps[64:128, 5, :], vw_w2aT[:, Fc, (2 * c + 1) * 64:(2 * c + 1) * 64 + 64],
                h_sb[:, Fc % 4, 512:1024],
                start=s, stop=st, tile_position=(0, 64)),
                mode="C")
            umm[(c, nb, Fc)] = P.tok("tensor")

        def emit_u_evac(c, nb):
            nonlocal_u = [umm[(c, nb, 31)], t_f1_all, qfm_used]
            u_evac[(c, nb)] = do("vector", lambda e, c=c, nb=nb: e.tensor_tensor(
                vw_u[:, c, nb * 512:(nb + 1) * 512], bank(5),
                vw_wrep[:, c, nb * 512:(nb + 1) * 512], ALU.mult), waits=nonlocal_u)
            return u_evac[(c, nb)]

        for s, (c, nb, Fc) in enumerate(seq):
            hp = 3 if Fc % 2 == 0 else 6
            w = [t_f1_all, t_w2aT]
            if first_h:
                w += [t_t_all, wout_evac[14], wout_evac[15], norm_read]
                first_h = False
            if (c, nb, Fc - 2) in gel:
                w.append(gel[(c, nb, Fc - 2)])
            elif Fc <= 1 and (c, nb) != (0, 0):
                prev = (c, nb - 1) if nb == 1 else (c - 1, 1)
                w.append(gel[prev + (30 + Fc,)])
            do("tensor", lambda e, c=c, nb=nb, Fc=Fc, hp=hp: e.matmul(
                bank(hp), vw_w1b[0:64, c, Fc * 128:(Fc + 1) * 128],
                vw_t[0:64, c, nb * 512:(nb + 1) * 512],
                start=True, stop=True, tile_position=(0, 0)),
                waits=w, mode="R")
            do("tensor", lambda e, c=c, nb=nb, Fc=Fc, hp=hp: e.matmul(
                bank(hp + 1), vw_w1b[64:128, c, Fc * 128:(Fc + 1) * 128],
                vw_t[64:128, c, nb * 512:(nb + 1) * 512],
                start=True, stop=True, tile_position=(64, 0)),
                mode="R")
            t_h = P.tok("tensor")
            wg = [t_h]
            if (c, nb, Fc - 4) in umm:
                wg.append(umm[(c, nb, Fc - 4)])
            gel[(c, nb, Fc)] = do("scalar", lambda e, Fc=Fc, hp=hp: e.activation(
                h_sb[:, Fc % 4, :], bankpair(hp), AF.Gelu), waits=wg)
            if s > 0:
                pc_, pn_, pf_ = seq[s - 1]
                emit_u(pc_, pn_, pf_)
                if pf_ == 31:
                    u_free = emit_u_evac(pc_, pn_)
        emit_u(*seq[-1])
        u_free = emit_u_evac(3, 1)
        t_u_all = P.tok("vector")
        if debug:
            do("gpsimd", lambda e: e.dma_start(out=dbg["u_sb"], in_=vw_u.rearrange("p c f -> p (c f)")),
               sem="dO", inc=16, waits=[t_u_all])

        y_prev = None
        for nt in range(NA):
            wy = [t_u_all]
            if y_prev is not None:
                wy.append(y_prev)
            for fh in range(2):
                if not (nt == 0 and fh == 0):
                    P.drain()
                for e_ in range(E):
                    half = e_ % 2
                    bk = (1 + fh) if half == 0 else (3 + fh)
                    s = (e_ < 2)
                    st = (e_ >= E - 2)
                    do("tensor", lambda en, e_=e_, fh=fh, bk=bk, nt=nt, s=s, st=st: en.matmul(
                        bank(bk), vw_u[(e_ % 2) * 64:(e_ % 2) * 64 + 64, e_ // 2, nt * 128:(nt + 1) * 128],
                        vw_w2b[(e_ % 2) * 64:(e_ % 2) * 64 + 64, e_ // 2, fh * 512:(fh + 1) * 512],
                        start=s, stop=st, tile_position=((e_ % 2) * 64, 0)),
                        waits=wy if (e_ == 0 and fh == 0) else (), mode="R")
            t_ymm = P.tok("tensor")
            do("vector", lambda e: e.tensor_copy(vw_y1, bankpair(1)), waits=[t_ymm])
            t_c2 = do("vector", lambda e: e.scalar_tensor_tensor(
                vw_y1, bankpair(3), 1.0, vw_y1, ALU.mult, ALU.add,
                accum_out=stats[:, 1:2]))
            y_prev = t_c2
            t_rs = emit_ln_stats2(vw_y1, [t_c2])
            t_lnA = do("vector", lambda e: e.tensor_scalar(
                vw_tmp, vw_y1, stats[:, 2:3], stats[:, 6:7], ALU.subtract, ALU.mult),
                waits=[t_rs])
            t_res = do("vector", lambda e, nt=nt: e.scalar_tensor_tensor(
                x_res[:, nt, :], x_res[:, nt, :], 1.0, vw_tmp, ALU.mult, ALU.add), waits=[t_lnA])
            t_store[nt] = do("sync", lambda e, nt=nt: e.dma_start(out=out_t[nt], in_=x_res[:, nt, :]),
                             sem="dO", inc=16, waits=[t_res])

        P.wait_only("sync", [P.tok("dO")])

        block = ctx.enter_context(nc.Block())
        for eng_name in ENGINES:
            ops = P.stream[eng_name]
            if not ops:
                continue

            def body(eng, ops=ops, eng_name=eng_name):
                seen = {}
                last_mode = None
                for waits, fn, sem, inc, mode in ops:
                    if eng_name == "tensor" and mode is not None:
                        if last_mode is not None and mode != last_mode:
                            eng.drain()
                        last_mode = mode
                    for (ws, wv) in waits:
                        if ws == eng_name or wv <= 0:
                            continue
                        if seen.get(ws, 0) >= wv:
                            continue
                        eng.wait_ge(sems[ws], wv)
                        seen[ws] = wv
                    if fn is None:
                        continue
                    inst = fn(eng)
                    if inst is not None and sem is not None:
                        inst.then_inc(sems[sem], inc)

            getattr(block, eng_name)(body)

    return nc


_CACHE = {}


def _get_nc(debug=False):
    key = bool(debug)
    if key not in _CACHE:
        _CACHE[key] = build_nc(debug=key)
    return _CACHE[key]


def kernel(L0_attn_x, L0_attn_W_qkv, L0_attn_qn_w, L0_attn_kn_w, L0_attn_W_out,
           L0_attn_b_out, L0_ffn_expert_weights, L0_ffn_expert_indices,
           L0_ffn_w1_A, L0_ffn_w1_B, L0_ffn_w2_A, L0_ffn_w2_B, _debug=False):
    x = np.ascontiguousarray(np.asarray(L0_attn_x, dtype=np.float32))
    shared = dict(
        wqkv=np.ascontiguousarray(np.asarray(L0_attn_W_qkv, dtype=np.float32)),
        qn=np.ascontiguousarray(np.asarray(L0_attn_qn_w, dtype=np.float32)),
        kn=np.ascontiguousarray(np.asarray(L0_attn_kn_w, dtype=np.float32)),
        wout=np.ascontiguousarray(np.asarray(L0_attn_W_out, dtype=np.float32)),
        bo=np.ascontiguousarray(np.asarray(L0_attn_b_out, dtype=np.float32)),
        w1a=np.ascontiguousarray(np.asarray(L0_ffn_w1_A, dtype=np.float32)),
        w1b=np.ascontiguousarray(np.asarray(L0_ffn_w1_B, dtype=np.float32)),
        w2a=np.ascontiguousarray(np.asarray(L0_ffn_w2_A, dtype=np.float32)),
        w2b=np.ascontiguousarray(np.asarray(L0_ffn_w2_B, dtype=np.float32)),
    )
    ew = np.ascontiguousarray(np.asarray(L0_ffn_expert_weights, dtype=np.float32))
    ei = np.ascontiguousarray(np.asarray(L0_ffn_expert_indices, dtype=np.int32))

    in_maps = []
    for c in range(8):
        bA = c // 2
        bB = 4 + c // 2
        r0 = (c % 2) * 1024
        xc = np.concatenate([x[bA, r0:r0 + 1024], x[bB, r0:r0 + 1024]], axis=0)
        g0 = bA * 2048 + r0
        m = dict(shared)
        m["x"] = np.ascontiguousarray(xc)
        m["ew"] = np.ascontiguousarray(ew[g0:g0 + 1024])
        m["ei"] = np.ascontiguousarray(ei[g0:g0 + 1024])
        in_maps.append(m)

    nc = _get_nc(debug=_debug)
    res = run_bass_kernel_spmd(nc, in_maps, core_ids=list(range(8)))

    y = np.empty((8, 2048, 1024), dtype=np.float32)
    for c in range(8):
        o = np.asarray(res.results[c]["out"], dtype=np.float32)
        bA = c // 2
        bB = 4 + c // 2
        r0 = (c % 2) * 1024
        y[bA, r0:r0 + 1024] = o[0:1024]
        y[bB, r0:r0 + 1024] = o[1024:2048]
    if _debug:
        return y, [res.results[c] for c in range(8)]
    return y


# version 08
# speedup vs baseline: 1.0075x; 1.0075x over previous
import numpy as np
from contextlib import ExitStack

import concourse.bass as bass
import concourse.mybir as mybir
from concourse.bass_utils import run_bass_kernel_spmd

F32 = mybir.dt.float32
BF16 = mybir.dt.bfloat16
I32 = mybir.dt.int32
AF = mybir.ActivationFunctionType
ALU = mybir.AluOpType
AX = mybir.AxisListType

NT = 16
NA = 8
D = 1024
DC = 8
H = 16
FF = 4096
E = 8
EPS_LN = 1e-5
EPS_RMS = 1e-6
EPS_DEN = 1e-6

ENGINES = ("tensor", "vector", "scalar", "gpsimd", "sync")


class Prog:

    DMA_SEMS = ("dx0", "dx1", "dW", "dW2", "dF0", "dF1", "dA0", "dA1", "dO", "dCC")

    def __init__(self):
        self.stream = {e: [] for e in ENGINES}
        self.cnt = {}

    def do(self, eng, fn, waits=(), sem=None, inc=1, mode=None):
        for (ws, wv) in waits:
            if ws in self.DMA_SEMS:
                assert wv == self.cnt.get(ws, 0), \
                    f"prefix wait on dma sem {ws}: {wv} != {self.cnt.get(ws, 0)}"
        if sem is None:
            sem = eng
            inc = 1
        c = self.cnt.get(sem, 0) + inc
        self.cnt[sem] = c
        self.stream[eng].append((tuple(waits), fn, sem, inc, mode))
        return (sem, c)

    def drain(self):
        self.stream["tensor"].append(((), lambda e: e.drain(), None, 0, None))

    def wait_only(self, eng, waits):
        self.stream[eng].append((tuple(waits), None, None, 0, None))

    def tok(self, sem):
        return (sem, self.cnt.get(sem, 0))


def build_nc(debug=False):
    nc = bass.Bass("TRN2", target_bir_lowering=False)

    d_x = nc.dram_tensor("x", [128, 16384], F32, kind="ExternalInput").ap()
    d_wqkv = nc.dram_tensor("wqkv", [128, 24576], F32, kind="ExternalInput").ap()
    d_qn = nc.dram_tensor("qn", [1024], F32, kind="ExternalInput").ap()
    d_kn = nc.dram_tensor("kn", [1024], F32, kind="ExternalInput").ap()
    d_wout = nc.dram_tensor("wout", [128, 8192], F32, kind="ExternalInput").ap()
    d_bo = nc.dram_tensor("bo", [1024], F32, kind="ExternalInput").ap()
    d_ew = nc.dram_tensor("ew", [1024, 2], F32, kind="ExternalInput").ap()
    d_ei = nc.dram_tensor("ei", [1024, 2], I32, kind="ExternalInput").ap()
    d_w1a = nc.dram_tensor("w1a", [128, 4096], F32, kind="ExternalInput").ap()
    d_w1b = nc.dram_tensor("w1b", [128, 16384], F32, kind="ExternalInput").ap()
    d_w2a = nc.dram_tensor("w2a", [128, 16384], F32, kind="ExternalInput").ap()
    d_w2b = nc.dram_tensor("w2b", [128, 4096], F32, kind="ExternalInput").ap()
    d_out = nc.dram_tensor("out", [2048, 1024], F32, kind="ExternalOutput").ap()

    d_ccin = nc.dram_tensor("cc_in", [64, 2080], F32)
    d_ccout = nc.dram_tensor("cc_out", [64, 2080], F32)
    d_gate = nc.dram_tensor("gate_scr", [8, 1024], F32)

    dbg = {}
    if debug:
        for name, shape in [
            ("ln1_0", [128, 1024]), ("q_str", [128, 16384]), ("k_0", [128, 1024]),
            ("v_0", [128, 1024]), ("cc_sb", [64, 2080]), ("kv_bd", [128, 2048]),
            ("sk", [128, 16]), ("recip0", [128, 16]), ("attn0", [128, 1024]),
            ("y1_0", [128, 1024]), ("xr_mid", [128, 16384]), ("t_sb", [128, 4096]),
            ("u_sb", [128, 4096]), ("wrep", [128, 4096]), ("wexp", [8, 1024]),
            ("qfm", [128, 1024]), ("ksbd", [128, 256]),
        ]:
            dbg[name] = nc.dram_tensor("dbg_" + name, shape, F32, kind="ExternalOutput").ap()

    with ExitStack() as ctx:
        sb = lambda name, shape, dt: ctx.enter_context(nc.sbuf_tensor(name, shape, dt))

        x_res = sb("x_res", [128, NT, 1024], F32)
        arena1 = sb("arena1", [128, 12288], F32)
        arena2 = sb("arena2", [128, 8192], F32)
        wout_sb = sb("wout_sb", [128, DC, 1024], BF16)
        w1aT = sb("w1aT", [128, DC, 512], BF16)
        h_sb = sb("h_sb", [128, 4, 1024], BF16)
        arena4 = sb("arena4", [128, 4096], BF16)
        arena3 = sb("arena3", [128, 2080], F32)
        ksum_bd = sb("ksum_bd", [128, 2, DC, 16], BF16)
        sk_sum = sb("sk_sum", [128, 16], F32)
        ident = sb("ident", [128, 128], BF16)
        ln1_tm = sb("ln1_tm", [128, 1024], BF16)
        ln1_fm = sb("ln1_fm", [128, 2, DC, 128], BF16)
        k_tile = sb("k_tile", [128, 1024], BF16)
        v_tile = sb("v_tile", [128, 1024], BF16)
        kn_rep = sb("kn_rep", [128, 1024], BF16)
        qn_fm = sb("qn_fm", [128, 8], F32)
        bo_bf = sb("bo_bf", [1, 1024], BF16)
        ones_r = sb("ones_r", [1, 128], BF16)
        ones_c = sb("ones_c", [128, 1], BF16)
        recips = sb("recips", [128, 32], F32)
        stats = sb("stats", [128, 16], F32)
        pc8 = sb("pc8", [8, 1], F32)
        cst = sb("cst", [128, 4], F32)

        ps = ctx.enter_context(nc.psum_tensor("ps", [128, 8, 512], F32))

        a1_bf = arena1.ap().bitcast(BF16)
        vw_qkv = a1_bf.rearrange("p (k f) -> p k f", k=DC)
        vw_w1b = a1_bf[:, 0:16384].rearrange("p (c f) -> p c f", c=4)
        vw_w2b = a1_bf[:, 16384:20480].rearrange("p (c f) -> p c f", c=4)
        vw_t = a1_bf[:, 20480:24576].rearrange("p (c f) -> p c f", c=4)

        a2_bf = arena2.ap().bitcast(BF16)
        vw_qstr = a2_bf.rearrange("p (t f) -> p t f", t=NT)
        vw_u = a2_bf[:, 0:4096].rearrange("p (c f) -> p c f", c=4)
        vw_sA = a2_bf[:, 4096:8192].rearrange("p (s c f) -> p s c f", s=2, c=4)

        vw_w1a_stage = h_sb.ap()

        vw_kvbd = arena4.ap()[:, 0:2048].rearrange("p (b f) -> p b f", b=2)
        vw_attn_tm = arena4.ap()[:, 2048:3072]
        vw_attn_fm = arena4.ap()[:, 3072:4096].rearrange("p (c f) -> p c f", c=DC)
        vw_wrep = arena4.ap().rearrange("p (c f) -> p c f", c=4)

        vw_cc = arena3.ap()[0:64, 0:2080]
        vw_y1 = arena3.ap()[:, 0:1024]
        vw_tmp = arena3.ap()[:, 1024:2048]
        vw_ewb = arena2.ap()[0:8, 2048:4096]
        vw_eib = arena2.ap()[0:8, 4096:6144].bitcast(I32)
        vw_eif = arena2.ap()[0:8, 6144:8192]
        vw_wexp = arena2.ap()[0:8, 4096:6144]

        q_fm = ln1_fm.ap()[:, 1, :, :]

        vw_w2aT = (x_res.ap()[:, 8:16, :].rearrange("p t f -> p (t f)")
                   .bitcast(BF16).rearrange("p (c f) -> p c f", c=32))

        def bank(i):
            return ps[:, i, :]

        def bankpair(i):
            return ps[:, i:i + 2, :].rearrange("p b f -> p (b f)")

        b0_bf = ps[:, 0, :].bitcast(BF16)

        semnames = ["tensor", "vector", "scalar", "gpsimd", "cc",
                    "dx0", "dx1", "dW", "dW2", "dF0", "dF1", "dA0", "dA1", "dO", "dCC"]
        sems = {n: ctx.enter_context(nc.semaphore(n)) for n in semnames}

        P = Prog()
        do = P.do

        out_t = d_out.rearrange("(t p) f -> t p f", p=128)
        x_t = d_x.rearrange("p (t f) -> p t f", t=NT)

        def emit_ln_stats2(x_ap, w_sq=(), s0=0, scratch=None):
            if scratch is None:
                scratch = vw_tmp
            t_sq = do("scalar", lambda e: e.activation(
                scratch, x_ap, AF.Square, accum_out=stats[:, s0:s0 + 1]), waits=list(w_sq))
            t3 = do("scalar", lambda e: e.activation(stats[:, s0 + 2:s0 + 3], stats[:, s0 + 1:s0 + 2],
                                                     AF.Copy, scale=cst[:, 2:3]))
            t4 = do("scalar", lambda e: e.activation(stats[:, s0 + 3:s0 + 4], stats[:, s0 + 2:s0 + 3], AF.Square))
            t5 = do("vector", lambda e: e.tensor_scalar(
                stats[:, s0 + 4:s0 + 5], stats[:, s0:s0 + 1], cst[:, 2:3], stats[:, s0 + 3:s0 + 4],
                ALU.mult, ALU.subtract), waits=[t4])
            t6 = do("scalar", lambda e: e.activation(stats[:, s0 + 5:s0 + 6], stats[:, s0 + 4:s0 + 5],
                                                     AF.Sqrt, bias=cst[:, 0:1]), waits=[t5])
            t7 = do("vector", lambda e: e.reciprocal(stats[:, s0 + 6:s0 + 7], stats[:, s0 + 5:s0 + 6]),
                    waits=[t6])
            return t7

        def emit_ln_stats(x_ap, scratch, w_first=()):
            t1 = do("vector", lambda e: e.scalar_tensor_tensor(
                scratch, x_ap, 1.0, x_ap, ALU.mult, ALU.mult,
                accum_out=stats[:, 0:1]), waits=list(w_first))
            t2 = do("vector", lambda e: e.tensor_reduce(stats[:, 1:2], x_ap, AX.X, ALU.add))
            t3 = do("scalar", lambda e: e.activation(stats[:, 2:3], stats[:, 1:2], AF.Copy,
                                                     scale=cst[:, 2:3]), waits=[t2])
            t4 = do("scalar", lambda e: e.activation(stats[:, 3:4], stats[:, 2:3], AF.Square))
            t5 = do("vector", lambda e: e.tensor_scalar(
                stats[:, 4:5], stats[:, 0:1], cst[:, 2:3], stats[:, 3:4],
                ALU.mult, ALU.subtract), waits=[t4, t1])
            t6 = do("scalar", lambda e: e.activation(stats[:, 5:6], stats[:, 4:5], AF.Sqrt,
                                                     bias=cst[:, 0:1]), waits=[t5])
            t7 = do("vector", lambda e: e.reciprocal(stats[:, 6:7], stats[:, 5:6]), waits=[t6])
            return t7

        t_x0 = do("sync", lambda e: e.dma_start(out=x_res[:, 0:8, :], in_=x_t[:, 0:8, :]),
                  sem="dx0", inc=16)
        t_x1 = do("sync", lambda e: e.dma_start(out=x_res[:, 8:16, :], in_=x_t[:, 8:16, :]),
                  sem="dx1", inc=16)

        t_w1a_l = do("gpsimd", lambda e: e.dma_start(out=vw_w1a_stage, in_=d_w1a.rearrange("p (c f) -> p c f", c=4)),
                     sem="dA0", inc=16)
        wq_src = d_wqkv.rearrange("p (k f) -> p k f", k=DC)
        t_wqkv1 = do("gpsimd", lambda e: e.dma_start(out=vw_qkv[:, :, 1024:2048], in_=wq_src[:, :, 1024:2048]),
                     sem="dW", inc=16)
        do("gpsimd", lambda e: e.dma_start(out=vw_qkv[:, :, 0:1024], in_=wq_src[:, :, 0:1024]),
           sem="dW2", inc=16)
        t_wqkv2 = do("gpsimd", lambda e: e.dma_start(out=vw_qkv[:, :, 2048:3072], in_=wq_src[:, :, 2048:3072]),
                     sem="dW2", inc=16)
        do("gpsimd", lambda e: e.dma_start(out=wout_sb[:, :, :], in_=d_wout.rearrange("p (k f) -> p k f", k=DC)),
           sem="dF0", inc=16)
        do("gpsimd", lambda e: e.dma_start(out=kn_rep[:, :], in_=d_kn.unsqueeze(0).broadcast_to([128, 1024])),
           sem="dF0", inc=16)
        do("gpsimd", lambda e: e.dma_start(out=bo_bf[0:1, :], in_=d_bo.unsqueeze(0)),
           sem="dF0", inc=16)
        do("gpsimd", lambda e: e.dma_start(out=vw_eib, in_=d_ei.rearrange("n k -> (n k)").unsqueeze(0).broadcast_to([8, 2048])),
           sem="dF0", inc=16)
        do("gpsimd", lambda e: e.dma_start(out=qn_fm[:, :], in_=d_qn.rearrange("(c p) -> p c", p=128), allow_slow_non_contiguous=True),
           sem="dF0", inc=16)
        do("gpsimd", lambda e: e.dma_start(out=vw_ewb, in_=d_ew.rearrange("n k -> (n k)").unsqueeze(0).broadcast_to([8, 2048])),
           sem="dF0", inc=16)
        t_f0_all = P.tok("dF0")
        t_qn = t_f0_all
        t_kn = t_f0_all

        do("gpsimd", lambda e: e.memset(ident[:, :], 0.0))
        t_ident = do("gpsimd", lambda e: e.affine_select(
            out=ident[:, :], in_=ident[:, :], compare_op=ALU.not_equal, fill=1.0,
            base=0, pattern=[[-1, 128]], channel_multiplier=1))
        do("gpsimd", lambda e: e.memset(ones_r[0:1, :], 1.0))
        t_ones = do("gpsimd", lambda e: e.memset(ones_c[:, :], 1.0))
        t_pc = do("gpsimd", lambda e: e.iota(pc8[:, :], [[0, 1]], channel_multiplier=1,
                                             allow_small_or_imprecise_dtypes=True))
        t_ksbd0 = do("gpsimd", lambda e: e.memset(ksum_bd.ap().rearrange("p a b c -> p (a b c)"), 0.0))
        t_kvbd0 = do("gpsimd", lambda e: e.memset(arena4.ap()[:, 0:2048], 0.0))
        do("gpsimd", lambda e: e.memset(cst[:, 0:1], EPS_LN))
        do("gpsimd", lambda e: e.memset(cst[:, 1:2], EPS_RMS))
        t_cst = do("gpsimd", lambda e: e.memset(cst[:, 2:3], 1.0 / 1024.0))
        t_cc0 = do("vector", lambda e: e.memset(vw_cc, 0.0))

        t_eif = do("vector", lambda e: e.tensor_copy(vw_eif, vw_eib), waits=[t_f0_all])
        t_m = do("vector", lambda e: e.tensor_single_scalar(vw_eif, vw_eif, pc8[:, 0:1], ALU.is_equal),
                 waits=[t_eif, t_pc])
        t_mw = do("vector", lambda e: e.tensor_tensor(vw_wexp, vw_eif, vw_ewb, ALU.mult),
                  waits=[t_m])
        t_wexp = do("vector", lambda e: e.tensor_reduce(
            vw_wexp[:, 0:1024], vw_wexp.rearrange("p (n k) -> p n k", k=2), AX.X, ALU.add),
            waits=[t_mw])
        t_gate_st = do("gpsimd", lambda e: e.dma_start(out=d_gate.ap(), in_=vw_wexp[:, 0:1024]),
                       sem="dF1", inc=16, waits=[t_wexp])
        if debug:
            do("gpsimd", lambda e: e.dma_start(out=dbg["wexp"], in_=vw_wexp[:, 0:1024]),
               sem="dO", inc=16, waits=[t_wexp])

        blocks = [(rc, fc) for rc in range(4) for fc in range(8)]
        t_w1aT = None
        for grp in range(4):
            chunk = blocks[grp * 8:(grp + 1) * 8]
            for idx, (rc, fc) in enumerate(chunk):
                w = [t_w1a_l, t_ident]
                if t_w1aT is not None:
                    w.append(t_w1aT)
                do("tensor", lambda e, rc=rc, fc=fc, idx=idx: e.transpose(
                    b0_bf[:, idx * 128:(idx + 1) * 128],
                    vw_w1a_stage[:, rc, fc * 128:(fc + 1) * 128], ident[:, :]),
                    waits=w if idx == 0 else (), mode="T")
            t_tp = P.tok("tensor")
            for idx, (rc, fc) in enumerate(chunk):
                do("vector", lambda e, rc=rc, fc=fc, idx=idx: e.tensor_copy(
                    w1aT[:, fc, rc * 128:(rc + 1) * 128], b0_bf[:, idx * 128:(idx + 1) * 128]),
                    waits=[t_tp] if idx == 0 else ())
            t_w1aT = P.tok("vector")

        GRP_COL = [1024, 1536, 0, 512, 2048, 2560]
        evac_qkv = {}
        tT_evac = {}
        last_qkv = {}
        kv_consumed = None
        kv_add = {}
        sq_done = None
        q_scaled = {}
        ln1_used = {}
        apply_tok = {}

        def emit_ln_statsA(i):
            t_xl = t_x0 if i < 8 else t_x1
            t1 = do("vector", lambda e, i=i: e.scalar_tensor_tensor(
                vw_attn_tm, x_res[:, i, :], 1.0, x_res[:, i, :], ALU.mult, ALU.mult,
                accum_out=stats[:, 0:1]), waits=[t_xl, t_cst])
            t2 = do("vector", lambda e, i=i: e.tensor_reduce(
                stats[:, 1:2], x_res[:, i, :], AX.X, ALU.add))
            t3 = do("scalar", lambda e: e.activation(stats[:, 2:3], stats[:, 1:2], AF.Copy,
                                                     scale=cst[:, 2:3]), waits=[t2])
            t4 = do("scalar", lambda e: e.activation(stats[:, 3:4], stats[:, 2:3], AF.Square))
            return t1, t4

        def emit_ln_statsB(t1, t4):
            t5 = do("vector", lambda e: e.tensor_scalar(
                stats[:, 4:5], stats[:, 0:1], cst[:, 2:3], stats[:, 3:4],
                ALU.mult, ALU.subtract), waits=[t4, t1])
            t6 = do("scalar", lambda e: e.activation(stats[:, 5:6], stats[:, 4:5], AF.Sqrt,
                                                     bias=cst[:, 0:1]), waits=[t5])
            return t6

        def emit_ln_apply(i, t6):
            t7 = do("vector", lambda e: e.reciprocal(stats[:, 6:7], stats[:, 5:6]), waits=[t6])
            w_ap = [t7]
            if i >= 1:
                w_ap.append(ln1_used[i - 1])
            return do("vector", lambda e, i=i: e.tensor_scalar(
                ln1_tm[:, :], x_res[:, i, :], stats[:, 2:3], stats[:, 6:7],
                ALU.subtract, ALU.mult), waits=w_ap)

        _t1, _t4 = emit_ln_statsA(0)
        _t6 = emit_ln_statsB(_t1, _t4)
        apply_tok[0] = emit_ln_apply(0, _t6)

        for i in range(NT):
            wT = [apply_tok[i], t_ident]
            if i == 0:
                wT.append(t_w1aT)
            else:
                wT.append(tT_evac[i - 1])
            for j in range(DC):
                do("tensor", lambda e, j=j: e.transpose(
                    b0_bf[:, j * 128:(j + 1) * 128], ln1_tm[:, j * 128:(j + 1) * 128], ident[:, :]),
                    waits=wT if j == 0 else (), mode="T")
            t_tp = P.tok("tensor")
            ln1_used[i] = t_tp
            wev = [t_tp]
            if i >= 2:
                wev.append(last_qkv[i - 2])
            t_ev = do("vector", lambda e, i=i: e.tensor_copy(
                ln1_fm[:, i % 2, :, :].rearrange("p c f -> p (c f)"), b0_bf[:, :]), waits=wev)
            tT_evac[i] = t_ev
            if debug and i == NT - 1:
                do("gpsimd", lambda e: e.dma_start(out=dbg["ln1_0"], in_=ln1_tm[:, :]),
                   sem="dO", inc=16, waits=[apply_tok[i]])

            ln_state = None
            for g in range(6):
                bk = 1 + (g % 2)
                col = GRP_COL[g]
                wq = [t_ev, t_wqkv1 if g < 2 else t_wqkv2]
                prev = (i, g - 2) if g >= 2 else ((i - 1, g + 4) if i >= 1 else None)
                if prev is not None:
                    wq.append(evac_qkv[prev])
                for k in range(DC):
                    do("tensor", lambda e, k=k, col=col, bk=bk, i=i: e.matmul(
                        bank(bk), ln1_fm[:, i % 2, k, :], vw_qkv[:, k, col:col + 512],
                        start=(k == 0), stop=(k == DC - 1)),
                        waits=wq if k == 0 else (), mode="P")
                t_mm = P.tok("tensor")
                if g == 5:
                    last_qkv[i] = t_mm
                if g < 2:
                    dest = k_tile[:, (g % 2) * 512:(g % 2) * 512 + 512]
                    wv = [t_mm]
                    if kv_consumed is not None and g == 0:
                        wv.append(kv_consumed)
                elif g < 4:
                    dest = vw_qstr[:, i, (g % 2) * 512:(g % 2) * 512 + 512]
                    wv = [t_mm]
                    if i in (8, 9) and g == 2:
                        wv.append(t_gate_st)
                    if i >= 1 and g == 2:
                        wv.append(q_scaled[i - 1])
                else:
                    dest = v_tile[:, (g % 2) * 512:(g % 2) * 512 + 512]
                    wv = [t_mm]
                    if kv_consumed is not None and g == 4:
                        wv.append(kv_consumed)
                t_e = do("vector", lambda e, dest=dest, bk=bk: e.tensor_copy(dest, bank(bk)), waits=wv)
                evac_qkv[(i, g)] = t_e

                if g == 1:
                    t_sqk = do("scalar", lambda e: e.activation(
                        vw_attn_fm.rearrange("p c f -> p (c f)"), k_tile[:, :], AF.Square,
                        accum_out=recips[:, 19:20]),
                        waits=[evac_qkv[(i, 0)], evac_qkv[(i, 1)]])
                    sq_done = t_sqk
                    t_rk0 = do("scalar", lambda e: e.activation(
                        recips[:, 20:21], recips[:, 19:20], AF.Sqrt,
                        scale=cst[:, 2:3], bias=cst[:, 1:2]))
                    t_kkn = do("gpsimd", lambda e: e.tensor_tensor(
                        k_tile[:, :], k_tile[:, :], kn_rep[:, :], ALU.mult),
                        waits=[t_sqk, t_kn])
                    t_rk = do("vector", lambda e: e.reciprocal(recips[:, 21:22], recips[:, 20:21]),
                              waits=[t_rk0])
                    t_krelu = do("scalar", lambda e: e.activation(
                        k_tile[:, :], k_tile[:, :], AF.Relu, scale=recips[:, 21:22]),
                        waits=[t_rk, t_kkn])
                if g == 2 and i < NT - 1:
                    ln_state = emit_ln_statsA(i + 1)
                if g == 3:
                    do("scalar", lambda e, i=i: e.activation(
                        vw_attn_fm.rearrange("p c f -> p (c f)"), vw_qstr[:, i, :], AF.Square,
                        accum_out=recips[:, 16:17]),
                        waits=[evac_qkv[(i, 2)], evac_qkv[(i, 3)]])
                    t_rq0 = do("scalar", lambda e: e.activation(
                        recips[:, 17:18], recips[:, 16:17], AF.Sqrt,
                        scale=cst[:, 2:3], bias=cst[:, 1:2]))
                    t_rq = do("vector", lambda e: e.reciprocal(recips[:, 18:19], recips[:, 17:18]),
                              waits=[t_rq0])
                    q_scaled[i] = do("scalar", lambda e, i=i: e.activation(
                        vw_qstr[:, i, :], vw_qstr[:, i, :], AF.Copy, scale=recips[:, 18:19]),
                        waits=[t_rq])
                if g == 4 and ln_state is not None:
                    ln_state = ln_state + (emit_ln_statsB(*ln_state),)

            if debug and i == NT - 1:
                do("gpsimd", lambda e: e.dma_start(out=dbg["k_0"], in_=k_tile[:, :]),
                   sem="dO", inc=16, waits=[t_krelu])
                do("gpsimd", lambda e: e.dma_start(out=dbg["v_0"], in_=v_tile[:, :]),
                   sem="dO", inc=16, waits=[evac_qkv[(i, 5)]])

            b = i // 8
            kvp = 3 if i % 2 == 0 else 5
            ksc = 16 * (i % 2)
            wkv = [t_krelu, evac_qkv[(i, 4)], evac_qkv[(i, 5)], t_ones]
            if i >= 1:
                wkv.append(kv_add[i - 1])
            for h in range(H):
                do("tensor", lambda e, h=h, kvp=kvp: e.matmul(
                    ps[0:64, kvp + h // 8, (h % 8) * 64:(h % 8) * 64 + 64],
                    k_tile[:, h * 64:(h + 1) * 64], v_tile[:, h * 64:(h + 1) * 64],
                    start=True, stop=True),
                    waits=wkv if h == 0 else (), mode="P")
            for h in range(H):
                do("tensor", lambda e, h=h, ksc=ksc: e.matmul(
                    ps[0:64, 7, ksc + h:ksc + h + 1],
                    k_tile[:, h * 64:(h + 1) * 64], ones_c[:, 0:1],
                    start=True, stop=True), mode="P")
            kv_consumed = P.tok("tensor")
            if ln_state is not None:
                apply_tok[i + 1] = emit_ln_apply(i + 1, ln_state[2])
            do("vector", lambda e, b=b, kvp=kvp: e.tensor_tensor(
                vw_cc[:, b * 1024:(b + 1) * 1024],
                ps[0:64, kvp:kvp + 2, :].rearrange("p b2 f -> p (b2 f)"),
                vw_cc[:, b * 1024:(b + 1) * 1024], ALU.add),
                waits=[kv_consumed, t_cc0])
            t_a2 = do("vector", lambda e, b=b, ksc=ksc: e.tensor_tensor(
                vw_cc[:, 2048 + b * 16: 2048 + b * 16 + 16],
                ps[0:64, 7, ksc:ksc + 16],
                vw_cc[:, 2048 + b * 16: 2048 + b * 16 + 16], ALU.add))
            kv_add[i] = t_a2

        qkv_done = last_qkv[NT - 1]

        t_ccready = kv_add[NT - 1]
        if debug:
            do("gpsimd", lambda e: e.dma_start(out=dbg["cc_sb"], in_=vw_cc),
               sem="dO", inc=16, waits=[t_ccready])
            do("gpsimd", lambda e: e.dma_start(out=dbg["q_str"], in_=vw_qstr.rearrange("p t f -> p (t f)")),
               sem="dO", inc=16, waits=[q_scaled[NT - 1]])

        t_ccst = do("sync", lambda e: e.dma_start(out=d_ccin.ap(), in_=vw_cc),
                    sem="dCC", inc=16, waits=[t_ccready])
        t_cc = do("gpsimd", lambda e: e.collective_compute(
            "AllReduce", ALU.add,
            replica_groups=[[0, 1], [2, 3], [4, 5], [6, 7]],
            ins=[d_ccin.ap().opt()], outs=[d_ccout.ap().opt()]),
            sem="cc", inc=1, waits=[t_ccst])
        kvbd_even = arena4.ap()[0:64, 0:2048].rearrange("p (b2 j f) -> p b2 j f", b2=2, j=8)
        kvbd_odd = arena4.ap()[64:128, 0:2048].rearrange("p (b2 j f) -> p b2 j f", b2=2, j=8)
        cc_kv = d_ccout.ap()[0:64, 0:2048].rearrange("p (b2 j two f) -> p b2 j two f", b2=2, j=8, two=2)
        do("gpsimd", lambda e: e.dma_start(out=kvbd_even[:, :, :, 0:64], in_=cc_kv[:, :, :, 0, :]),
           sem="dCC", inc=16, waits=[t_cc, t_kvbd0])
        do("gpsimd", lambda e: e.dma_start(out=kvbd_odd[:, :, :, 64:128], in_=cc_kv[:, :, :, 1, :]),
           sem="dCC", inc=16)
        cc_ks = d_ccout.ap()[0:64, 2048:2080].rearrange("p (b2 j two) -> p b2 j two", b2=2, two=2)
        do("sync", lambda e: e.dma_start(
            out=sk_sum[0:64, :].rearrange("p (b2 j) -> p b2 j", b2=2),
            in_=cc_ks[:, :, :, 0], allow_slow_non_contiguous=True), sem="dCC", inc=16, waits=[t_cc])
        do("sync", lambda e: e.dma_start(
            out=sk_sum[64:128, :].rearrange("p (b2 j) -> p b2 j", b2=2),
            in_=cc_ks[:, :, :, 1], allow_slow_non_contiguous=True), sem="dCC", inc=16)
        t_cc_all = P.tok("dCC")
        t_kvbd = t_cc_all
        t_sk = t_cc_all
        for b in range(2):
            for j in range(8):
                do("vector", lambda e, b=b, j=j: e.tensor_copy(
                    ksum_bd[0:64, b, j, 2 * j: 2 * j + 1], sk_sum[0:64, 8 * b + j: 8 * b + j + 1]),
                    waits=[t_sk, t_ksbd0] if (b == 0 and j == 0) else ())
                do("vector", lambda e, b=b, j=j: e.tensor_copy(
                    ksum_bd[64:128, b, j, 2 * j + 1: 2 * j + 2], sk_sum[64:128, 8 * b + j: 8 * b + j + 1]))
        t_ksbd = P.tok("vector")
        if debug:
            do("gpsimd", lambda e: e.dma_start(out=dbg["kv_bd"], in_=vw_kvbd.rearrange("p b f -> p (b f)")),
               sem="dO", inc=16, waits=[t_kvbd, t_ksbd])
            do("gpsimd", lambda e: e.dma_start(out=dbg["sk"], in_=sk_sum[:, :]),
               sem="dO", inc=16, waits=[t_sk])
            do("gpsimd", lambda e: e.dma_start(out=dbg["ksbd"], in_=ksum_bd.ap().rearrange("p a b c -> p (a b c)")),
               sem="dO", inc=16, waits=[t_ksbd])

        b0_free = tT_evac[NT - 1]
        qfm_used = None
        attn_evac = None
        t_evac_t = None
        wout_evac = {}
        norm_read = None
        t_store = {}
        attn_div_prev = None
        last_wout = None
        attn_all_done = None
        pend = None

        for i in range(NT):
            b = i // 8
            wT = [q_scaled[i], b0_free, t_kvbd, t_ksbd]
            if i == 0:
                wT.append(qkv_done)
            if qfm_used is not None:
                wT.append(qfm_used)
            for j in range(DC):
                do("tensor", lambda e, i=i, j=j: e.transpose(
                    b0_bf[:, j * 128:(j + 1) * 128], vw_qstr[:, i, j * 128:(j + 1) * 128], ident[:, :]),
                    waits=wT if j == 0 else (), mode="T")
            t_qT = P.tok("tensor")
            for j in range(DC):
                do("vector", lambda e, j=j: e.tensor_scalar(
                    q_fm[:, j, :], b0_bf[:, j * 128:(j + 1) * 128],
                    qn_fm[:, j:j + 1], 0.0, ALU.mult, ALU.max),
                    waits=[t_qT, t_qn] if j == 0 else ())
            t_qfm = P.tok("vector")
            b0_free = t_qfm
            if debug and i == NT - 1:
                do("gpsimd", lambda e: e.dma_start(out=dbg["qfm"], in_=q_fm.rearrange("p c f -> p (c f)")),
                   sem="dO", inc=16, waits=[t_qfm])

            wN = [t_qfm, t_ksbd]
            if norm_read is not None:
                wN.append(norm_read)
            if i == 0:
                wN.append(kv_add[NT - 1])
            for j in range(DC):
                do("tensor", lambda e, j=j, b=b: e.matmul(
                    bank(7)[:, 0:16], q_fm[:, j, :], ksum_bd[:, b, j, :],
                    start=(j == 0), stop=(j == DC - 1)),
                    waits=wN if j == 0 else (), mode="P")
            t_norm = P.tok("tensor")
            wA = [t_kvbd]
            if t_evac_t is not None:
                wA.append(t_evac_t)
            elif attn_evac is not None:
                wA.append(attn_evac)
            if i == 0:
                wA += [evac_qkv[(NT - 1, 4)], evac_qkv[(NT - 1, 5)]]
            for j in range(DC):
                do("tensor", lambda e, j=j, b=b: e.matmul(
                    bankpair(1)[:, j * 128:(j + 1) * 128], q_fm[:, j, :], vw_kvbd[:, b, j * 128:(j + 1) * 128],
                    start=True, stop=True),
                    waits=wA if j == 0 else (), mode="P")
            t_attnmm = P.tok("tensor")
            qfm_used = t_attnmm
            if i == NT - 1:
                attn_all_done = t_attnmm

            t_eps = do("scalar", lambda e: e.activation(
                recips[:, 16:32], bank(7)[:, 0:16], AF.Identity, bias=cst[:, 1:2]),
                waits=[t_norm])
            norm_read = t_eps
            t_rec = do("vector", lambda e: e.reciprocal(recips[:, 0:16], recips[:, 16:32]),
                       waits=[t_eps])
            if debug and i == NT - 1:
                do("gpsimd", lambda e: e.dma_start(out=dbg["recip0"], in_=recips[:, 0:16]),
                   sem="dO", inc=16, waits=[t_rec])

            wD = [t_attnmm, t_rec]
            if i == 0:
                wD.append(sq_done)
            if attn_div_prev is not None:
                wD.append(attn_div_prev)
            t_div = do("vector", lambda e: e.tensor_tensor(
                vw_attn_tm.rearrange("p (h d) -> p h d", d=64),
                bankpair(1).rearrange("p (h d) -> p h d", d=64),
                recips[:, 0:16].unsqueeze(-1).broadcast_to([128, 16, 64]),
                ALU.mult), waits=wD)
            attn_evac = t_div
            if debug and i == NT - 1:
                do("gpsimd", lambda e: e.dma_start(out=dbg["attn0"], in_=vw_attn_tm),
                   sem="dO", inc=16, waits=[t_div])

            for j in range(DC):
                do("tensor", lambda e, j=j: e.transpose(
                    b0_bf[:, j * 128:(j + 1) * 128], vw_attn_tm[:, j * 128:(j + 1) * 128], ident[:, :]),
                    waits=[t_div, b0_free] if j == 0 else (), mode="T")
            t_aT = P.tok("tensor")
            t_afm = do("vector", lambda e: e.tensor_copy(
                vw_attn_fm.rearrange("p c f -> p (c f)"), b0_bf[:, :]),
                waits=[t_aT] + ([sq_done] if i == 0 else []))
            attn_div_prev = t_afm
            b0_free = t_afm

            wb = 3 if i % 2 == 0 else 5
            wW = [t_afm, t_f0_all, t_ones]
            if i < 2:
                wW.append(kv_add[NT - 1])
            elif (i - 2) in wout_evac:
                wW.append(wout_evac[i - 2])
            for half in range(2):
                for j in range(DC):
                    do("tensor", lambda e, j=j, half=half, wb=wb: e.matmul(
                        bank(wb + half), vw_attn_fm[:, j, :], wout_sb[:, j, half * 512:(half + 1) * 512],
                        start=(j == 0), stop=False),
                        waits=wW if (j == 0 and half == 0) else (), mode="P")
                do("tensor", lambda e, half=half, wb=wb: e.matmul(
                    bank(wb + half), ones_r[0:1, :], bo_bf[0:1, half * 512:(half + 1) * 512],
                    start=False, stop=True), mode="P")
            t_wout = P.tok("tensor")
            last_wout = t_wout

            if pend is not None:
                pi, t_ln2p = pend
                for j in range(DC):
                    do("tensor", lambda e, j=j: e.transpose(
                        b0_bf[:, j * 128:(j + 1) * 128], ln1_tm[:, j * 128:(j + 1) * 128], ident[:, :]),
                        waits=[t_ln2p, b0_free] if j == 0 else (), mode="T")
                t_l2T = P.tok("tensor")
                l2T_done = t_l2T
                t_l2fm = do("vector", lambda e: e.tensor_copy(
                    ln1_fm[:, 0, :, :].rearrange("p c f -> p (c f)"), b0_bf[:, :]), waits=[t_l2T])
                b0_free = t_l2fm
                wt_ = [t_l2fm, t_div, t_w1aT]
                for m in range(4):
                    for k in range(DC):
                        do("tensor", lambda e, m=m, k=k: e.matmul(
                            bankpair(1)[:, m * 128:(m + 1) * 128], w1aT[:, k, m * 128:(m + 1) * 128],
                            ln1_fm[:, 0, k, :], start=(k == 0), stop=(k == DC - 1)),
                            waits=wt_ if (m == 0 and k == 0) else (), mode="P")
                t_tmm = P.tok("tensor")
                wev = [t_tmm]
                if pi == 0:
                    wev.append(qkv_done)
                t_evac_t = do("vector", lambda e, pi=pi: e.tensor_copy(
                    vw_t[:, :, pi * 128:(pi + 1) * 128],
                    bankpair(1)[:, 0:512].rearrange("p (m n) -> p m n", n=128)), waits=wev)
                pend = None

            wY = [t_wout]
            if i == 0:
                wY.append(t_cc_all)
            t_y1 = do("vector", lambda e, wb=wb: e.tensor_scalar(
                vw_y1, bankpair(wb), 1.0, 0.0, ALU.mult, ALU.add,
                accum_out=stats[:, 1:2]), waits=wY)
            wout_evac[i] = t_y1
            if debug and i == NT - 1:
                do("gpsimd", lambda e: e.dma_start(out=dbg["y1_0"], in_=vw_y1),
                   sem="dO", inc=16, waits=[t_y1])
            t_rs = emit_ln_stats2(vw_y1, [t_y1])
            t_lnA = do("vector", lambda e: e.tensor_scalar(
                vw_tmp, vw_y1, stats[:, 2:3], stats[:, 6:7], ALU.subtract, ALU.mult),
                waits=[t_rs])
            t_res = do("vector", lambda e, i=i: e.scalar_tensor_tensor(
                x_res[:, i, :], x_res[:, i, :], 1.0, vw_tmp, ALU.mult, ALU.add,
                accum_out=stats[:, 9:10]), waits=[t_lnA])

            if i >= 8:
                t_store[i] = do("sync", lambda e, i=i: e.dma_start(out=out_t[i], in_=x_res[:, i, :]),
                                sem="dO", inc=16, waits=[t_res])
            else:
                t_rs2 = emit_ln_stats2(x_res[:, i, :], [t_res], s0=8)
                w_l2 = [t_rs2]
                if i == 0:
                    w_l2.append(ln1_used[NT - 1])
                else:
                    w_l2.append(l2T_done)
                t_ln2 = do("vector", lambda e, i=i: e.tensor_scalar(
                    ln1_tm[:, :], x_res[:, i, :], stats[:, 10:11], stats[:, 14:15], ALU.subtract, ALU.mult),
                    waits=w_l2)
                pend = (i, t_ln2)

            if i == NT - 1 and pend is not None:
                pass

        t_t_all = t_evac_t
        if debug:
            do("gpsimd", lambda e: e.dma_start(
                out=dbg["xr_mid"], in_=x_res.ap().rearrange("p t f -> p (t f)")),
                sem="dO", inc=16, waits=[P.tok("vector")])

        do("gpsimd", lambda e: e.dma_start(
            out=vw_w1b, in_=d_w1b.rearrange("p (c f) -> p c f", c=4)),
            sem="dF1", inc=16, waits=[qkv_done])
        do("gpsimd", lambda e: e.dma_start(
            out=vw_w2b, in_=d_w2b.rearrange("p (c f) -> p c f", c=4)),
            sem="dF1", inc=16)
        for cpair in range(4):
            for e2 in range(2):
                do("gpsimd", lambda e, cpair=cpair, e2=e2: e.dma_start(
                    out=vw_wrep[e2 * 64:(e2 + 1) * 64, cpair, :],
                    in_=d_gate.ap()[2 * cpair + e2: 2 * cpair + e2 + 1, :].broadcast_to([64, 1024])),
                    sem="dF1", inc=16,
                    waits=([P.tok("dF1"), last_wout, attn_all_done]
                           + ([P.tok("dO")] if debug else [])) if (cpair == 0 and e2 == 0) else ())
        t_f1_all = P.tok("dF1")
        if debug:
            do("gpsimd", lambda e: e.dma_start(out=dbg["wrep"], in_=vw_wrep.rearrange("p c f -> p (c f)")),
               sem="dO", inc=16, waits=[t_f1_all])
            do("gpsimd", lambda e: e.dma_start(out=dbg["t_sb"], in_=vw_t.rearrange("p c f -> p (c f)")),
               sem="dO", inc=16, waits=[t_t_all])

        stores_done = P.tok("dO")
        t_sA = {}
        sA_used = {}
        b0f = b0_free
        for c8 in range(8):
            slot = c8 % 2
            wL = [qfm_used]
            if c8 >= 2:
                wL.append(sA_used[c8 - 2])
            t_sA[c8] = do("gpsimd", lambda e, c8=c8, slot=slot: e.dma_start(
                out=vw_sA[:, slot, :, :].rearrange("p c f -> p (c f)"),
                in_=d_w2a.rearrange("p (c f) -> p c f", c=4)[:, :, c8 * 512:(c8 + 1) * 512]),
                sem="dA0" if slot == 0 else "dA1", inc=16, waits=wL)
            blocks = [(rc, fc) for rc in range(4) for fc in range(4)]
            for halfg in range(2):
                chunk = blocks[halfg * 8:(halfg + 1) * 8]
                for idx, (rc, fc) in enumerate(chunk):
                    w = [t_sA[c8], b0f] if idx == 0 else ()
                    do("tensor", lambda e, slot=slot, rc=rc, fc=fc, idx=idx: e.transpose(
                        b0_bf[:, idx * 128:(idx + 1) * 128],
                        vw_sA[:, slot, rc, fc * 128:(fc + 1) * 128], ident[:, :]),
                        waits=w, mode="T")
                t_tp = P.tok("tensor")
                for idx, (rc, fc) in enumerate(chunk):
                    do("vector", lambda e, c8=c8, rc=rc, fc=fc, idx=idx: e.tensor_copy(
                        vw_w2aT[:, c8 * 4 + fc, rc * 128:(rc + 1) * 128],
                        b0_bf[:, idx * 128:(idx + 1) * 128]),
                        waits=[t_tp, stores_done] if idx == 0 else ())
                b0f = P.tok("vector")
            sA_used[c8] = b0f
        t_w2aT = b0f

        u_evac = {}
        gel = {}
        umm = {}
        u_free = None
        first_h = True
        seq = [(c, nb, Fc) for c in range(4) for nb in range(2) for Fc in range(32)]

        def emit_u(c, nb, Fc):
            w = [gel[(c, nb, Fc)]]
            if Fc == 0:
                w.append(u_free if u_free is not None else wout_evac[15])
            u_start = (Fc == 0)
            u_stop = (Fc == 31)
            do("tensor", lambda e, c=c, Fc=Fc, s=u_start, st=u_stop: e.matmul(
                ps[0:64, 5, :], vw_w2aT[:, Fc, (2 * c) * 64:(2 * c) * 64 + 64],
                h_sb[:, Fc % 4, 0:512],
                start=s, stop=st, tile_position=(0, 0)),
                waits=w, mode="C")
            do("tensor", lambda e, c=c, Fc=Fc, s=u_start, st=u_stop: e.matmul(
                ps[64:128, 5, :], vw_w2aT[:, Fc, (2 * c + 1) * 64:(2 * c + 1) * 64 + 64],
                h_sb[:, Fc % 4, 512:1024],
                start=s, stop=st, tile_position=(0, 64)),
                mode="C")
            umm[(c, nb, Fc)] = P.tok("tensor")

        def emit_u_evac(c, nb):
            nonlocal_u = [umm[(c, nb, 31)], t_f1_all, qfm_used]
            u_evac[(c, nb)] = do("vector", lambda e, c=c, nb=nb: e.tensor_tensor(
                vw_u[:, c, nb * 512:(nb + 1) * 512], bank(5),
                vw_wrep[:, c, nb * 512:(nb + 1) * 512], ALU.mult), waits=nonlocal_u)
            return u_evac[(c, nb)]

        for s, (c, nb, Fc) in enumerate(seq):
            hp = 3 if Fc % 2 == 0 else 6
            w = [t_f1_all, t_w2aT]
            if first_h:
                w += [t_t_all, wout_evac[14], wout_evac[15], norm_read]
                first_h = False
            if (c, nb, Fc - 2) in gel:
                w.append(gel[(c, nb, Fc - 2)])
            elif Fc <= 1 and (c, nb) != (0, 0):
                prev = (c, nb - 1) if nb == 1 else (c - 1, 1)
                w.append(gel[prev + (30 + Fc,)])
            do("tensor", lambda e, c=c, nb=nb, Fc=Fc, hp=hp: e.matmul(
                bank(hp), vw_w1b[0:64, c, Fc * 128:(Fc + 1) * 128],
                vw_t[0:64, c, nb * 512:(nb + 1) * 512],
                start=True, stop=True, tile_position=(0, 0)),
                waits=w, mode="R")
            do("tensor", lambda e, c=c, nb=nb, Fc=Fc, hp=hp: e.matmul(
                bank(hp + 1), vw_w1b[64:128, c, Fc * 128:(Fc + 1) * 128],
                vw_t[64:128, c, nb * 512:(nb + 1) * 512],
                start=True, stop=True, tile_position=(64, 0)),
                mode="R")
            t_h = P.tok("tensor")
            wg = [t_h]
            if (c, nb, Fc - 4) in umm:
                wg.append(umm[(c, nb, Fc - 4)])
            gel[(c, nb, Fc)] = do("scalar", lambda e, Fc=Fc, hp=hp: e.activation(
                h_sb[:, Fc % 4, :], bankpair(hp), AF.Gelu), waits=wg)
            if s > 0:
                pc_, pn_, pf_ = seq[s - 1]
                emit_u(pc_, pn_, pf_)
                if pf_ == 31:
                    u_free = emit_u_evac(pc_, pn_)
        emit_u(*seq[-1])
        u_free = emit_u_evac(3, 1)
        t_u_all = P.tok("vector")
        if debug:
            do("gpsimd", lambda e: e.dma_start(out=dbg["u_sb"], in_=vw_u.rearrange("p c f -> p (c f)")),
               sem="dO", inc=16, waits=[t_u_all])

        y_prev = None
        for nt in range(NA):
            wy = [t_u_all]
            if y_prev is not None:
                wy.append(y_prev)
            for fh in range(2):
                if not (nt == 0 and fh == 0):
                    P.drain()
                for e_ in range(E):
                    half = e_ % 2
                    bk = (1 + fh) if half == 0 else (3 + fh)
                    s = (e_ < 2)
                    st = (e_ >= E - 2)
                    do("tensor", lambda en, e_=e_, fh=fh, bk=bk, nt=nt, s=s, st=st: en.matmul(
                        bank(bk), vw_u[(e_ % 2) * 64:(e_ % 2) * 64 + 64, e_ // 2, nt * 128:(nt + 1) * 128],
                        vw_w2b[(e_ % 2) * 64:(e_ % 2) * 64 + 64, e_ // 2, fh * 512:(fh + 1) * 512],
                        start=s, stop=st, tile_position=((e_ % 2) * 64, 0)),
                        waits=wy if (e_ == 0 and fh == 0) else (), mode="R")
            t_ymm = P.tok("tensor")
            do("vector", lambda e: e.tensor_copy(vw_y1, bankpair(1)), waits=[t_ymm])
            t_c2 = do("vector", lambda e: e.scalar_tensor_tensor(
                vw_y1, bankpair(3), 1.0, vw_y1, ALU.mult, ALU.add,
                accum_out=stats[:, 1:2]))
            y_prev = t_c2
            t_rs = emit_ln_stats2(vw_y1, [t_c2])
            t_lnA = do("vector", lambda e: e.tensor_scalar(
                vw_tmp, vw_y1, stats[:, 2:3], stats[:, 6:7], ALU.subtract, ALU.mult),
                waits=[t_rs])
            t_res = do("vector", lambda e, nt=nt: e.scalar_tensor_tensor(
                x_res[:, nt, :], x_res[:, nt, :], 1.0, vw_tmp, ALU.mult, ALU.add), waits=[t_lnA])
            t_store[nt] = do("sync", lambda e, nt=nt: e.dma_start(out=out_t[nt], in_=x_res[:, nt, :]),
                             sem="dO", inc=16, waits=[t_res])

        P.wait_only("sync", [P.tok("dO")])

        block = ctx.enter_context(nc.Block())
        for eng_name in ENGINES:
            ops = P.stream[eng_name]
            if not ops:
                continue

            def body(eng, ops=ops, eng_name=eng_name):
                seen = {}
                last_mode = None
                for waits, fn, sem, inc, mode in ops:
                    if eng_name == "tensor" and mode is not None:
                        if last_mode is not None and mode != last_mode:
                            eng.drain()
                        last_mode = mode
                    for (ws, wv) in waits:
                        if ws == eng_name or wv <= 0:
                            continue
                        if seen.get(ws, 0) >= wv:
                            continue
                        eng.wait_ge(sems[ws], wv)
                        seen[ws] = wv
                    if fn is None:
                        continue
                    inst = fn(eng)
                    if inst is not None and sem is not None:
                        inst.then_inc(sems[sem], inc)

            getattr(block, eng_name)(body)

    return nc


_CACHE = {}


def _get_nc(debug=False):
    key = bool(debug)
    if key not in _CACHE:
        _CACHE[key] = build_nc(debug=key)
    return _CACHE[key]


def kernel(L0_attn_x, L0_attn_W_qkv, L0_attn_qn_w, L0_attn_kn_w, L0_attn_W_out,
           L0_attn_b_out, L0_ffn_expert_weights, L0_ffn_expert_indices,
           L0_ffn_w1_A, L0_ffn_w1_B, L0_ffn_w2_A, L0_ffn_w2_B, _debug=False):
    def pmajor(a):
        a = np.asarray(a, dtype=np.float32)
        r = a.shape[0] // 128
        return np.ascontiguousarray(
            a.reshape(r, 128, a.shape[1]).transpose(1, 0, 2).reshape(128, r * a.shape[1]))

    x = np.ascontiguousarray(np.asarray(L0_attn_x, dtype=np.float32))
    shared = dict(
        wqkv=pmajor(L0_attn_W_qkv),
        qn=np.ascontiguousarray(np.asarray(L0_attn_qn_w, dtype=np.float32)),
        kn=np.ascontiguousarray(np.asarray(L0_attn_kn_w, dtype=np.float32)),
        wout=pmajor(L0_attn_W_out),
        bo=np.ascontiguousarray(np.asarray(L0_attn_b_out, dtype=np.float32)),
        w1a=pmajor(L0_ffn_w1_A),
        w1b=pmajor(L0_ffn_w1_B),
        w2a=pmajor(L0_ffn_w2_A),
        w2b=pmajor(L0_ffn_w2_B),
    )
    ew = np.ascontiguousarray(np.asarray(L0_ffn_expert_weights, dtype=np.float32))
    ei = np.ascontiguousarray(np.asarray(L0_ffn_expert_indices, dtype=np.int32))

    in_maps = []
    for c in range(8):
        bA = c // 2
        bB = 4 + c // 2
        r0 = (c % 2) * 1024
        xc = np.concatenate([x[bA, r0:r0 + 1024], x[bB, r0:r0 + 1024]], axis=0)
        g0 = bA * 2048 + r0
        m = dict(shared)
        m["x"] = pmajor(xc)
        m["ew"] = np.ascontiguousarray(ew[g0:g0 + 1024])
        m["ei"] = np.ascontiguousarray(ei[g0:g0 + 1024])
        in_maps.append(m)

    nc = _get_nc(debug=_debug)
    res = run_bass_kernel_spmd(nc, in_maps, core_ids=list(range(8)))

    y = np.empty((8, 2048, 1024), dtype=np.float32)
    for c in range(8):
        o = np.asarray(res.results[c]["out"], dtype=np.float32)
        bA = c // 2
        bB = 4 + c // 2
        r0 = (c % 2) * 1024
        y[bA, r0:r0 + 1024] = o[0:1024]
        y[bB, r0:r0 + 1024] = o[1024:2048]
    if _debug:
        return y, [res.results[c] for c in range(8)]
    return y


# version 10
# speedup vs baseline: 1.0632x; 1.0553x over previous
import numpy as np
from contextlib import ExitStack

import concourse.bass as bass
import concourse.mybir as mybir
from concourse.bass_utils import run_bass_kernel_spmd

F32 = mybir.dt.float32
BF16 = mybir.dt.bfloat16
I32 = mybir.dt.int32
AF = mybir.ActivationFunctionType
ALU = mybir.AluOpType
AX = mybir.AxisListType

NT = 16
NA = 8
D = 1024
DC = 8
H = 16
FF = 4096
E = 8
EPS_LN = 1e-5
EPS_RMS = 1e-6
EPS_DEN = 1e-6

ENGINES = ("tensor", "vector", "scalar", "gpsimd", "sync")


class Prog:

    DMA_SEMS = ("dx0", "dx1", "dW", "dW2", "dF0", "dF1", "dA0", "dA1", "dO", "dCC", "dM", "dCA", "dCB", "dSB")

    def __init__(self):
        self.stream = {e: [] for e in ENGINES}
        self.cnt = {}

    def do(self, eng, fn, waits=(), sem=None, inc=1, mode=None):
        for (ws, wv) in waits:
            if ws in self.DMA_SEMS:
                assert wv == self.cnt.get(ws, 0), \
                    f"prefix wait on dma sem {ws}: {wv} != {self.cnt.get(ws, 0)}"
        if sem is None:
            sem = eng
            inc = 1
        c = self.cnt.get(sem, 0) + inc
        self.cnt[sem] = c
        self.stream[eng].append((tuple(waits), fn, sem, inc, mode))
        return (sem, c)

    def drain(self):
        self.stream["tensor"].append(((), lambda e: e.drain(), None, 0, None))

    def wait_only(self, eng, waits):
        self.stream[eng].append((tuple(waits), None, None, 0, None))

    def tok(self, sem):
        return (sem, self.cnt.get(sem, 0))


def build_nc(debug=False):
    nc = bass.Bass("TRN2", target_bir_lowering=False)

    d_x = nc.dram_tensor("x", [128, 16384], F32, kind="ExternalInput").ap()
    d_wqkv = nc.dram_tensor("wqkv", [128, 24576], F32, kind="ExternalInput").ap()
    d_qn = nc.dram_tensor("qn", [1024], F32, kind="ExternalInput").ap()
    d_kn = nc.dram_tensor("kn", [1024], F32, kind="ExternalInput").ap()
    d_wout = nc.dram_tensor("wout", [128, 8192], F32, kind="ExternalInput").ap()
    d_bo = nc.dram_tensor("bo", [1024], F32, kind="ExternalInput").ap()
    d_ew = nc.dram_tensor("ew", [1024, 2], F32, kind="ExternalInput").ap()
    d_ei = nc.dram_tensor("ei", [1024, 2], I32, kind="ExternalInput").ap()
    d_w1a = nc.dram_tensor("w1a", [128, 4096], F32, kind="ExternalInput").ap()
    d_w1b = nc.dram_tensor("w1b", [128, 16384], F32, kind="ExternalInput").ap()
    d_w2a = nc.dram_tensor("w2a", [128, 16384], F32, kind="ExternalInput").ap()
    d_w2b = nc.dram_tensor("w2b", [128, 4096], F32, kind="ExternalInput").ap()
    d_out = nc.dram_tensor("out", [2048, 1024], F32, kind="ExternalOutput").ap()

    d_ccinA = nc.dram_tensor("cc_inA", [64, 1040], F32)
    d_ccoutA = nc.dram_tensor("cc_outA", [64, 1040], F32)
    d_ccinB = nc.dram_tensor("cc_inB", [64, 1040], F32)
    d_ccoutB = nc.dram_tensor("cc_outB", [64, 1040], F32)
    d_gate = nc.dram_tensor("gate_scr", [8, 1024], F32)

    dbg = {}
    if debug:
        for name, shape in [
            ("ln1_0", [128, 1024]), ("q_str", [128, 16384]), ("k_0", [128, 1024]),
            ("v_0", [128, 1024]), ("cc_sb", [64, 2080]), ("kv_bd", [128, 2048]),
            ("sk", [128, 16]), ("recip0", [128, 16]), ("attn0", [128, 1024]),
            ("y1_0", [128, 1024]), ("xr_mid", [128, 16384]), ("t_sb", [128, 4096]),
            ("u_sb", [128, 4096]), ("wrep", [128, 4096]), ("wexp", [8, 1024]),
            ("qfm", [128, 1024]), ("ksbd", [128, 256]),
        ]:
            dbg[name] = nc.dram_tensor("dbg_" + name, shape, F32, kind="ExternalOutput").ap()

    with ExitStack() as ctx:
        sb = lambda name, shape, dt: ctx.enter_context(nc.sbuf_tensor(name, shape, dt))

        x_res = sb("x_res", [128, NT, 1024], F32)
        arena1 = sb("arena1", [128, 12288], F32)
        arena2 = sb("arena2", [128, 8192], F32)
        wout_sb = sb("wout_sb", [128, DC, 1024], BF16)
        w1aT = sb("w1aT", [128, DC, 512], BF16)
        h_sb = sb("h_sb", [128, 4, 1024], BF16)
        arena4 = sb("arena4", [128, 4096], BF16)
        arena3 = sb("arena3", [128, 2080], F32)
        ksum_bd = sb("ksum_bd", [128, 2, DC, 16], BF16)
        sk_sum = sb("sk_sum", [128, 16], F32)
        ident = sb("ident", [128, 128], BF16)
        ln1_tm = sb("ln1_tm", [128, 1024], BF16)
        ln1_fm = sb("ln1_fm", [128, 2, DC, 128], BF16)
        k_tile = sb("k_tile", [128, 1024], BF16)
        v_tile = sb("v_tile", [128, 1024], BF16)
        kn_rep = sb("kn_rep", [128, 1024], BF16)
        qn_fm = sb("qn_fm", [128, 8], F32)
        bo_bf = sb("bo_bf", [1, 1024], BF16)
        ones_r = sb("ones_r", [1, 128], BF16)
        ones_c = sb("ones_c", [128, 1], BF16)
        recips = sb("recips", [128, 32], F32)
        stats = sb("stats", [128, 16], F32)
        pc8 = sb("pc8", [8, 1], F32)
        cst = sb("cst", [128, 4], F32)

        ps = ctx.enter_context(nc.psum_tensor("ps", [128, 8, 512], F32))

        a1_bf = arena1.ap().bitcast(BF16)
        vw_qkv = a1_bf.rearrange("p (k f) -> p k f", k=DC)
        vw_w1b = a1_bf[:, 0:16384].rearrange("p (c f) -> p c f", c=4)
        vw_w2b = a1_bf[:, 16384:20480].rearrange("p (c f) -> p c f", c=4)
        vw_t = a1_bf[:, 20480:24576].rearrange("p (c f) -> p c f", c=4)

        a2_bf = arena2.ap().bitcast(BF16)
        vw_qstr = a2_bf.rearrange("p (t f) -> p t f", t=NT)
        vw_u = a2_bf[:, 0:4096].rearrange("p (c f) -> p c f", c=4)
        vw_sA = a2_bf[:, 4096:8192].rearrange("p (s c f) -> p s c f", s=2, c=4)

        vw_w1a_stage = h_sb.ap()

        vw_kvbd = arena4.ap()[:, 0:2048].rearrange("p (b f) -> p b f", b=2)
        vw_attn_tm = arena4.ap()[:, 2048:3072]
        vw_attn_fm = arena4.ap()[:, 3072:4096].rearrange("p (c f) -> p c f", c=DC)
        vw_wrep = arena4.ap().rearrange("p (c f) -> p c f", c=4)

        vw_cc = arena3.ap()[0:64, 0:2080]
        vw_y1 = arena3.ap()[:, 0:1024]
        vw_tmp = arena3.ap()[:, 1024:2048]
        vw_ewb = arena2.ap()[0:8, 2048:4096]
        vw_eib = arena2.ap()[0:8, 4096:6144].bitcast(I32)
        vw_eif = arena2.ap()[0:8, 6144:8192]
        vw_wexp = arena2.ap()[0:8, 4096:6144]

        q_fm = ln1_fm.ap()[:, 1, :, :]

        vw_w2aT = (x_res.ap()[:, 8:16, :].rearrange("p t f -> p (t f)")
                   .bitcast(BF16).rearrange("p (c f) -> p c f", c=32))

        def bank(i):
            return ps[:, i, :]

        def bankpair(i):
            return ps[:, i:i + 2, :].rearrange("p b f -> p (b f)")

        b0_bf = ps[:, 0, :].bitcast(BF16)

        semnames = ["tensor", "vector", "scalar", "gpsimd", "cc",
                    "dx0", "dx1", "dW", "dW2", "dF0", "dF1", "dA0", "dA1", "dO", "dCC", "dM",
                    "dCA", "dCB", "dSB", "ccB"]
        sems = {n: ctx.enter_context(nc.semaphore(n)) for n in semnames}

        P = Prog()
        do = P.do

        out_t = d_out.rearrange("(t p) f -> t p f", p=128)
        x_t = d_x.rearrange("p (t f) -> p t f", t=NT)

        def emit_ln_stats2(x_ap, w_sq=(), s0=0, scratch=None):
            if scratch is None:
                scratch = vw_tmp
            t_sq = do("scalar", lambda e: e.activation(
                scratch, x_ap, AF.Square, accum_out=stats[:, s0:s0 + 1]), waits=list(w_sq))
            t3 = do("scalar", lambda e: e.activation(stats[:, s0 + 2:s0 + 3], stats[:, s0 + 1:s0 + 2],
                                                     AF.Copy, scale=cst[:, 2:3]))
            t4 = do("scalar", lambda e: e.activation(stats[:, s0 + 3:s0 + 4], stats[:, s0 + 2:s0 + 3], AF.Square))
            t5 = do("vector", lambda e: e.tensor_scalar(
                stats[:, s0 + 4:s0 + 5], stats[:, s0:s0 + 1], cst[:, 2:3], stats[:, s0 + 3:s0 + 4],
                ALU.mult, ALU.subtract), waits=[t4])
            t6 = do("scalar", lambda e: e.activation(stats[:, s0 + 5:s0 + 6], stats[:, s0 + 4:s0 + 5],
                                                     AF.Sqrt, bias=cst[:, 0:1]), waits=[t5])
            t7 = do("vector", lambda e: e.reciprocal(stats[:, s0 + 6:s0 + 7], stats[:, s0 + 5:s0 + 6]),
                    waits=[t6])
            return t7

        def emit_ln_stats(x_ap, scratch, w_first=()):
            t1 = do("vector", lambda e: e.scalar_tensor_tensor(
                scratch, x_ap, 1.0, x_ap, ALU.mult, ALU.mult,
                accum_out=stats[:, 0:1]), waits=list(w_first))
            t2 = do("vector", lambda e: e.tensor_reduce(stats[:, 1:2], x_ap, AX.X, ALU.add))
            t3 = do("scalar", lambda e: e.activation(stats[:, 2:3], stats[:, 1:2], AF.Copy,
                                                     scale=cst[:, 2:3]), waits=[t2])
            t4 = do("scalar", lambda e: e.activation(stats[:, 3:4], stats[:, 2:3], AF.Square))
            t5 = do("vector", lambda e: e.tensor_scalar(
                stats[:, 4:5], stats[:, 0:1], cst[:, 2:3], stats[:, 3:4],
                ALU.mult, ALU.subtract), waits=[t4, t1])
            t6 = do("scalar", lambda e: e.activation(stats[:, 5:6], stats[:, 4:5], AF.Sqrt,
                                                     bias=cst[:, 0:1]), waits=[t5])
            t7 = do("vector", lambda e: e.reciprocal(stats[:, 6:7], stats[:, 5:6]), waits=[t6])
            return t7

        t_x0 = do("sync", lambda e: e.dma_start(out=x_res[:, 0:8, :], in_=x_t[:, 0:8, :]),
                  sem="dx0", inc=16)
        t_x1 = do("sync", lambda e: e.dma_start(out=x_res[:, 8:16, :], in_=x_t[:, 8:16, :]),
                  sem="dx1", inc=16)

        do("gpsimd", lambda e: e.dma_start(out=kn_rep[:, :], in_=d_kn.unsqueeze(0).broadcast_to([128, 1024])),
           sem="dM", inc=16)
        do("gpsimd", lambda e: e.dma_start(out=bo_bf[0:1, :], in_=d_bo.unsqueeze(0)),
           sem="dM", inc=16)
        do("gpsimd", lambda e: e.dma_start(out=vw_eib, in_=d_ei.rearrange("n k -> (n k)").unsqueeze(0).broadcast_to([8, 2048])),
           sem="dM", inc=16)
        do("gpsimd", lambda e: e.dma_start(out=qn_fm[:, :], in_=d_qn.rearrange("(c p) -> p c", p=128), allow_slow_non_contiguous=True),
           sem="dM", inc=16)
        do("gpsimd", lambda e: e.dma_start(out=vw_ewb, in_=d_ew.rearrange("n k -> (n k)").unsqueeze(0).broadcast_to([8, 2048])),
           sem="dM", inc=16)
        t_misc = P.tok("dM")
        t_w1a_l = do("gpsimd", lambda e: e.dma_start(out=vw_w1a_stage, in_=d_w1a.rearrange("p (c f) -> p c f", c=4)),
                     sem="dA0", inc=16)
        wq_src = d_wqkv.rearrange("p (k f) -> p k f", k=DC)
        t_wqkv1 = do("gpsimd", lambda e: e.dma_start(out=vw_qkv[:, :, 1024:2048], in_=wq_src[:, :, 1024:2048]),
                     sem="dW", inc=16)
        do("gpsimd", lambda e: e.dma_start(out=vw_qkv[:, :, 0:1024], in_=wq_src[:, :, 0:1024]),
           sem="dW2", inc=16)
        t_wqkv2 = do("gpsimd", lambda e: e.dma_start(out=vw_qkv[:, :, 2048:3072], in_=wq_src[:, :, 2048:3072]),
                     sem="dW2", inc=16)
        do("gpsimd", lambda e: e.dma_start(out=wout_sb[:, :, :], in_=d_wout.rearrange("p (k f) -> p k f", k=DC)),
           sem="dF0", inc=16)
        t_f0_all = P.tok("dF0")
        t_qn = t_misc
        t_kn = t_misc

        do("gpsimd", lambda e: e.memset(ident[:, :], 0.0))
        t_ident = do("gpsimd", lambda e: e.affine_select(
            out=ident[:, :], in_=ident[:, :], compare_op=ALU.not_equal, fill=1.0,
            base=0, pattern=[[-1, 128]], channel_multiplier=1))
        do("gpsimd", lambda e: e.memset(ones_r[0:1, :], 1.0))
        t_ones = do("gpsimd", lambda e: e.memset(ones_c[:, :], 1.0))
        t_pc = do("gpsimd", lambda e: e.iota(pc8[:, :], [[0, 1]], channel_multiplier=1,
                                             allow_small_or_imprecise_dtypes=True))
        t_ksbd0 = do("gpsimd", lambda e: e.memset(ksum_bd.ap().rearrange("p a b c -> p (a b c)"), 0.0))
        t_kvbd0 = do("gpsimd", lambda e: e.memset(arena4.ap()[:, 0:2048], 0.0))
        do("gpsimd", lambda e: e.memset(cst[:, 0:1], EPS_LN))
        do("gpsimd", lambda e: e.memset(cst[:, 1:2], EPS_RMS))
        t_cst = do("gpsimd", lambda e: e.memset(cst[:, 2:3], 1.0 / 1024.0))
        t_cc0 = do("vector", lambda e: e.memset(vw_cc, 0.0))

        t_eif = do("vector", lambda e: e.tensor_copy(vw_eif, vw_eib), waits=[t_misc])
        t_m = do("vector", lambda e: e.tensor_single_scalar(vw_eif, vw_eif, pc8[:, 0:1], ALU.is_equal),
                 waits=[t_eif, t_pc])
        t_mw = do("vector", lambda e: e.tensor_tensor(vw_wexp, vw_eif, vw_ewb, ALU.mult),
                  waits=[t_m])
        t_wexp = do("vector", lambda e: e.tensor_reduce(
            vw_wexp[:, 0:1024], vw_wexp.rearrange("p (n k) -> p n k", k=2), AX.X, ALU.add),
            waits=[t_mw])
        t_gate_st = do("gpsimd", lambda e: e.dma_start(out=d_gate.ap(), in_=vw_wexp[:, 0:1024]),
                       sem="dF1", inc=16, waits=[t_wexp])
        if debug:
            do("gpsimd", lambda e: e.dma_start(out=dbg["wexp"], in_=vw_wexp[:, 0:1024]),
               sem="dO", inc=16, waits=[t_wexp])

        blocks = [(rc, fc) for rc in range(4) for fc in range(8)]
        t_w1aT = None
        for grp in range(4):
            chunk = blocks[grp * 8:(grp + 1) * 8]
            for idx, (rc, fc) in enumerate(chunk):
                w = [t_w1a_l, t_ident]
                if t_w1aT is not None:
                    w.append(t_w1aT)
                do("tensor", lambda e, rc=rc, fc=fc, idx=idx: e.transpose(
                    b0_bf[:, idx * 128:(idx + 1) * 128],
                    vw_w1a_stage[:, rc, fc * 128:(fc + 1) * 128], ident[:, :]),
                    waits=w if idx == 0 else (), mode="T")
            t_tp = P.tok("tensor")
            for idx, (rc, fc) in enumerate(chunk):
                do("vector", lambda e, rc=rc, fc=fc, idx=idx: e.tensor_copy(
                    w1aT[:, fc, rc * 128:(rc + 1) * 128], b0_bf[:, idx * 128:(idx + 1) * 128]),
                    waits=[t_tp] if idx == 0 else ())
            t_w1aT = P.tok("vector")

        GRP_COL = [1024, 1536, 0, 512, 2048, 2560]
        evac_qkv = {}
        tT_evac = {}
        last_qkv = {}
        kv_consumed = None
        kv_add = {}
        sq_done = None
        q_scaled = {}
        ln1_used = {}
        apply_tok = {}

        def emit_ln_statsA(i):
            t_xl = t_x0 if i < 8 else t_x1
            t1 = do("vector", lambda e, i=i: e.scalar_tensor_tensor(
                vw_attn_tm, x_res[:, i, :], 1.0, x_res[:, i, :], ALU.mult, ALU.mult,
                accum_out=stats[:, 0:1]), waits=[t_xl, t_cst])
            t2 = do("vector", lambda e, i=i: e.tensor_reduce(
                stats[:, 1:2], x_res[:, i, :], AX.X, ALU.add))
            t3 = do("scalar", lambda e: e.activation(stats[:, 2:3], stats[:, 1:2], AF.Copy,
                                                     scale=cst[:, 2:3]), waits=[t2])
            t4 = do("scalar", lambda e: e.activation(stats[:, 3:4], stats[:, 2:3], AF.Square))
            return t1, t4

        def emit_ln_statsB(t1, t4):
            t5 = do("vector", lambda e: e.tensor_scalar(
                stats[:, 4:5], stats[:, 0:1], cst[:, 2:3], stats[:, 3:4],
                ALU.mult, ALU.subtract), waits=[t4, t1])
            t6 = do("scalar", lambda e: e.activation(stats[:, 5:6], stats[:, 4:5], AF.Sqrt,
                                                     bias=cst[:, 0:1]), waits=[t5])
            return t6

        def emit_ln_apply(i, t6):
            t7 = do("vector", lambda e: e.reciprocal(stats[:, 6:7], stats[:, 5:6]), waits=[t6])
            w_ap = [t7]
            if i >= 1:
                w_ap.append(ln1_used[i - 1])
            return do("vector", lambda e, i=i: e.tensor_scalar(
                ln1_tm[:, :], x_res[:, i, :], stats[:, 2:3], stats[:, 6:7],
                ALU.subtract, ALU.mult), waits=w_ap)

        _t1, _t4 = emit_ln_statsA(0)
        _t6 = emit_ln_statsB(_t1, _t4)
        apply_tok[0] = emit_ln_apply(0, _t6)

        for i in range(NT):
            wT = [apply_tok[i], t_ident]
            if i == 0:
                wT.append(t_w1aT)
            else:
                wT.append(tT_evac[i - 1])
            for j in range(DC):
                do("tensor", lambda e, j=j: e.transpose(
                    b0_bf[:, j * 128:(j + 1) * 128], ln1_tm[:, j * 128:(j + 1) * 128], ident[:, :]),
                    waits=wT if j == 0 else (), mode="T")
            t_tp = P.tok("tensor")
            ln1_used[i] = t_tp
            wev = [t_tp]
            if i >= 2:
                wev.append(last_qkv[i - 2])
            t_ev = do("vector", lambda e, i=i: e.tensor_copy(
                ln1_fm[:, i % 2, :, :].rearrange("p c f -> p (c f)"), b0_bf[:, :]), waits=wev)
            tT_evac[i] = t_ev
            if debug and i == NT - 1:
                do("gpsimd", lambda e: e.dma_start(out=dbg["ln1_0"], in_=ln1_tm[:, :]),
                   sem="dO", inc=16, waits=[apply_tok[i]])

            ln_state = None
            for g in range(6):
                bk = 1 + (g % 2)
                col = GRP_COL[g]
                wq = [t_ev, t_wqkv1 if g < 2 else t_wqkv2]
                prev = (i, g - 2) if g >= 2 else ((i - 1, g + 4) if i >= 1 else None)
                if prev is not None:
                    wq.append(evac_qkv[prev])
                for k in range(DC):
                    do("tensor", lambda e, k=k, col=col, bk=bk, i=i: e.matmul(
                        bank(bk), ln1_fm[:, i % 2, k, :], vw_qkv[:, k, col:col + 512],
                        start=(k == 0), stop=(k == DC - 1)),
                        waits=wq if k == 0 else (), mode="P")
                t_mm = P.tok("tensor")
                if g == 5:
                    last_qkv[i] = t_mm
                if g < 2:
                    dest = k_tile[:, (g % 2) * 512:(g % 2) * 512 + 512]
                    wv = [t_mm]
                    if kv_consumed is not None and g == 0:
                        wv.append(kv_consumed)
                elif g < 4:
                    dest = vw_qstr[:, i, (g % 2) * 512:(g % 2) * 512 + 512]
                    wv = [t_mm]
                    if i in (8, 9) and g == 2:
                        wv.append(t_gate_st)
                    if i >= 1 and g == 2:
                        wv.append(q_scaled[i - 1])
                else:
                    dest = v_tile[:, (g % 2) * 512:(g % 2) * 512 + 512]
                    wv = [t_mm]
                    if kv_consumed is not None and g == 4:
                        wv.append(kv_consumed)
                t_e = do("vector", lambda e, dest=dest, bk=bk: e.tensor_copy(dest, bank(bk)), waits=wv)
                evac_qkv[(i, g)] = t_e

                if g == 1:
                    t_sqk = do("scalar", lambda e: e.activation(
                        vw_attn_fm.rearrange("p c f -> p (c f)"), k_tile[:, :], AF.Square,
                        accum_out=recips[:, 19:20]),
                        waits=[evac_qkv[(i, 0)], evac_qkv[(i, 1)]])
                    sq_done = t_sqk
                    t_rk0 = do("scalar", lambda e: e.activation(
                        recips[:, 20:21], recips[:, 19:20], AF.Sqrt,
                        scale=cst[:, 2:3], bias=cst[:, 1:2]))
                    t_kkn = do("gpsimd", lambda e: e.tensor_tensor(
                        k_tile[:, :], k_tile[:, :], kn_rep[:, :], ALU.mult),
                        waits=[t_sqk, t_kn])
                    t_rk = do("vector", lambda e: e.reciprocal(recips[:, 21:22], recips[:, 20:21]),
                              waits=[t_rk0])
                    t_krelu = do("scalar", lambda e: e.activation(
                        k_tile[:, :], k_tile[:, :], AF.Relu, scale=recips[:, 21:22]),
                        waits=[t_rk, t_kkn])
                if g == 2 and i < NT - 1:
                    ln_state = emit_ln_statsA(i + 1)
                if g == 3:
                    do("scalar", lambda e, i=i: e.activation(
                        vw_attn_fm.rearrange("p c f -> p (c f)"), vw_qstr[:, i, :], AF.Square,
                        accum_out=recips[:, 16:17]),
                        waits=[evac_qkv[(i, 2)], evac_qkv[(i, 3)]])
                    t_rq0 = do("scalar", lambda e: e.activation(
                        recips[:, 17:18], recips[:, 16:17], AF.Sqrt,
                        scale=cst[:, 2:3], bias=cst[:, 1:2]))
                    t_rq = do("vector", lambda e: e.reciprocal(recips[:, 18:19], recips[:, 17:18]),
                              waits=[t_rq0])
                    q_scaled[i] = do("scalar", lambda e, i=i: e.activation(
                        vw_qstr[:, i, :], vw_qstr[:, i, :], AF.Copy, scale=recips[:, 18:19]),
                        waits=[t_rq])
                if g == 4 and ln_state is not None:
                    ln_state = ln_state + (emit_ln_statsB(*ln_state),)

            if debug and i == NT - 1:
                do("gpsimd", lambda e: e.dma_start(out=dbg["k_0"], in_=k_tile[:, :]),
                   sem="dO", inc=16, waits=[t_krelu])
                do("gpsimd", lambda e: e.dma_start(out=dbg["v_0"], in_=v_tile[:, :]),
                   sem="dO", inc=16, waits=[evac_qkv[(i, 5)]])

            b = i // 8
            kvp = 3 if i % 2 == 0 else 5
            ksc = 16 * (i % 2)
            wkv = [t_krelu, evac_qkv[(i, 4)], evac_qkv[(i, 5)], t_ones]
            if i >= 1:
                wkv.append(kv_add[i - 1])
            for h in range(H):
                do("tensor", lambda e, h=h, kvp=kvp: e.matmul(
                    ps[0:64, kvp + h // 8, (h % 8) * 64:(h % 8) * 64 + 64],
                    k_tile[:, h * 64:(h + 1) * 64], v_tile[:, h * 64:(h + 1) * 64],
                    start=True, stop=True),
                    waits=wkv if h == 0 else (), mode="P")
            for h in range(H):
                do("tensor", lambda e, h=h, ksc=ksc: e.matmul(
                    ps[0:64, 7, ksc + h:ksc + h + 1],
                    k_tile[:, h * 64:(h + 1) * 64], ones_c[:, 0:1],
                    start=True, stop=True), mode="P")
            kv_consumed = P.tok("tensor")
            if ln_state is not None:
                apply_tok[i + 1] = emit_ln_apply(i + 1, ln_state[2])
            do("vector", lambda e, b=b, kvp=kvp: e.tensor_tensor(
                vw_cc[:, b * 1040:b * 1040 + 1024],
                ps[0:64, kvp:kvp + 2, :].rearrange("p b2 f -> p (b2 f)"),
                vw_cc[:, b * 1040:b * 1040 + 1024], ALU.add),
                waits=[kv_consumed, t_cc0])
            t_a2 = do("vector", lambda e, b=b, ksc=ksc: e.tensor_tensor(
                vw_cc[:, b * 1040 + 1024: b * 1040 + 1040],
                ps[0:64, 7, ksc:ksc + 16],
                vw_cc[:, b * 1040 + 1024: b * 1040 + 1040], ALU.add))
            kv_add[i] = t_a2

            if i == 7:
                t_ccstA = do("sync", lambda e: e.dma_start(out=d_ccinA.ap(), in_=vw_cc[:, 0:1040]),
                             sem="dCC", inc=16, waits=[kv_add[7]])
            if i == 8:
                t_ccA = do("gpsimd", lambda e: e.collective_compute(
                    "AllReduce", ALU.add,
                    replica_groups=[[0, 1], [2, 3], [4, 5], [6, 7]],
                    ins=[d_ccinA.ap().opt()], outs=[d_ccoutA.ap().opt()]),
                    sem="cc", inc=1, waits=[t_ccstA])
            if i == 9:
                cc_ksA = d_ccoutA.ap()[0:64, 1024:1040].rearrange("p (j two) -> p j two", two=2)
                do("sync", lambda e: e.dma_start(out=sk_sum[0:64, 0:8], in_=cc_ksA[:, :, 0],
                                                 allow_slow_non_contiguous=True),
                   sem="dCA", inc=16, waits=[t_ccA])
                do("sync", lambda e: e.dma_start(out=sk_sum[64:128, 0:8], in_=cc_ksA[:, :, 1],
                                                 allow_slow_non_contiguous=True),
                   sem="dCA", inc=16)

        qkv_done = last_qkv[NT - 1]

        if debug:
            do("gpsimd", lambda e: e.dma_start(out=dbg["cc_sb"], in_=vw_cc),
               sem="dO", inc=16, waits=[kv_add[NT - 1]])
            do("gpsimd", lambda e: e.dma_start(out=dbg["q_str"], in_=vw_qstr.rearrange("p t f -> p (t f)")),
               sem="dO", inc=16, waits=[q_scaled[NT - 1]])

        t_ccstB = do("sync", lambda e: e.dma_start(out=d_ccinB.ap(), in_=vw_cc[:, 1040:2080]),
                     sem="dSB", inc=16, waits=[kv_add[NT - 1]])
        t_ccB = do("gpsimd", lambda e: e.collective_compute(
            "AllReduce", ALU.add,
            replica_groups=[[0, 1], [2, 3], [4, 5], [6, 7]],
            ins=[d_ccinB.ap().opt()], outs=[d_ccoutB.ap().opt()]),
            sem="ccB", inc=1, waits=[t_ccstB])
        kvbd_even = arena4.ap()[0:64, 0:2048].rearrange("p (b2 j f) -> p b2 j f", b2=2, j=8)
        kvbd_odd = arena4.ap()[64:128, 0:2048].rearrange("p (b2 j f) -> p b2 j f", b2=2, j=8)
        cc_kvA = d_ccoutA.ap()[0:64, 0:1024].rearrange("p (j two f) -> p j two f", j=8, two=2)
        cc_kvB = d_ccoutB.ap()[0:64, 0:1024].rearrange("p (j two f) -> p j two f", j=8, two=2)
        do("gpsimd", lambda e: e.dma_start(out=kvbd_even[:, 0, :, 0:64], in_=cc_kvA[:, :, 0, :]),
           sem="dCA", inc=16, waits=[t_ccA, t_kvbd0])
        do("gpsimd", lambda e: e.dma_start(out=kvbd_odd[:, 0, :, 64:128], in_=cc_kvA[:, :, 1, :]),
           sem="dCA", inc=16)
        t_rbA = P.tok("dCA")
        cc_ksB = d_ccoutB.ap()[0:64, 1024:1040].rearrange("p (j two) -> p j two", two=2)
        do("sync", lambda e: e.dma_start(out=sk_sum[0:64, 8:16], in_=cc_ksB[:, :, 0],
                                         allow_slow_non_contiguous=True),
           sem="dCB", inc=16, waits=[t_ccB])
        do("sync", lambda e: e.dma_start(out=sk_sum[64:128, 8:16], in_=cc_ksB[:, :, 1],
                                         allow_slow_non_contiguous=True),
           sem="dCB", inc=16)
        do("gpsimd", lambda e: e.dma_start(out=kvbd_even[:, 1, :, 0:64], in_=cc_kvB[:, :, 0, :]),
           sem="dCB", inc=16, waits=[t_ccB])
        do("gpsimd", lambda e: e.dma_start(out=kvbd_odd[:, 1, :, 64:128], in_=cc_kvB[:, :, 1, :]),
           sem="dCB", inc=16)
        t_rbB = P.tok("dCB")
        for j in range(8):
            do("vector", lambda e, j=j: e.tensor_copy(
                ksum_bd[0:64, 0, j, 2 * j: 2 * j + 1], sk_sum[0:64, j: j + 1]),
                waits=[t_rbA, t_ksbd0] if j == 0 else ())
            do("vector", lambda e, j=j: e.tensor_copy(
                ksum_bd[64:128, 0, j, 2 * j + 1: 2 * j + 2], sk_sum[64:128, j: j + 1]))
        t_ksbdA = P.tok("vector")
        t_ksbdB = None

        b0_free = tT_evac[NT - 1]
        qfm_used = None
        attn_evac = None
        t_evac_t = None
        wout_evac = {}
        norm_read = None
        t_store = {}
        attn_div_prev = None
        last_wout = None
        attn_all_done = None
        pend = None

        for i in range(NT):
            b = i // 8
            if i == 4:
                for j in range(8):
                    do("vector", lambda e, j=j: e.tensor_copy(
                        ksum_bd[0:64, 1, j, 2 * j: 2 * j + 1], sk_sum[0:64, 8 + j: 9 + j]),
                        waits=[t_rbB] if j == 0 else ())
                    do("vector", lambda e, j=j: e.tensor_copy(
                        ksum_bd[64:128, 1, j, 2 * j + 1: 2 * j + 2], sk_sum[64:128, 8 + j: 9 + j]))
                t_ksbdB = P.tok("vector")
                if debug:
                    do("gpsimd", lambda e: e.dma_start(out=dbg["kv_bd"], in_=vw_kvbd.rearrange("p b f -> p (b f)")),
                       sem="dO", inc=16, waits=[t_rbA, t_rbB])
                    do("gpsimd", lambda e: e.dma_start(out=dbg["sk"], in_=sk_sum[:, :]),
                       sem="dO", inc=16, waits=[t_rbB])
                    do("gpsimd", lambda e: e.dma_start(out=dbg["ksbd"], in_=ksum_bd.ap().rearrange("p a b c -> p (a b c)")),
                       sem="dO", inc=16, waits=[t_ksbdB])
            t_kvbd_b = t_rbA if b == 0 else t_rbB
            t_ksbd_b = t_ksbdA if b == 0 else t_ksbdB
            wT = [q_scaled[i], b0_free, t_kvbd_b, t_ksbd_b]
            if i == 0:
                wT.append(qkv_done)
            if qfm_used is not None:
                wT.append(qfm_used)
            for j in range(DC):
                do("tensor", lambda e, i=i, j=j: e.transpose(
                    b0_bf[:, j * 128:(j + 1) * 128], vw_qstr[:, i, j * 128:(j + 1) * 128], ident[:, :]),
                    waits=wT if j == 0 else (), mode="T")
            t_qT = P.tok("tensor")
            for j in range(DC):
                do("vector", lambda e, j=j: e.tensor_scalar(
                    q_fm[:, j, :], b0_bf[:, j * 128:(j + 1) * 128],
                    qn_fm[:, j:j + 1], 0.0, ALU.mult, ALU.max),
                    waits=[t_qT, t_qn] if j == 0 else ())
            t_qfm = P.tok("vector")
            b0_free = t_qfm
            if debug and i == NT - 1:
                do("gpsimd", lambda e: e.dma_start(out=dbg["qfm"], in_=q_fm.rearrange("p c f -> p (c f)")),
                   sem="dO", inc=16, waits=[t_qfm])

            wN = [t_qfm, t_ksbd_b]
            if norm_read is not None:
                wN.append(norm_read)
            if i == 0:
                wN.append(kv_add[NT - 1])
            for j in range(DC):
                do("tensor", lambda e, j=j, b=b: e.matmul(
                    bank(7)[:, 0:16], q_fm[:, j, :], ksum_bd[:, b, j, :],
                    start=(j == 0), stop=(j == DC - 1)),
                    waits=wN if j == 0 else (), mode="P")
            t_norm = P.tok("tensor")
            wA = [t_kvbd_b]
            if t_evac_t is not None:
                wA.append(t_evac_t)
            elif attn_evac is not None:
                wA.append(attn_evac)
            if i == 0:
                wA += [evac_qkv[(NT - 1, 4)], evac_qkv[(NT - 1, 5)]]
            for j in range(DC):
                do("tensor", lambda e, j=j, b=b: e.matmul(
                    bankpair(1)[:, j * 128:(j + 1) * 128], q_fm[:, j, :], vw_kvbd[:, b, j * 128:(j + 1) * 128],
                    start=True, stop=True),
                    waits=wA if j == 0 else (), mode="P")
            t_attnmm = P.tok("tensor")
            qfm_used = t_attnmm
            if i == NT - 1:
                attn_all_done = t_attnmm

            t_eps = do("scalar", lambda e: e.activation(
                recips[:, 16:32], bank(7)[:, 0:16], AF.Identity, bias=cst[:, 1:2]),
                waits=[t_norm])
            norm_read = t_eps
            t_rec = do("vector", lambda e: e.reciprocal(recips[:, 0:16], recips[:, 16:32]),
                       waits=[t_eps])
            if debug and i == NT - 1:
                do("gpsimd", lambda e: e.dma_start(out=dbg["recip0"], in_=recips[:, 0:16]),
                   sem="dO", inc=16, waits=[t_rec])

            wD = [t_attnmm, t_rec]
            if i == 0:
                wD.append(sq_done)
            if attn_div_prev is not None:
                wD.append(attn_div_prev)
            t_div = do("vector", lambda e: e.tensor_tensor(
                vw_attn_tm.rearrange("p (h d) -> p h d", d=64),
                bankpair(1).rearrange("p (h d) -> p h d", d=64),
                recips[:, 0:16].unsqueeze(-1).broadcast_to([128, 16, 64]),
                ALU.mult), waits=wD)
            attn_evac = t_div
            if debug and i == NT - 1:
                do("gpsimd", lambda e: e.dma_start(out=dbg["attn0"], in_=vw_attn_tm),
                   sem="dO", inc=16, waits=[t_div])

            for j in range(DC):
                do("tensor", lambda e, j=j: e.transpose(
                    b0_bf[:, j * 128:(j + 1) * 128], vw_attn_tm[:, j * 128:(j + 1) * 128], ident[:, :]),
                    waits=[t_div, b0_free] if j == 0 else (), mode="T")
            t_aT = P.tok("tensor")
            t_afm = do("vector", lambda e: e.tensor_copy(
                vw_attn_fm.rearrange("p c f -> p (c f)"), b0_bf[:, :]),
                waits=[t_aT] + ([sq_done] if i == 0 else []))
            attn_div_prev = t_afm
            b0_free = t_afm

            wb = 3 if i % 2 == 0 else 5
            wW = [t_afm, t_f0_all, t_misc, t_ones]
            if i < 2:
                wW.append(kv_add[NT - 1])
            elif (i - 2) in wout_evac:
                wW.append(wout_evac[i - 2])
            for half in range(2):
                for j in range(DC):
                    do("tensor", lambda e, j=j, half=half, wb=wb: e.matmul(
                        bank(wb + half), vw_attn_fm[:, j, :], wout_sb[:, j, half * 512:(half + 1) * 512],
                        start=(j == 0), stop=False),
                        waits=wW if (j == 0 and half == 0) else (), mode="P")
                do("tensor", lambda e, half=half, wb=wb: e.matmul(
                    bank(wb + half), ones_r[0:1, :], bo_bf[0:1, half * 512:(half + 1) * 512],
                    start=False, stop=True), mode="P")
            t_wout = P.tok("tensor")
            last_wout = t_wout

            if pend is not None:
                pi, t_ln2p = pend
                for j in range(DC):
                    do("tensor", lambda e, j=j: e.transpose(
                        b0_bf[:, j * 128:(j + 1) * 128], ln1_tm[:, j * 128:(j + 1) * 128], ident[:, :]),
                        waits=[t_ln2p, b0_free] if j == 0 else (), mode="T")
                t_l2T = P.tok("tensor")
                l2T_done = t_l2T
                t_l2fm = do("vector", lambda e: e.tensor_copy(
                    ln1_fm[:, 0, :, :].rearrange("p c f -> p (c f)"), b0_bf[:, :]), waits=[t_l2T])
                b0_free = t_l2fm
                wt_ = [t_l2fm, t_div, t_w1aT]
                for m in range(4):
                    for k in range(DC):
                        do("tensor", lambda e, m=m, k=k: e.matmul(
                            bankpair(1)[:, m * 128:(m + 1) * 128], w1aT[:, k, m * 128:(m + 1) * 128],
                            ln1_fm[:, 0, k, :], start=(k == 0), stop=(k == DC - 1)),
                            waits=wt_ if (m == 0 and k == 0) else (), mode="P")
                t_tmm = P.tok("tensor")
                wev = [t_tmm]
                if pi == 0:
                    wev.append(qkv_done)
                t_evac_t = do("vector", lambda e, pi=pi: e.tensor_copy(
                    vw_t[:, :, pi * 128:(pi + 1) * 128],
                    bankpair(1)[:, 0:512].rearrange("p (m n) -> p m n", n=128)), waits=wev)
                pend = None

            wY = [t_wout]
            if i == 0:
                wY += [t_ccstA, t_ccstB]
            t_y1 = do("vector", lambda e, wb=wb: e.tensor_scalar(
                vw_y1, bankpair(wb), 1.0, 0.0, ALU.mult, ALU.add,
                accum_out=stats[:, 1:2]), waits=wY)
            wout_evac[i] = t_y1
            if debug and i == NT - 1:
                do("gpsimd", lambda e: e.dma_start(out=dbg["y1_0"], in_=vw_y1),
                   sem="dO", inc=16, waits=[t_y1])
            t_rs = emit_ln_stats2(vw_y1, [t_y1])
            t_lnA = do("vector", lambda e: e.tensor_scalar(
                vw_tmp, vw_y1, stats[:, 2:3], stats[:, 6:7], ALU.subtract, ALU.mult),
                waits=[t_rs])
            t_res = do("vector", lambda e, i=i: e.scalar_tensor_tensor(
                x_res[:, i, :], x_res[:, i, :], 1.0, vw_tmp, ALU.mult, ALU.add,
                accum_out=stats[:, 9:10]), waits=[t_lnA])

            if i >= 8:
                t_store[i] = do("sync", lambda e, i=i: e.dma_start(out=out_t[i], in_=x_res[:, i, :]),
                                sem="dO", inc=16, waits=[t_res])
            else:
                t_rs2 = emit_ln_stats2(x_res[:, i, :], [t_res], s0=8)
                w_l2 = [t_rs2]
                if i == 0:
                    w_l2.append(ln1_used[NT - 1])
                else:
                    w_l2.append(l2T_done)
                t_ln2 = do("vector", lambda e, i=i: e.tensor_scalar(
                    ln1_tm[:, :], x_res[:, i, :], stats[:, 10:11], stats[:, 14:15], ALU.subtract, ALU.mult),
                    waits=w_l2)
                pend = (i, t_ln2)

            if i == NT - 1 and pend is not None:
                pass

        t_t_all = t_evac_t
        if debug:
            do("gpsimd", lambda e: e.dma_start(
                out=dbg["xr_mid"], in_=x_res.ap().rearrange("p t f -> p (t f)")),
                sem="dO", inc=16, waits=[P.tok("vector")])

        do("gpsimd", lambda e: e.dma_start(
            out=vw_w1b, in_=d_w1b.rearrange("p (c f) -> p c f", c=4)),
            sem="dF1", inc=16, waits=[qkv_done])
        do("gpsimd", lambda e: e.dma_start(
            out=vw_w2b, in_=d_w2b.rearrange("p (c f) -> p c f", c=4)),
            sem="dF1", inc=16)
        for cpair in range(4):
            for e2 in range(2):
                do("gpsimd", lambda e, cpair=cpair, e2=e2: e.dma_start(
                    out=vw_wrep[e2 * 64:(e2 + 1) * 64, cpair, :],
                    in_=d_gate.ap()[2 * cpair + e2: 2 * cpair + e2 + 1, :].broadcast_to([64, 1024])),
                    sem="dF1", inc=16,
                    waits=([P.tok("dF1"), last_wout, attn_all_done]
                           + ([P.tok("dO")] if debug else [])) if (cpair == 0 and e2 == 0) else ())
        t_f1_all = P.tok("dF1")
        if debug:
            do("gpsimd", lambda e: e.dma_start(out=dbg["wrep"], in_=vw_wrep.rearrange("p c f -> p (c f)")),
               sem="dO", inc=16, waits=[t_f1_all])
            do("gpsimd", lambda e: e.dma_start(out=dbg["t_sb"], in_=vw_t.rearrange("p c f -> p (c f)")),
               sem="dO", inc=16, waits=[t_t_all])

        stores_done = P.tok("dO")
        t_sA = {}
        sA_used = {}
        b0f = b0_free
        for c8 in range(8):
            slot = c8 % 2
            wL = [qfm_used]
            if c8 >= 2:
                wL.append(sA_used[c8 - 2])
            t_sA[c8] = do("gpsimd", lambda e, c8=c8, slot=slot: e.dma_start(
                out=vw_sA[:, slot, :, :].rearrange("p c f -> p (c f)"),
                in_=d_w2a.rearrange("p (c f) -> p c f", c=4)[:, :, c8 * 512:(c8 + 1) * 512]),
                sem="dA0" if slot == 0 else "dA1", inc=16, waits=wL)
            blocks = [(rc, fc) for rc in range(4) for fc in range(4)]
            for halfg in range(2):
                chunk = blocks[halfg * 8:(halfg + 1) * 8]
                for idx, (rc, fc) in enumerate(chunk):
                    w = [t_sA[c8], b0f] if idx == 0 else ()
                    do("tensor", lambda e, slot=slot, rc=rc, fc=fc, idx=idx: e.transpose(
                        b0_bf[:, idx * 128:(idx + 1) * 128],
                        vw_sA[:, slot, rc, fc * 128:(fc + 1) * 128], ident[:, :]),
                        waits=w, mode="T")
                t_tp = P.tok("tensor")
                for idx, (rc, fc) in enumerate(chunk):
                    do("vector", lambda e, c8=c8, rc=rc, fc=fc, idx=idx: e.tensor_copy(
                        vw_w2aT[:, c8 * 4 + fc, rc * 128:(rc + 1) * 128],
                        b0_bf[:, idx * 128:(idx + 1) * 128]),
                        waits=[t_tp, stores_done] if idx == 0 else ())
                b0f = P.tok("vector")
            sA_used[c8] = b0f
        t_w2aT = b0f

        u_evac = {}
        gel = {}
        umm = {}
        u_free = None
        first_h = True
        seq = [(c, nb, Fc) for c in range(4) for nb in range(2) for Fc in range(32)]

        def emit_u(c, nb, Fc):
            w = [gel[(c, nb, Fc)]]
            if Fc == 0:
                w.append(u_free if u_free is not None else wout_evac[15])
            u_start = (Fc == 0)
            u_stop = (Fc == 31)
            do("tensor", lambda e, c=c, Fc=Fc, s=u_start, st=u_stop: e.matmul(
                ps[0:64, 5, :], vw_w2aT[:, Fc, (2 * c) * 64:(2 * c) * 64 + 64],
                h_sb[:, Fc % 4, 0:512],
                start=s, stop=st, tile_position=(0, 0)),
                waits=w, mode="C")
            do("tensor", lambda e, c=c, Fc=Fc, s=u_start, st=u_stop: e.matmul(
                ps[64:128, 5, :], vw_w2aT[:, Fc, (2 * c + 1) * 64:(2 * c + 1) * 64 + 64],
                h_sb[:, Fc % 4, 512:1024],
                start=s, stop=st, tile_position=(0, 64)),
                mode="C")
            umm[(c, nb, Fc)] = P.tok("tensor")

        def emit_u_evac(c, nb):
            nonlocal_u = [umm[(c, nb, 31)], t_f1_all, qfm_used]
            u_evac[(c, nb)] = do("vector", lambda e, c=c, nb=nb: e.tensor_tensor(
                vw_u[:, c, nb * 512:(nb + 1) * 512], bank(5),
                vw_wrep[:, c, nb * 512:(nb + 1) * 512], ALU.mult), waits=nonlocal_u)
            return u_evac[(c, nb)]

        for s, (c, nb, Fc) in enumerate(seq):
            hp = 3 if Fc % 2 == 0 else 6
            w = [t_f1_all, t_w2aT]
            if first_h:
                w += [t_t_all, wout_evac[14], wout_evac[15], norm_read]
                first_h = False
            if (c, nb, Fc - 2) in gel:
                w.append(gel[(c, nb, Fc - 2)])
            elif Fc <= 1 and (c, nb) != (0, 0):
                prev = (c, nb - 1) if nb == 1 else (c - 1, 1)
                w.append(gel[prev + (30 + Fc,)])
            do("tensor", lambda e, c=c, nb=nb, Fc=Fc, hp=hp: e.matmul(
                bank(hp), vw_w1b[0:64, c, Fc * 128:(Fc + 1) * 128],
                vw_t[0:64, c, nb * 512:(nb + 1) * 512],
                start=True, stop=True, tile_position=(0, 0)),
                waits=w, mode="R")
            do("tensor", lambda e, c=c, nb=nb, Fc=Fc, hp=hp: e.matmul(
                bank(hp + 1), vw_w1b[64:128, c, Fc * 128:(Fc + 1) * 128],
                vw_t[64:128, c, nb * 512:(nb + 1) * 512],
                start=True, stop=True, tile_position=(64, 0)),
                mode="R")
            t_h = P.tok("tensor")
            wg = [t_h]
            if (c, nb, Fc - 4) in umm:
                wg.append(umm[(c, nb, Fc - 4)])
            gel[(c, nb, Fc)] = do("scalar", lambda e, Fc=Fc, hp=hp: e.activation(
                h_sb[:, Fc % 4, :], bankpair(hp), AF.Gelu), waits=wg)
            if s > 0:
                pc_, pn_, pf_ = seq[s - 1]
                emit_u(pc_, pn_, pf_)
                if pf_ == 31:
                    u_free = emit_u_evac(pc_, pn_)
        emit_u(*seq[-1])
        u_free = emit_u_evac(3, 1)
        t_u_all = P.tok("vector")
        if debug:
            do("gpsimd", lambda e: e.dma_start(out=dbg["u_sb"], in_=vw_u.rearrange("p c f -> p (c f)")),
               sem="dO", inc=16, waits=[t_u_all])

        y_prev = None
        for nt in range(NA):
            wy = [t_u_all]
            if y_prev is not None:
                wy.append(y_prev)
            for fh in range(2):
                if not (nt == 0 and fh == 0):
                    P.drain()
                for e_ in range(E):
                    half = e_ % 2
                    bk = (1 + fh) if half == 0 else (3 + fh)
                    s = (e_ < 2)
                    st = (e_ >= E - 2)
                    do("tensor", lambda en, e_=e_, fh=fh, bk=bk, nt=nt, s=s, st=st: en.matmul(
                        bank(bk), vw_u[(e_ % 2) * 64:(e_ % 2) * 64 + 64, e_ // 2, nt * 128:(nt + 1) * 128],
                        vw_w2b[(e_ % 2) * 64:(e_ % 2) * 64 + 64, e_ // 2, fh * 512:(fh + 1) * 512],
                        start=s, stop=st, tile_position=((e_ % 2) * 64, 0)),
                        waits=wy if (e_ == 0 and fh == 0) else (), mode="R")
            t_ymm = P.tok("tensor")
            do("vector", lambda e: e.tensor_copy(vw_y1, bankpair(1)), waits=[t_ymm])
            t_c2 = do("vector", lambda e: e.scalar_tensor_tensor(
                vw_y1, bankpair(3), 1.0, vw_y1, ALU.mult, ALU.add,
                accum_out=stats[:, 1:2]))
            y_prev = t_c2
            t_rs = emit_ln_stats2(vw_y1, [t_c2])
            t_lnA = do("vector", lambda e: e.tensor_scalar(
                vw_tmp, vw_y1, stats[:, 2:3], stats[:, 6:7], ALU.subtract, ALU.mult),
                waits=[t_rs])
            t_res = do("vector", lambda e, nt=nt: e.scalar_tensor_tensor(
                x_res[:, nt, :], x_res[:, nt, :], 1.0, vw_tmp, ALU.mult, ALU.add), waits=[t_lnA])
            t_store[nt] = do("sync", lambda e, nt=nt: e.dma_start(out=out_t[nt], in_=x_res[:, nt, :]),
                             sem="dO", inc=16, waits=[t_res])

        P.wait_only("sync", [P.tok("dO")])

        block = ctx.enter_context(nc.Block())
        for eng_name in ENGINES:
            ops = P.stream[eng_name]
            if not ops:
                continue

            def body(eng, ops=ops, eng_name=eng_name):
                seen = {}
                last_mode = None
                for waits, fn, sem, inc, mode in ops:
                    if eng_name == "tensor" and mode is not None:
                        if last_mode is not None and mode != last_mode:
                            eng.drain()
                        last_mode = mode
                    for (ws, wv) in waits:
                        if ws == eng_name or wv <= 0:
                            continue
                        if seen.get(ws, 0) >= wv:
                            continue
                        eng.wait_ge(sems[ws], wv)
                        seen[ws] = wv
                    if fn is None:
                        continue
                    inst = fn(eng)
                    if inst is not None and sem is not None:
                        inst.then_inc(sems[sem], inc)

            getattr(block, eng_name)(body)

    return nc


_CACHE = {}


def _get_nc(debug=False):
    key = bool(debug)
    if key not in _CACHE:
        _CACHE[key] = build_nc(debug=key)
    return _CACHE[key]


def kernel(L0_attn_x, L0_attn_W_qkv, L0_attn_qn_w, L0_attn_kn_w, L0_attn_W_out,
           L0_attn_b_out, L0_ffn_expert_weights, L0_ffn_expert_indices,
           L0_ffn_w1_A, L0_ffn_w1_B, L0_ffn_w2_A, L0_ffn_w2_B, _debug=False):
    def pmajor(a):
        a = np.asarray(a, dtype=np.float32)
        r = a.shape[0] // 128
        return np.ascontiguousarray(
            a.reshape(r, 128, a.shape[1]).transpose(1, 0, 2).reshape(128, r * a.shape[1]))

    x = np.ascontiguousarray(np.asarray(L0_attn_x, dtype=np.float32))
    shared = dict(
        wqkv=pmajor(L0_attn_W_qkv),
        qn=np.ascontiguousarray(np.asarray(L0_attn_qn_w, dtype=np.float32)),
        kn=np.ascontiguousarray(np.asarray(L0_attn_kn_w, dtype=np.float32)),
        wout=pmajor(L0_attn_W_out),
        bo=np.ascontiguousarray(np.asarray(L0_attn_b_out, dtype=np.float32)),
        w1a=pmajor(L0_ffn_w1_A),
        w1b=pmajor(L0_ffn_w1_B),
        w2a=pmajor(L0_ffn_w2_A),
        w2b=pmajor(L0_ffn_w2_B),
    )
    ew = np.ascontiguousarray(np.asarray(L0_ffn_expert_weights, dtype=np.float32))
    ei = np.ascontiguousarray(np.asarray(L0_ffn_expert_indices, dtype=np.int32))

    in_maps = []
    for c in range(8):
        bA = c // 2
        bB = 4 + c // 2
        r0 = (c % 2) * 1024
        xc = np.concatenate([x[bA, r0:r0 + 1024], x[bB, r0:r0 + 1024]], axis=0)
        g0 = bA * 2048 + r0
        m = dict(shared)
        m["x"] = pmajor(xc)
        m["ew"] = np.ascontiguousarray(ew[g0:g0 + 1024])
        m["ei"] = np.ascontiguousarray(ei[g0:g0 + 1024])
        in_maps.append(m)

    nc = _get_nc(debug=_debug)
    res = run_bass_kernel_spmd(nc, in_maps, core_ids=list(range(8)))

    y = np.empty((8, 2048, 1024), dtype=np.float32)
    for c in range(8):
        o = np.asarray(res.results[c]["out"], dtype=np.float32)
        bA = c // 2
        bB = 4 + c // 2
        r0 = (c % 2) * 1024
        y[bA, r0:r0 + 1024] = o[0:1024]
        y[bB, r0:r0 + 1024] = o[1024:2048]
    if _debug:
        return y, [res.results[c] for c in range(8)]
    return y


# version 11
# speedup vs baseline: 1.0663x; 1.0029x over previous
import numpy as np
from contextlib import ExitStack

import concourse.bass as bass
import concourse.mybir as mybir
from concourse.bass_utils import run_bass_kernel_spmd

F32 = mybir.dt.float32
BF16 = mybir.dt.bfloat16
I32 = mybir.dt.int32
AF = mybir.ActivationFunctionType
ALU = mybir.AluOpType
AX = mybir.AxisListType

NT = 16
NA = 8
D = 1024
DC = 8
H = 16
FF = 4096
E = 8
EPS_LN = 1e-5
EPS_RMS = 1e-6
EPS_DEN = 1e-6

ENGINES = ("tensor", "vector", "scalar", "gpsimd", "sync")


class Prog:

    DMA_SEMS = ("dx0", "dx1", "dx2", "dW", "dW2", "dF0", "dF1", "dA0", "dA1", "dO", "dCC", "dM", "dCA", "dCB", "dSB")

    def __init__(self):
        self.stream = {e: [] for e in ENGINES}
        self.cnt = {}

    def do(self, eng, fn, waits=(), sem=None, inc=1, mode=None):
        for (ws, wv) in waits:
            if ws in self.DMA_SEMS:
                assert wv == self.cnt.get(ws, 0), \
                    f"prefix wait on dma sem {ws}: {wv} != {self.cnt.get(ws, 0)}"
        if sem is None:
            sem = eng
            inc = 1
        c = self.cnt.get(sem, 0) + inc
        self.cnt[sem] = c
        self.stream[eng].append((tuple(waits), fn, sem, inc, mode))
        return (sem, c)

    def drain(self):
        self.stream["tensor"].append(((), lambda e: e.drain(), None, 0, None))

    def wait_only(self, eng, waits):
        self.stream[eng].append((tuple(waits), None, None, 0, None))

    def tok(self, sem):
        return (sem, self.cnt.get(sem, 0))


def build_nc(debug=False):
    nc = bass.Bass("TRN2", target_bir_lowering=False)

    d_x = nc.dram_tensor("x", [128, 16384], F32, kind="ExternalInput").ap()
    d_wqkv = nc.dram_tensor("wqkv", [128, 24576], F32, kind="ExternalInput").ap()
    d_qn = nc.dram_tensor("qn", [1024], F32, kind="ExternalInput").ap()
    d_kn = nc.dram_tensor("kn", [1024], F32, kind="ExternalInput").ap()
    d_wout = nc.dram_tensor("wout", [128, 8192], F32, kind="ExternalInput").ap()
    d_bo = nc.dram_tensor("bo", [1024], F32, kind="ExternalInput").ap()
    d_ew = nc.dram_tensor("ew", [1024, 2], F32, kind="ExternalInput").ap()
    d_ei = nc.dram_tensor("ei", [1024, 2], I32, kind="ExternalInput").ap()
    d_w1a = nc.dram_tensor("w1a", [128, 4096], F32, kind="ExternalInput").ap()
    d_w1b = nc.dram_tensor("w1b", [128, 16384], F32, kind="ExternalInput").ap()
    d_w2a = nc.dram_tensor("w2a", [128, 16384], F32, kind="ExternalInput").ap()
    d_w2b = nc.dram_tensor("w2b", [128, 4096], F32, kind="ExternalInput").ap()
    d_out = nc.dram_tensor("out", [2048, 1024], F32, kind="ExternalOutput").ap()

    d_ccinA = nc.dram_tensor("cc_inA", [64, 1040], F32)
    d_ccoutA = nc.dram_tensor("cc_outA", [64, 1040], F32)
    d_ccinB = nc.dram_tensor("cc_inB", [64, 1040], F32)
    d_ccoutB = nc.dram_tensor("cc_outB", [64, 1040], F32)
    d_gate = nc.dram_tensor("gate_scr", [8, 1024], F32)

    dbg = {}
    if debug:
        for name, shape in [
            ("ln1_0", [128, 1024]), ("q_str", [128, 16384]), ("k_0", [128, 1024]),
            ("v_0", [128, 1024]), ("cc_sb", [64, 2080]), ("kv_bd", [128, 2048]),
            ("sk", [128, 16]), ("recip0", [128, 16]), ("attn0", [128, 1024]),
            ("y1_0", [128, 1024]), ("xr_mid", [128, 16384]), ("t_sb", [128, 4096]),
            ("u_sb", [128, 4096]), ("wrep", [128, 4096]), ("wexp", [8, 1024]),
            ("qfm", [128, 1024]), ("ksbd", [128, 256]),
        ]:
            dbg[name] = nc.dram_tensor("dbg_" + name, shape, F32, kind="ExternalOutput").ap()

    with ExitStack() as ctx:
        sb = lambda name, shape, dt: ctx.enter_context(nc.sbuf_tensor(name, shape, dt))

        x_res = sb("x_res", [128, NT, 1024], F32)
        arena1 = sb("arena1", [128, 12288], F32)
        arena2 = sb("arena2", [128, 8192], F32)
        wout_sb = sb("wout_sb", [128, DC, 1024], BF16)
        w1aT = sb("w1aT", [128, DC, 512], BF16)
        h_sb = sb("h_sb", [128, 4, 1024], BF16)
        arena4 = sb("arena4", [128, 4096], BF16)
        arena3 = sb("arena3", [128, 2080], F32)
        ksum_bd = sb("ksum_bd", [128, 2, DC, 16], BF16)
        sk_sum = sb("sk_sum", [128, 16], F32)
        ident = sb("ident", [128, 128], BF16)
        ln1_tm = sb("ln1_tm", [128, 1024], BF16)
        ln1_fm = sb("ln1_fm", [128, 2, DC, 128], BF16)
        k_tile = sb("k_tile", [128, 1024], BF16)
        v_tile = sb("v_tile", [128, 1024], BF16)
        kn_rep = sb("kn_rep", [128, 1024], BF16)
        qn_fm = sb("qn_fm", [128, 8], F32)
        bo_bf = sb("bo_bf", [1, 1024], BF16)
        ones_r = sb("ones_r", [1, 128], BF16)
        ones_c = sb("ones_c", [128, 1], BF16)
        recips = sb("recips", [128, 32], F32)
        stats = sb("stats", [128, 16], F32)
        pc8 = sb("pc8", [8, 1], F32)
        cst = sb("cst", [128, 4], F32)

        ps = ctx.enter_context(nc.psum_tensor("ps", [128, 8, 512], F32))

        a1_bf = arena1.ap().bitcast(BF16)
        vw_qkv = a1_bf.rearrange("p (k f) -> p k f", k=DC)
        vw_w1b = a1_bf[:, 0:16384].rearrange("p (c f) -> p c f", c=4)
        vw_w2b = a1_bf[:, 16384:20480].rearrange("p (c f) -> p c f", c=4)
        vw_t = a1_bf[:, 20480:24576].rearrange("p (c f) -> p c f", c=4)

        a2_bf = arena2.ap().bitcast(BF16)
        vw_qstr = a2_bf.rearrange("p (t f) -> p t f", t=NT)
        vw_u = a2_bf[:, 0:4096].rearrange("p (c f) -> p c f", c=4)
        vw_sA = a2_bf[:, 4096:8192].rearrange("p (s c f) -> p s c f", s=2, c=4)

        vw_w1a_stage = h_sb.ap()

        vw_kvbd = arena4.ap()[:, 0:2048].rearrange("p (b f) -> p b f", b=2)
        vw_attn_tm = arena4.ap()[:, 2048:3072]
        vw_attn_fm = arena4.ap()[:, 3072:4096].rearrange("p (c f) -> p c f", c=DC)
        vw_wrep = arena4.ap().rearrange("p (c f) -> p c f", c=4)

        vw_cc = arena3.ap()[0:64, 0:2080]
        vw_y1 = arena3.ap()[:, 0:1024]
        vw_tmp = arena3.ap()[:, 1024:2048]
        vw_ewb = arena2.ap()[0:8, 2048:4096]
        vw_eib = arena2.ap()[0:8, 4096:6144].bitcast(I32)
        vw_eif = arena2.ap()[0:8, 6144:8192]
        vw_wexp = arena2.ap()[0:8, 4096:6144]

        q_fm = ln1_fm.ap()[:, 1, :, :]

        vw_w2aT = (x_res.ap()[:, 8:16, :].rearrange("p t f -> p (t f)")
                   .bitcast(BF16).rearrange("p (c f) -> p c f", c=32))

        def bank(i):
            return ps[:, i, :]

        def bankpair(i):
            return ps[:, i:i + 2, :].rearrange("p b f -> p (b f)")

        b0_bf = ps[:, 0, :].bitcast(BF16)

        semnames = ["tensor", "vector", "scalar", "gpsimd", "cc",
                    "dx0", "dx1", "dx2", "dW", "dW2", "dF0", "dF1", "dA0", "dA1", "dO", "dCC", "dM",
                    "dCA", "dCB", "dSB", "ccB"]
        sems = {n: ctx.enter_context(nc.semaphore(n)) for n in semnames}

        P = Prog()
        do = P.do

        out_t = d_out.rearrange("(t p) f -> t p f", p=128)
        x_t = d_x.rearrange("p (t f) -> p t f", t=NT)

        def emit_ln_stats2(x_ap, w_sq=(), s0=0, scratch=None):
            if scratch is None:
                scratch = vw_tmp
            t_sq = do("scalar", lambda e: e.activation(
                scratch, x_ap, AF.Square, accum_out=stats[:, s0:s0 + 1]), waits=list(w_sq))
            t3 = do("scalar", lambda e: e.activation(stats[:, s0 + 2:s0 + 3], stats[:, s0 + 1:s0 + 2],
                                                     AF.Copy, scale=cst[:, 2:3]))
            t4 = do("scalar", lambda e: e.activation(stats[:, s0 + 3:s0 + 4], stats[:, s0 + 2:s0 + 3], AF.Square))
            t5 = do("vector", lambda e: e.tensor_scalar(
                stats[:, s0 + 4:s0 + 5], stats[:, s0:s0 + 1], cst[:, 2:3], stats[:, s0 + 3:s0 + 4],
                ALU.mult, ALU.subtract), waits=[t4])
            t6 = do("scalar", lambda e: e.activation(stats[:, s0 + 5:s0 + 6], stats[:, s0 + 4:s0 + 5],
                                                     AF.Sqrt, bias=cst[:, 0:1]), waits=[t5])
            t7 = do("vector", lambda e: e.reciprocal(stats[:, s0 + 6:s0 + 7], stats[:, s0 + 5:s0 + 6]),
                    waits=[t6])
            return t7

        def emit_ln_stats(x_ap, scratch, w_first=()):
            t1 = do("vector", lambda e: e.scalar_tensor_tensor(
                scratch, x_ap, 1.0, x_ap, ALU.mult, ALU.mult,
                accum_out=stats[:, 0:1]), waits=list(w_first))
            t2 = do("vector", lambda e: e.tensor_reduce(stats[:, 1:2], x_ap, AX.X, ALU.add))
            t3 = do("scalar", lambda e: e.activation(stats[:, 2:3], stats[:, 1:2], AF.Copy,
                                                     scale=cst[:, 2:3]), waits=[t2])
            t4 = do("scalar", lambda e: e.activation(stats[:, 3:4], stats[:, 2:3], AF.Square))
            t5 = do("vector", lambda e: e.tensor_scalar(
                stats[:, 4:5], stats[:, 0:1], cst[:, 2:3], stats[:, 3:4],
                ALU.mult, ALU.subtract), waits=[t4, t1])
            t6 = do("scalar", lambda e: e.activation(stats[:, 5:6], stats[:, 4:5], AF.Sqrt,
                                                     bias=cst[:, 0:1]), waits=[t5])
            t7 = do("vector", lambda e: e.reciprocal(stats[:, 6:7], stats[:, 5:6]), waits=[t6])
            return t7

        t_x0 = do("sync", lambda e: e.dma_start(out=x_res[:, 0:1, :], in_=x_t[:, 0:1, :]),
                  sem="dx0", inc=16)
        t_x0b = do("sync", lambda e: e.dma_start(out=x_res[:, 1:8, :], in_=x_t[:, 1:8, :]),
                   sem="dx2", inc=16)
        t_x1 = do("sync", lambda e: e.dma_start(out=x_res[:, 8:16, :], in_=x_t[:, 8:16, :]),
                  sem="dx1", inc=16)

        do("gpsimd", lambda e: e.dma_start(out=kn_rep[:, :], in_=d_kn.unsqueeze(0).broadcast_to([128, 1024])),
           sem="dM", inc=16)
        do("gpsimd", lambda e: e.dma_start(out=bo_bf[0:1, :], in_=d_bo.unsqueeze(0)),
           sem="dM", inc=16)
        do("gpsimd", lambda e: e.dma_start(out=vw_eib, in_=d_ei.rearrange("n k -> (n k)").unsqueeze(0).broadcast_to([8, 2048])),
           sem="dM", inc=16)
        do("gpsimd", lambda e: e.dma_start(out=qn_fm[:, :], in_=d_qn.rearrange("(c p) -> p c", p=128), allow_slow_non_contiguous=True),
           sem="dM", inc=16)
        do("gpsimd", lambda e: e.dma_start(out=vw_ewb, in_=d_ew.rearrange("n k -> (n k)").unsqueeze(0).broadcast_to([8, 2048])),
           sem="dM", inc=16)
        t_misc = P.tok("dM")
        t_w1a_l = do("gpsimd", lambda e: e.dma_start(out=vw_w1a_stage, in_=d_w1a.rearrange("p (c f) -> p c f", c=4)),
                     sem="dA0", inc=16)
        wq_src = d_wqkv.rearrange("p (k f) -> p k f", k=DC)
        t_wqkv1 = do("gpsimd", lambda e: e.dma_start(out=vw_qkv[:, :, 1024:2048], in_=wq_src[:, :, 1024:2048]),
                     sem="dW", inc=16)
        do("gpsimd", lambda e: e.dma_start(out=vw_qkv[:, :, 0:1024], in_=wq_src[:, :, 0:1024]),
           sem="dW2", inc=16)
        t_wqkv2 = do("gpsimd", lambda e: e.dma_start(out=vw_qkv[:, :, 2048:3072], in_=wq_src[:, :, 2048:3072]),
                     sem="dW2", inc=16)
        do("gpsimd", lambda e: e.dma_start(out=wout_sb[:, :, :], in_=d_wout.rearrange("p (k f) -> p k f", k=DC)),
           sem="dF0", inc=16)
        t_f0_all = P.tok("dF0")
        t_qn = t_misc
        t_kn = t_misc

        do("gpsimd", lambda e: e.memset(ident[:, :], 0.0))
        t_ident = do("gpsimd", lambda e: e.affine_select(
            out=ident[:, :], in_=ident[:, :], compare_op=ALU.not_equal, fill=1.0,
            base=0, pattern=[[-1, 128]], channel_multiplier=1))
        do("gpsimd", lambda e: e.memset(ones_r[0:1, :], 1.0))
        t_ones = do("gpsimd", lambda e: e.memset(ones_c[:, :], 1.0))
        t_pc = do("gpsimd", lambda e: e.iota(pc8[:, :], [[0, 1]], channel_multiplier=1,
                                             allow_small_or_imprecise_dtypes=True))
        t_ksbd0 = do("gpsimd", lambda e: e.memset(ksum_bd.ap().rearrange("p a b c -> p (a b c)"), 0.0))
        t_kvbd0 = do("gpsimd", lambda e: e.memset(arena4.ap()[:, 0:2048], 0.0))
        do("gpsimd", lambda e: e.memset(cst[:, 0:1], EPS_LN))
        do("gpsimd", lambda e: e.memset(cst[:, 1:2], EPS_RMS))
        t_cst = do("gpsimd", lambda e: e.memset(cst[:, 2:3], 1.0 / 1024.0))
        t_cc0 = do("vector", lambda e: e.memset(vw_cc, 0.0))

        t_eif = do("vector", lambda e: e.tensor_copy(vw_eif, vw_eib), waits=[t_misc])
        t_m = do("vector", lambda e: e.tensor_single_scalar(vw_eif, vw_eif, pc8[:, 0:1], ALU.is_equal),
                 waits=[t_eif, t_pc])
        t_mw = do("vector", lambda e: e.tensor_tensor(vw_wexp, vw_eif, vw_ewb, ALU.mult),
                  waits=[t_m])
        t_wexp = do("vector", lambda e: e.tensor_reduce(
            vw_wexp[:, 0:1024], vw_wexp.rearrange("p (n k) -> p n k", k=2), AX.X, ALU.add),
            waits=[t_mw])
        t_gate_st = do("gpsimd", lambda e: e.dma_start(out=d_gate.ap(), in_=vw_wexp[:, 0:1024]),
                       sem="dF1", inc=16, waits=[t_wexp])
        if debug:
            do("gpsimd", lambda e: e.dma_start(out=dbg["wexp"], in_=vw_wexp[:, 0:1024]),
               sem="dO", inc=16, waits=[t_wexp])

        blocks = [(rc, fc) for rc in range(4) for fc in range(8)]
        t_w1aT = None
        for grp in range(4):
            chunk = blocks[grp * 8:(grp + 1) * 8]
            for idx, (rc, fc) in enumerate(chunk):
                w = [t_w1a_l, t_ident]
                if t_w1aT is not None:
                    w.append(t_w1aT)
                do("tensor", lambda e, rc=rc, fc=fc, idx=idx: e.transpose(
                    b0_bf[:, idx * 128:(idx + 1) * 128],
                    vw_w1a_stage[:, rc, fc * 128:(fc + 1) * 128], ident[:, :]),
                    waits=w if idx == 0 else (), mode="T")
            t_tp = P.tok("tensor")
            for idx, (rc, fc) in enumerate(chunk):
                do("vector", lambda e, rc=rc, fc=fc, idx=idx: e.tensor_copy(
                    w1aT[:, fc, rc * 128:(rc + 1) * 128], b0_bf[:, idx * 128:(idx + 1) * 128]),
                    waits=[t_tp] if idx == 0 else ())
            t_w1aT = P.tok("vector")

        GRP_COL = [1024, 1536, 0, 512, 2048, 2560]
        evac_qkv = {}
        tT_evac = {}
        last_qkv = {}
        kv_consumed = None
        kv_add = {}
        sq_done = None
        q_scaled = {}
        ln1_used = {}
        apply_tok = {}

        def emit_ln_statsA(i):
            t_xl = t_x0 if i == 0 else (t_x0b if i < 8 else t_x1)
            t1 = do("vector", lambda e, i=i: e.scalar_tensor_tensor(
                vw_attn_tm, x_res[:, i, :], 1.0, x_res[:, i, :], ALU.mult, ALU.mult,
                accum_out=stats[:, 0:1]), waits=[t_xl, t_cst])
            t2 = do("vector", lambda e, i=i: e.tensor_reduce(
                stats[:, 1:2], x_res[:, i, :], AX.X, ALU.add))
            t3 = do("scalar", lambda e: e.activation(stats[:, 2:3], stats[:, 1:2], AF.Copy,
                                                     scale=cst[:, 2:3]), waits=[t2])
            t4 = do("scalar", lambda e: e.activation(stats[:, 3:4], stats[:, 2:3], AF.Square))
            return t1, t4

        def emit_ln_statsB(t1, t4):
            t5 = do("vector", lambda e: e.tensor_scalar(
                stats[:, 4:5], stats[:, 0:1], cst[:, 2:3], stats[:, 3:4],
                ALU.mult, ALU.subtract), waits=[t4, t1])
            t6 = do("scalar", lambda e: e.activation(stats[:, 5:6], stats[:, 4:5], AF.Sqrt,
                                                     bias=cst[:, 0:1]), waits=[t5])
            return t6

        def emit_ln_apply(i, t6):
            t7 = do("vector", lambda e: e.reciprocal(stats[:, 6:7], stats[:, 5:6]), waits=[t6])
            w_ap = [t7]
            if i >= 1:
                w_ap.append(ln1_used[i - 1])
            return do("vector", lambda e, i=i: e.tensor_scalar(
                ln1_tm[:, :], x_res[:, i, :], stats[:, 2:3], stats[:, 6:7],
                ALU.subtract, ALU.mult), waits=w_ap)

        _t1, _t4 = emit_ln_statsA(0)
        _t6 = emit_ln_statsB(_t1, _t4)
        apply_tok[0] = emit_ln_apply(0, _t6)

        for i in range(NT):
            wT = [apply_tok[i], t_ident]
            if i == 0:
                wT.append(t_w1aT)
            else:
                wT.append(tT_evac[i - 1])
            for j in range(DC):
                do("tensor", lambda e, j=j: e.transpose(
                    b0_bf[:, j * 128:(j + 1) * 128], ln1_tm[:, j * 128:(j + 1) * 128], ident[:, :]),
                    waits=wT if j == 0 else (), mode="T")
            t_tp = P.tok("tensor")
            ln1_used[i] = t_tp
            wev = [t_tp]
            if i >= 2:
                wev.append(last_qkv[i - 2])
            t_ev = do("vector", lambda e, i=i: e.tensor_copy(
                ln1_fm[:, i % 2, :, :].rearrange("p c f -> p (c f)"), b0_bf[:, :]), waits=wev)
            tT_evac[i] = t_ev
            if debug and i == NT - 1:
                do("gpsimd", lambda e: e.dma_start(out=dbg["ln1_0"], in_=ln1_tm[:, :]),
                   sem="dO", inc=16, waits=[apply_tok[i]])

            ln_state = None
            for g in range(6):
                bk = 1 + (g % 2)
                col = GRP_COL[g]
                wq = [t_ev, t_wqkv1 if g < 2 else t_wqkv2]
                prev = (i, g - 2) if g >= 2 else ((i - 1, g + 4) if i >= 1 else None)
                if prev is not None:
                    wq.append(evac_qkv[prev])
                for k in range(DC):
                    do("tensor", lambda e, k=k, col=col, bk=bk, i=i: e.matmul(
                        bank(bk), ln1_fm[:, i % 2, k, :], vw_qkv[:, k, col:col + 512],
                        start=(k == 0), stop=(k == DC - 1)),
                        waits=wq if k == 0 else (), mode="P")
                t_mm = P.tok("tensor")
                if g == 5:
                    last_qkv[i] = t_mm
                if g < 2:
                    dest = k_tile[:, (g % 2) * 512:(g % 2) * 512 + 512]
                    wv = [t_mm]
                    if kv_consumed is not None and g == 0:
                        wv.append(kv_consumed)
                elif g < 4:
                    dest = vw_qstr[:, i, (g % 2) * 512:(g % 2) * 512 + 512]
                    wv = [t_mm]
                    if i in (8, 9) and g == 2:
                        wv.append(t_gate_st)
                    if i >= 1 and g == 2:
                        wv.append(q_scaled[i - 1])
                else:
                    dest = v_tile[:, (g % 2) * 512:(g % 2) * 512 + 512]
                    wv = [t_mm]
                    if kv_consumed is not None and g == 4:
                        wv.append(kv_consumed)
                t_e = do("vector", lambda e, dest=dest, bk=bk: e.tensor_copy(dest, bank(bk)), waits=wv)
                evac_qkv[(i, g)] = t_e

                if g == 1:
                    t_sqk = do("scalar", lambda e: e.activation(
                        vw_attn_fm.rearrange("p c f -> p (c f)"), k_tile[:, :], AF.Square,
                        accum_out=recips[:, 19:20]),
                        waits=[evac_qkv[(i, 0)], evac_qkv[(i, 1)]])
                    sq_done = t_sqk
                    t_rk0 = do("scalar", lambda e: e.activation(
                        recips[:, 20:21], recips[:, 19:20], AF.Sqrt,
                        scale=cst[:, 2:3], bias=cst[:, 1:2]))
                    t_kkn = do("gpsimd", lambda e: e.tensor_tensor(
                        k_tile[:, :], k_tile[:, :], kn_rep[:, :], ALU.mult),
                        waits=[t_sqk, t_kn])
                    t_rk = do("vector", lambda e: e.reciprocal(recips[:, 21:22], recips[:, 20:21]),
                              waits=[t_rk0])
                    t_krelu = do("scalar", lambda e: e.activation(
                        k_tile[:, :], k_tile[:, :], AF.Relu, scale=recips[:, 21:22]),
                        waits=[t_rk, t_kkn])
                if g == 2 and i < NT - 1:
                    ln_state = emit_ln_statsA(i + 1)
                if g == 3:
                    do("scalar", lambda e, i=i: e.activation(
                        vw_attn_fm.rearrange("p c f -> p (c f)"), vw_qstr[:, i, :], AF.Square,
                        accum_out=recips[:, 16:17]),
                        waits=[evac_qkv[(i, 2)], evac_qkv[(i, 3)]])
                    t_rq0 = do("scalar", lambda e: e.activation(
                        recips[:, 17:18], recips[:, 16:17], AF.Sqrt,
                        scale=cst[:, 2:3], bias=cst[:, 1:2]))
                    t_rq = do("vector", lambda e: e.reciprocal(recips[:, 18:19], recips[:, 17:18]),
                              waits=[t_rq0])
                    q_scaled[i] = do("scalar", lambda e, i=i: e.activation(
                        vw_qstr[:, i, :], vw_qstr[:, i, :], AF.Copy, scale=recips[:, 18:19]),
                        waits=[t_rq])
                if g == 4 and ln_state is not None:
                    ln_state = ln_state + (emit_ln_statsB(*ln_state),)

            if debug and i == NT - 1:
                do("gpsimd", lambda e: e.dma_start(out=dbg["k_0"], in_=k_tile[:, :]),
                   sem="dO", inc=16, waits=[t_krelu])
                do("gpsimd", lambda e: e.dma_start(out=dbg["v_0"], in_=v_tile[:, :]),
                   sem="dO", inc=16, waits=[evac_qkv[(i, 5)]])

            b = i // 8
            kvp = 3 if i % 2 == 0 else 5
            ksc = 16 * (i % 2)
            wkv = [t_krelu, evac_qkv[(i, 4)], evac_qkv[(i, 5)], t_ones]
            if i >= 1:
                wkv.append(kv_add[i - 1])
            for h in range(H):
                do("tensor", lambda e, h=h, kvp=kvp: e.matmul(
                    ps[0:64, kvp + h // 8, (h % 8) * 64:(h % 8) * 64 + 64],
                    k_tile[:, h * 64:(h + 1) * 64], v_tile[:, h * 64:(h + 1) * 64],
                    start=True, stop=True),
                    waits=wkv if h == 0 else (), mode="P")
            for h in range(H):
                do("tensor", lambda e, h=h, ksc=ksc: e.matmul(
                    ps[0:64, 7, ksc + h:ksc + h + 1],
                    k_tile[:, h * 64:(h + 1) * 64], ones_c[:, 0:1],
                    start=True, stop=True), mode="P")
            kv_consumed = P.tok("tensor")
            if ln_state is not None:
                apply_tok[i + 1] = emit_ln_apply(i + 1, ln_state[2])
            do("vector", lambda e, b=b, kvp=kvp: e.tensor_tensor(
                vw_cc[:, b * 1040:b * 1040 + 1024],
                ps[0:64, kvp:kvp + 2, :].rearrange("p b2 f -> p (b2 f)"),
                vw_cc[:, b * 1040:b * 1040 + 1024], ALU.add),
                waits=[kv_consumed, t_cc0])
            t_a2 = do("vector", lambda e, b=b, ksc=ksc: e.tensor_tensor(
                vw_cc[:, b * 1040 + 1024: b * 1040 + 1040],
                ps[0:64, 7, ksc:ksc + 16],
                vw_cc[:, b * 1040 + 1024: b * 1040 + 1040], ALU.add))
            kv_add[i] = t_a2

            if i == 7:
                t_ccstA = do("sync", lambda e: e.dma_start(out=d_ccinA.ap(), in_=vw_cc[:, 0:1040]),
                             sem="dCC", inc=16, waits=[kv_add[7]])
            if i == 8:
                t_ccA = do("gpsimd", lambda e: e.collective_compute(
                    "AllReduce", ALU.add,
                    replica_groups=[[0, 1], [2, 3], [4, 5], [6, 7]],
                    ins=[d_ccinA.ap().opt()], outs=[d_ccoutA.ap().opt()]),
                    sem="cc", inc=1, waits=[t_ccstA])
            if i == 9:
                cc_ksA = d_ccoutA.ap()[0:64, 1024:1040].rearrange("p (j two) -> p j two", two=2)
                do("sync", lambda e: e.dma_start(out=sk_sum[0:64, 0:8], in_=cc_ksA[:, :, 0],
                                                 allow_slow_non_contiguous=True),
                   sem="dCA", inc=16, waits=[t_ccA])
                do("sync", lambda e: e.dma_start(out=sk_sum[64:128, 0:8], in_=cc_ksA[:, :, 1],
                                                 allow_slow_non_contiguous=True),
                   sem="dCA", inc=16)

        qkv_done = last_qkv[NT - 1]

        if debug:
            do("gpsimd", lambda e: e.dma_start(out=dbg["cc_sb"], in_=vw_cc),
               sem="dO", inc=16, waits=[kv_add[NT - 1]])
            do("gpsimd", lambda e: e.dma_start(out=dbg["q_str"], in_=vw_qstr.rearrange("p t f -> p (t f)")),
               sem="dO", inc=16, waits=[q_scaled[NT - 1]])

        t_ccstB = do("sync", lambda e: e.dma_start(out=d_ccinB.ap(), in_=vw_cc[:, 1040:2080]),
                     sem="dSB", inc=16, waits=[kv_add[NT - 1]])
        t_ccB = do("gpsimd", lambda e: e.collective_compute(
            "AllReduce", ALU.add,
            replica_groups=[[0, 1], [2, 3], [4, 5], [6, 7]],
            ins=[d_ccinB.ap().opt()], outs=[d_ccoutB.ap().opt()]),
            sem="ccB", inc=1, waits=[t_ccstB])
        kvbd_even = arena4.ap()[0:64, 0:2048].rearrange("p (b2 j f) -> p b2 j f", b2=2, j=8)
        kvbd_odd = arena4.ap()[64:128, 0:2048].rearrange("p (b2 j f) -> p b2 j f", b2=2, j=8)
        cc_kvA = d_ccoutA.ap()[0:64, 0:1024].rearrange("p (j two f) -> p j two f", j=8, two=2)
        cc_kvB = d_ccoutB.ap()[0:64, 0:1024].rearrange("p (j two f) -> p j two f", j=8, two=2)
        do("gpsimd", lambda e: e.dma_start(out=kvbd_even[:, 0, :, 0:64], in_=cc_kvA[:, :, 0, :]),
           sem="dCA", inc=16, waits=[t_ccA, t_kvbd0])
        do("gpsimd", lambda e: e.dma_start(out=kvbd_odd[:, 0, :, 64:128], in_=cc_kvA[:, :, 1, :]),
           sem="dCA", inc=16)
        t_rbA = P.tok("dCA")
        cc_ksB = d_ccoutB.ap()[0:64, 1024:1040].rearrange("p (j two) -> p j two", two=2)
        do("sync", lambda e: e.dma_start(out=sk_sum[0:64, 8:16], in_=cc_ksB[:, :, 0],
                                         allow_slow_non_contiguous=True),
           sem="dCB", inc=16, waits=[t_ccB])
        do("sync", lambda e: e.dma_start(out=sk_sum[64:128, 8:16], in_=cc_ksB[:, :, 1],
                                         allow_slow_non_contiguous=True),
           sem="dCB", inc=16)
        do("gpsimd", lambda e: e.dma_start(out=kvbd_even[:, 1, :, 0:64], in_=cc_kvB[:, :, 0, :]),
           sem="dCB", inc=16, waits=[t_ccB])
        do("gpsimd", lambda e: e.dma_start(out=kvbd_odd[:, 1, :, 64:128], in_=cc_kvB[:, :, 1, :]),
           sem="dCB", inc=16)
        t_rbB = P.tok("dCB")
        for j in range(8):
            do("vector", lambda e, j=j: e.tensor_copy(
                ksum_bd[0:64, 0, j, 2 * j: 2 * j + 1], sk_sum[0:64, j: j + 1]),
                waits=[t_rbA, t_ksbd0] if j == 0 else ())
            do("vector", lambda e, j=j: e.tensor_copy(
                ksum_bd[64:128, 0, j, 2 * j + 1: 2 * j + 2], sk_sum[64:128, j: j + 1]))
        t_ksbdA = P.tok("vector")
        t_ksbdB = None

        b0_free = tT_evac[NT - 1]
        qfm_used = None
        attn_evac = None
        t_evac_t = None
        wout_evac = {}
        norm_read = None
        t_store = {}
        attn_div_prev = None
        last_wout = None
        attn_all_done = None
        pend = None

        P4O = [0, 1, 2, 3, 8, 9, 10, 11, 12, 13, 14, 15, 4, 5, 6, 7]
        l2T_done = None

        def flush_pend():
            nonlocal pend, b0_free, t_evac_t, l2T_done
            pi, t_ln2p = pend
            for j in range(DC):
                do("tensor", lambda e, j=j: e.transpose(
                    b0_bf[:, j * 128:(j + 1) * 128], ln1_tm[:, j * 128:(j + 1) * 128], ident[:, :]),
                    waits=[t_ln2p, b0_free] if j == 0 else (), mode="T")
            t_l2T = P.tok("tensor")
            l2T_done = t_l2T
            t_l2fm = do("vector", lambda e: e.tensor_copy(
                ln1_fm[:, 0, :, :].rearrange("p c f -> p (c f)"), b0_bf[:, :]), waits=[t_l2T])
            b0_free = t_l2fm
            wt_ = [t_l2fm, attn_evac, t_w1aT]
            for m in range(4):
                for k in range(DC):
                    do("tensor", lambda e, m=m, k=k: e.matmul(
                        bankpair(1)[:, m * 128:(m + 1) * 128], w1aT[:, k, m * 128:(m + 1) * 128],
                        ln1_fm[:, 0, k, :], start=(k == 0), stop=(k == DC - 1)),
                        waits=wt_ if (m == 0 and k == 0) else (), mode="P")
            t_tmm = P.tok("tensor")
            wev = [t_tmm]
            if pi == 0:
                wev.append(qkv_done)
            t_evac_t = do("vector", lambda e, pi=pi: e.tensor_copy(
                vw_t[:, :, pi * 128:(pi + 1) * 128],
                bankpair(1)[:, 0:512].rearrange("p (m n) -> p m n", n=128)), waits=wev)
            pend = None

        for pos in range(NT):
            i = P4O[pos]
            b = i // 8
            if pos == 3:
                for j in range(8):
                    do("vector", lambda e, j=j: e.tensor_copy(
                        ksum_bd[0:64, 1, j, 2 * j: 2 * j + 1], sk_sum[0:64, 8 + j: 9 + j]),
                        waits=[t_rbB] if j == 0 else ())
                    do("vector", lambda e, j=j: e.tensor_copy(
                        ksum_bd[64:128, 1, j, 2 * j + 1: 2 * j + 2], sk_sum[64:128, 8 + j: 9 + j]))
                t_ksbdB = P.tok("vector")
                if debug:
                    do("gpsimd", lambda e: e.dma_start(out=dbg["kv_bd"], in_=vw_kvbd.rearrange("p b f -> p (b f)")),
                       sem="dO", inc=16, waits=[t_rbA, t_rbB])
                    do("gpsimd", lambda e: e.dma_start(out=dbg["sk"], in_=sk_sum[:, :]),
                       sem="dO", inc=16, waits=[t_rbB])
                    do("gpsimd", lambda e: e.dma_start(out=dbg["ksbd"], in_=ksum_bd.ap().rearrange("p a b c -> p (a b c)")),
                       sem="dO", inc=16, waits=[t_ksbdB])
            t_kvbd_b = t_rbA if b == 0 else t_rbB
            t_ksbd_b = t_ksbdA if b == 0 else t_ksbdB
            wT = [q_scaled[i], b0_free, t_kvbd_b, t_ksbd_b]
            if pos == 0:
                wT.append(qkv_done)
            if qfm_used is not None:
                wT.append(qfm_used)
            for j in range(DC):
                do("tensor", lambda e, i=i, j=j: e.transpose(
                    b0_bf[:, j * 128:(j + 1) * 128], vw_qstr[:, i, j * 128:(j + 1) * 128], ident[:, :]),
                    waits=wT if j == 0 else (), mode="T")
            t_qT = P.tok("tensor")
            for j in range(DC):
                do("vector", lambda e, j=j: e.tensor_scalar(
                    q_fm[:, j, :], b0_bf[:, j * 128:(j + 1) * 128],
                    qn_fm[:, j:j + 1], 0.0, ALU.mult, ALU.max),
                    waits=[t_qT, t_qn] if j == 0 else ())
            t_qfm = P.tok("vector")
            b0_free = t_qfm
            if debug and pos == NT - 1:
                do("gpsimd", lambda e: e.dma_start(out=dbg["qfm"], in_=q_fm.rearrange("p c f -> p (c f)")),
                   sem="dO", inc=16, waits=[t_qfm])

            wN = [t_qfm, t_ksbd_b]
            if norm_read is not None:
                wN.append(norm_read)
            if pos == 0:
                wN.append(kv_add[NT - 1])
            for j in range(DC):
                do("tensor", lambda e, j=j, b=b: e.matmul(
                    bank(7)[:, 0:16], q_fm[:, j, :], ksum_bd[:, b, j, :],
                    start=(j == 0), stop=(j == DC - 1)),
                    waits=wN if j == 0 else (), mode="P")
            t_norm = P.tok("tensor")
            wA = [t_kvbd_b]
            if t_evac_t is not None:
                wA.append(t_evac_t)
            if attn_evac is not None:
                wA.append(attn_evac)
            if pos == 0:
                wA += [evac_qkv[(NT - 1, 4)], evac_qkv[(NT - 1, 5)]]
            for j in range(DC):
                do("tensor", lambda e, j=j, b=b: e.matmul(
                    bankpair(1)[:, j * 128:(j + 1) * 128], q_fm[:, j, :], vw_kvbd[:, b, j * 128:(j + 1) * 128],
                    start=True, stop=True),
                    waits=wA if j == 0 else (), mode="P")
            t_attnmm = P.tok("tensor")
            qfm_used = t_attnmm
            if pos == NT - 1:
                attn_all_done = t_attnmm

            t_eps = do("scalar", lambda e: e.activation(
                recips[:, 16:32], bank(7)[:, 0:16], AF.Identity, bias=cst[:, 1:2]),
                waits=[t_norm])
            norm_read = t_eps
            t_rec = do("vector", lambda e: e.reciprocal(recips[:, 0:16], recips[:, 16:32]),
                       waits=[t_eps])
            if debug and pos == NT - 1:
                do("gpsimd", lambda e: e.dma_start(out=dbg["recip0"], in_=recips[:, 0:16]),
                   sem="dO", inc=16, waits=[t_rec])

            wD = [t_attnmm, t_rec]
            if pos == 0:
                wD.append(sq_done)
            if attn_div_prev is not None:
                wD.append(attn_div_prev)
            t_div = do("vector", lambda e: e.tensor_tensor(
                vw_attn_tm.rearrange("p (h d) -> p h d", d=64),
                bankpair(1).rearrange("p (h d) -> p h d", d=64),
                recips[:, 0:16].unsqueeze(-1).broadcast_to([128, 16, 64]),
                ALU.mult), waits=wD)
            attn_evac = t_div
            if debug and pos == NT - 1:
                do("gpsimd", lambda e: e.dma_start(out=dbg["attn0"], in_=vw_attn_tm),
                   sem="dO", inc=16, waits=[t_div])

            for j in range(DC):
                do("tensor", lambda e, j=j: e.transpose(
                    b0_bf[:, j * 128:(j + 1) * 128], vw_attn_tm[:, j * 128:(j + 1) * 128], ident[:, :]),
                    waits=[t_div, b0_free] if j == 0 else (), mode="T")
            t_aT = P.tok("tensor")
            t_afm = do("vector", lambda e: e.tensor_copy(
                vw_attn_fm.rearrange("p c f -> p (c f)"), b0_bf[:, :]),
                waits=[t_aT] + ([sq_done] if i == 0 else []))
            attn_div_prev = t_afm
            b0_free = t_afm

            wb = 3 if pos % 2 == 0 else 5
            wW = [t_afm, t_f0_all, t_misc, t_ones]
            if pos < 2:
                wW.append(kv_add[NT - 1])
            elif (pos - 2) in wout_evac:
                wW.append(wout_evac[pos - 2])
            for half in range(2):
                for j in range(DC):
                    do("tensor", lambda e, j=j, half=half, wb=wb: e.matmul(
                        bank(wb + half), vw_attn_fm[:, j, :], wout_sb[:, j, half * 512:(half + 1) * 512],
                        start=(j == 0), stop=False),
                        waits=wW if (j == 0 and half == 0) else (), mode="P")
                do("tensor", lambda e, half=half, wb=wb: e.matmul(
                    bank(wb + half), ones_r[0:1, :], bo_bf[0:1, half * 512:(half + 1) * 512],
                    start=False, stop=True), mode="P")
            t_wout = P.tok("tensor")
            last_wout = t_wout

            if pend is not None:
                flush_pend()

            wY = [t_wout]
            if pos == 0:
                wY += [t_ccstA, t_ccstB]
            t_y1 = do("vector", lambda e, wb=wb: e.tensor_scalar(
                vw_y1, bankpair(wb), 1.0, 0.0, ALU.mult, ALU.add,
                accum_out=stats[:, 1:2]), waits=wY)
            wout_evac[pos] = t_y1
            if debug and i == NT - 1:
                do("gpsimd", lambda e: e.dma_start(out=dbg["y1_0"], in_=vw_y1),
                   sem="dO", inc=16, waits=[t_y1])
            t_rs = emit_ln_stats2(vw_y1, [t_y1])
            t_lnA = do("vector", lambda e: e.tensor_scalar(
                vw_tmp, vw_y1, stats[:, 2:3], stats[:, 6:7], ALU.subtract, ALU.mult),
                waits=[t_rs])
            t_res = do("vector", lambda e, i=i: e.scalar_tensor_tensor(
                x_res[:, i, :], x_res[:, i, :], 1.0, vw_tmp, ALU.mult, ALU.add,
                accum_out=stats[:, 9:10]), waits=[t_lnA])

            if i >= 8:
                t_store[i] = do("sync", lambda e, i=i: e.dma_start(out=out_t[i], in_=x_res[:, i, :]),
                                sem="dO", inc=16, waits=[t_res])
            else:
                t_rs2 = emit_ln_stats2(x_res[:, i, :], [t_res], s0=8)
                w_l2 = [t_rs2]
                if pos == 0:
                    w_l2.append(ln1_used[NT - 1])
                else:
                    w_l2.append(l2T_done)
                t_ln2 = do("vector", lambda e, i=i: e.tensor_scalar(
                    ln1_tm[:, :], x_res[:, i, :], stats[:, 10:11], stats[:, 14:15], ALU.subtract, ALU.mult),
                    waits=w_l2)
                pend = (i, t_ln2)

        if pend is not None:
            flush_pend()

        t_t_all = t_evac_t
        if debug:
            do("gpsimd", lambda e: e.dma_start(
                out=dbg["xr_mid"], in_=x_res.ap().rearrange("p t f -> p (t f)")),
                sem="dO", inc=16, waits=[P.tok("vector")])

        do("gpsimd", lambda e: e.dma_start(
            out=vw_w1b, in_=d_w1b.rearrange("p (c f) -> p c f", c=4)),
            sem="dF1", inc=16, waits=[qkv_done])
        do("gpsimd", lambda e: e.dma_start(
            out=vw_w2b, in_=d_w2b.rearrange("p (c f) -> p c f", c=4)),
            sem="dF1", inc=16)
        for cpair in range(4):
            for e2 in range(2):
                do("gpsimd", lambda e, cpair=cpair, e2=e2: e.dma_start(
                    out=vw_wrep[e2 * 64:(e2 + 1) * 64, cpair, :],
                    in_=d_gate.ap()[2 * cpair + e2: 2 * cpair + e2 + 1, :].broadcast_to([64, 1024])),
                    sem="dF1", inc=16,
                    waits=([P.tok("dF1"), last_wout, attn_all_done]
                           + ([P.tok("dO")] if debug else [])) if (cpair == 0 and e2 == 0) else ())
        t_f1_all = P.tok("dF1")
        if debug:
            do("gpsimd", lambda e: e.dma_start(out=dbg["wrep"], in_=vw_wrep.rearrange("p c f -> p (c f)")),
               sem="dO", inc=16, waits=[t_f1_all])
            do("gpsimd", lambda e: e.dma_start(out=dbg["t_sb"], in_=vw_t.rearrange("p c f -> p (c f)")),
               sem="dO", inc=16, waits=[t_t_all])

        stores_done = P.tok("dO")
        t_sA = {}
        sA_used = {}
        b0f = b0_free
        for c8 in range(8):
            slot = c8 % 2
            wL = [qfm_used]
            if c8 >= 2:
                wL.append(sA_used[c8 - 2])
            t_sA[c8] = do("gpsimd", lambda e, c8=c8, slot=slot: e.dma_start(
                out=vw_sA[:, slot, :, :].rearrange("p c f -> p (c f)"),
                in_=d_w2a.rearrange("p (c f) -> p c f", c=4)[:, :, c8 * 512:(c8 + 1) * 512]),
                sem="dA0" if slot == 0 else "dA1", inc=16, waits=wL)
            blocks = [(rc, fc) for rc in range(4) for fc in range(4)]
            for halfg in range(2):
                chunk = blocks[halfg * 8:(halfg + 1) * 8]
                for idx, (rc, fc) in enumerate(chunk):
                    w = [t_sA[c8], b0f] if idx == 0 else ()
                    do("tensor", lambda e, slot=slot, rc=rc, fc=fc, idx=idx: e.transpose(
                        b0_bf[:, idx * 128:(idx + 1) * 128],
                        vw_sA[:, slot, rc, fc * 128:(fc + 1) * 128], ident[:, :]),
                        waits=w, mode="T")
                t_tp = P.tok("tensor")
                for idx, (rc, fc) in enumerate(chunk):
                    do("vector", lambda e, c8=c8, rc=rc, fc=fc, idx=idx: e.tensor_copy(
                        vw_w2aT[:, c8 * 4 + fc, rc * 128:(rc + 1) * 128],
                        b0_bf[:, idx * 128:(idx + 1) * 128]),
                        waits=[t_tp, stores_done] if idx == 0 else ())
                b0f = P.tok("vector")
            sA_used[c8] = b0f
        t_w2aT = b0f

        u_evac = {}
        gel = {}
        umm = {}
        u_free = None
        first_h = True
        seq = [(c, nb, Fc) for c in range(4) for nb in range(2) for Fc in range(32)]

        def emit_u(c, nb, Fc):
            w = [gel[(c, nb, Fc)]]
            if Fc == 0:
                w.append(u_free if u_free is not None else wout_evac[15])
            u_start = (Fc == 0)
            u_stop = (Fc == 31)
            do("tensor", lambda e, c=c, Fc=Fc, s=u_start, st=u_stop: e.matmul(
                ps[0:64, 5, :], vw_w2aT[:, Fc, (2 * c) * 64:(2 * c) * 64 + 64],
                h_sb[:, Fc % 4, 0:512],
                start=s, stop=st, tile_position=(0, 0)),
                waits=w, mode="C")
            do("tensor", lambda e, c=c, Fc=Fc, s=u_start, st=u_stop: e.matmul(
                ps[64:128, 5, :], vw_w2aT[:, Fc, (2 * c + 1) * 64:(2 * c + 1) * 64 + 64],
                h_sb[:, Fc % 4, 512:1024],
                start=s, stop=st, tile_position=(0, 64)),
                mode="C")
            umm[(c, nb, Fc)] = P.tok("tensor")

        def emit_u_evac(c, nb):
            nonlocal_u = [umm[(c, nb, 31)], t_f1_all, qfm_used]
            u_evac[(c, nb)] = do("vector", lambda e, c=c, nb=nb: e.tensor_tensor(
                vw_u[:, c, nb * 512:(nb + 1) * 512], bank(5),
                vw_wrep[:, c, nb * 512:(nb + 1) * 512], ALU.mult), waits=nonlocal_u)
            return u_evac[(c, nb)]

        for s, (c, nb, Fc) in enumerate(seq):
            hp = 3 if Fc % 2 == 0 else 6
            w = [t_f1_all, t_w2aT]
            if first_h:
                w += [t_t_all, wout_evac[14], wout_evac[15], norm_read]
                first_h = False
            if (c, nb, Fc - 2) in gel:
                w.append(gel[(c, nb, Fc - 2)])
            elif Fc <= 1 and (c, nb) != (0, 0):
                prev = (c, nb - 1) if nb == 1 else (c - 1, 1)
                w.append(gel[prev + (30 + Fc,)])
            do("tensor", lambda e, c=c, nb=nb, Fc=Fc, hp=hp: e.matmul(
                bank(hp), vw_w1b[0:64, c, Fc * 128:(Fc + 1) * 128],
                vw_t[0:64, c, nb * 512:(nb + 1) * 512],
                start=True, stop=True, tile_position=(0, 0)),
                waits=w, mode="R")
            do("tensor", lambda e, c=c, nb=nb, Fc=Fc, hp=hp: e.matmul(
                bank(hp + 1), vw_w1b[64:128, c, Fc * 128:(Fc + 1) * 128],
                vw_t[64:128, c, nb * 512:(nb + 1) * 512],
                start=True, stop=True, tile_position=(64, 0)),
                mode="R")
            t_h = P.tok("tensor")
            wg = [t_h]
            if (c, nb, Fc - 4) in umm:
                wg.append(umm[(c, nb, Fc - 4)])
            gel[(c, nb, Fc)] = do("scalar", lambda e, Fc=Fc, hp=hp: e.activation(
                h_sb[:, Fc % 4, :], bankpair(hp), AF.Gelu), waits=wg)
            if s > 0:
                pc_, pn_, pf_ = seq[s - 1]
                emit_u(pc_, pn_, pf_)
                if pf_ == 31:
                    u_free = emit_u_evac(pc_, pn_)
        emit_u(*seq[-1])
        u_free = emit_u_evac(3, 1)
        t_u_all = P.tok("vector")
        if debug:
            do("gpsimd", lambda e: e.dma_start(out=dbg["u_sb"], in_=vw_u.rearrange("p c f -> p (c f)")),
               sem="dO", inc=16, waits=[t_u_all])

        y_prev = None
        for nt in range(NA):
            wy = [t_u_all]
            if y_prev is not None:
                wy.append(y_prev)
            for fh in range(2):
                if not (nt == 0 and fh == 0):
                    P.drain()
                for e_ in range(E):
                    half = e_ % 2
                    bk = (1 + fh) if half == 0 else (3 + fh)
                    s = (e_ < 2)
                    st = (e_ >= E - 2)
                    do("tensor", lambda en, e_=e_, fh=fh, bk=bk, nt=nt, s=s, st=st: en.matmul(
                        bank(bk), vw_u[(e_ % 2) * 64:(e_ % 2) * 64 + 64, e_ // 2, nt * 128:(nt + 1) * 128],
                        vw_w2b[(e_ % 2) * 64:(e_ % 2) * 64 + 64, e_ // 2, fh * 512:(fh + 1) * 512],
                        start=s, stop=st, tile_position=((e_ % 2) * 64, 0)),
                        waits=wy if (e_ == 0 and fh == 0) else (), mode="R")
            t_ymm = P.tok("tensor")
            do("vector", lambda e: e.tensor_copy(vw_y1, bankpair(1)), waits=[t_ymm])
            t_c2 = do("vector", lambda e: e.scalar_tensor_tensor(
                vw_y1, bankpair(3), 1.0, vw_y1, ALU.mult, ALU.add,
                accum_out=stats[:, 1:2]))
            y_prev = t_c2
            t_rs = emit_ln_stats2(vw_y1, [t_c2])
            t_lnA = do("vector", lambda e: e.tensor_scalar(
                vw_tmp, vw_y1, stats[:, 2:3], stats[:, 6:7], ALU.subtract, ALU.mult),
                waits=[t_rs])
            t_res = do("vector", lambda e, nt=nt: e.scalar_tensor_tensor(
                x_res[:, nt, :], x_res[:, nt, :], 1.0, vw_tmp, ALU.mult, ALU.add), waits=[t_lnA])
            t_store[nt] = do("sync", lambda e, nt=nt: e.dma_start(out=out_t[nt], in_=x_res[:, nt, :]),
                             sem="dO", inc=16, waits=[t_res])

        P.wait_only("sync", [P.tok("dO")])

        block = ctx.enter_context(nc.Block())
        for eng_name in ENGINES:
            ops = P.stream[eng_name]
            if not ops:
                continue

            def body(eng, ops=ops, eng_name=eng_name):
                seen = {}
                last_mode = None
                for waits, fn, sem, inc, mode in ops:
                    if eng_name == "tensor" and mode is not None:
                        if last_mode is not None and mode != last_mode:
                            eng.drain()
                        last_mode = mode
                    for (ws, wv) in waits:
                        if ws == eng_name or wv <= 0:
                            continue
                        if seen.get(ws, 0) >= wv:
                            continue
                        eng.wait_ge(sems[ws], wv)
                        seen[ws] = wv
                    if fn is None:
                        continue
                    inst = fn(eng)
                    if inst is not None and sem is not None:
                        inst.then_inc(sems[sem], inc)

            getattr(block, eng_name)(body)

    return nc


_CACHE = {}


def _get_nc(debug=False):
    key = bool(debug)
    if key not in _CACHE:
        _CACHE[key] = build_nc(debug=key)
    return _CACHE[key]


def kernel(L0_attn_x, L0_attn_W_qkv, L0_attn_qn_w, L0_attn_kn_w, L0_attn_W_out,
           L0_attn_b_out, L0_ffn_expert_weights, L0_ffn_expert_indices,
           L0_ffn_w1_A, L0_ffn_w1_B, L0_ffn_w2_A, L0_ffn_w2_B, _debug=False):
    def pmajor(a):
        a = np.asarray(a, dtype=np.float32)
        r = a.shape[0] // 128
        return np.ascontiguousarray(
            a.reshape(r, 128, a.shape[1]).transpose(1, 0, 2).reshape(128, r * a.shape[1]))

    x = np.ascontiguousarray(np.asarray(L0_attn_x, dtype=np.float32))
    shared = dict(
        wqkv=pmajor(L0_attn_W_qkv),
        qn=np.ascontiguousarray(np.asarray(L0_attn_qn_w, dtype=np.float32)),
        kn=np.ascontiguousarray(np.asarray(L0_attn_kn_w, dtype=np.float32)),
        wout=pmajor(L0_attn_W_out),
        bo=np.ascontiguousarray(np.asarray(L0_attn_b_out, dtype=np.float32)),
        w1a=pmajor(L0_ffn_w1_A),
        w1b=pmajor(L0_ffn_w1_B),
        w2a=pmajor(L0_ffn_w2_A),
        w2b=pmajor(L0_ffn_w2_B),
    )
    ew = np.ascontiguousarray(np.asarray(L0_ffn_expert_weights, dtype=np.float32))
    ei = np.ascontiguousarray(np.asarray(L0_ffn_expert_indices, dtype=np.int32))

    in_maps = []
    for c in range(8):
        bA = c // 2
        bB = 4 + c // 2
        r0 = (c % 2) * 1024
        xc = np.concatenate([x[bA, r0:r0 + 1024], x[bB, r0:r0 + 1024]], axis=0)
        g0 = bA * 2048 + r0
        m = dict(shared)
        m["x"] = pmajor(xc)
        m["ew"] = np.ascontiguousarray(ew[g0:g0 + 1024])
        m["ei"] = np.ascontiguousarray(ei[g0:g0 + 1024])
        in_maps.append(m)

    nc = _get_nc(debug=_debug)
    res = run_bass_kernel_spmd(nc, in_maps, core_ids=list(range(8)))

    y = np.empty((8, 2048, 1024), dtype=np.float32)
    for c in range(8):
        o = np.asarray(res.results[c]["out"], dtype=np.float32)
        bA = c // 2
        bB = 4 + c // 2
        r0 = (c % 2) * 1024
        y[bA, r0:r0 + 1024] = o[0:1024]
        y[bB, r0:r0 + 1024] = o[1024:2048]
    if _debug:
        return y, [res.results[c] for c in range(8)]
    return y


# version 12
# speedup vs baseline: 1.0678x; 1.0014x over previous
import numpy as np
from contextlib import ExitStack

import concourse.bass as bass
import concourse.mybir as mybir
from concourse.bass_utils import run_bass_kernel_spmd

F32 = mybir.dt.float32
BF16 = mybir.dt.bfloat16
I32 = mybir.dt.int32
AF = mybir.ActivationFunctionType
ALU = mybir.AluOpType
AX = mybir.AxisListType

NT = 16
NA = 8
D = 1024
DC = 8
H = 16
FF = 4096
E = 8
EPS_LN = 1e-5
EPS_RMS = 1e-6
EPS_DEN = 1e-6

ENGINES = ("tensor", "vector", "scalar", "gpsimd", "sync")


class Prog:

    DMA_SEMS = ("dx0", "dx1", "dx2", "dW", "dW2", "dF0", "dF1", "dA0", "dA1", "dO", "dCC", "dM", "dCA", "dCB", "dSB")

    def __init__(self):
        self.stream = {e: [] for e in ENGINES}
        self.cnt = {}

    def do(self, eng, fn, waits=(), sem=None, inc=1, mode=None):
        for (ws, wv) in waits:
            if ws in self.DMA_SEMS:
                assert wv == self.cnt.get(ws, 0), \
                    f"prefix wait on dma sem {ws}: {wv} != {self.cnt.get(ws, 0)}"
        if sem is None:
            sem = eng
            inc = 1
        c = self.cnt.get(sem, 0) + inc
        self.cnt[sem] = c
        self.stream[eng].append((tuple(waits), fn, sem, inc, mode))
        return (sem, c)

    def drain(self):
        self.stream["tensor"].append(((), lambda e: e.drain(), None, 0, None))

    def wait_only(self, eng, waits):
        self.stream[eng].append((tuple(waits), None, None, 0, None))

    def tok(self, sem):
        return (sem, self.cnt.get(sem, 0))


def build_nc(debug=False):
    nc = bass.Bass("TRN2", target_bir_lowering=False)

    d_x = nc.dram_tensor("x", [128, 16384], F32, kind="ExternalInput").ap()
    d_wqkv = nc.dram_tensor("wqkv", [128, 24576], F32, kind="ExternalInput").ap()
    d_qn = nc.dram_tensor("qn", [1024], F32, kind="ExternalInput").ap()
    d_kn = nc.dram_tensor("kn", [1024], F32, kind="ExternalInput").ap()
    d_wout = nc.dram_tensor("wout", [128, 8192], F32, kind="ExternalInput").ap()
    d_bo = nc.dram_tensor("bo", [1024], F32, kind="ExternalInput").ap()
    d_ew = nc.dram_tensor("ew", [1024, 2], F32, kind="ExternalInput").ap()
    d_ei = nc.dram_tensor("ei", [1024, 2], I32, kind="ExternalInput").ap()
    d_w1a = nc.dram_tensor("w1a", [128, 4096], F32, kind="ExternalInput").ap()
    d_w1b = nc.dram_tensor("w1b", [128, 16384], F32, kind="ExternalInput").ap()
    d_w2a = nc.dram_tensor("w2a", [128, 16384], F32, kind="ExternalInput").ap()
    d_w2b = nc.dram_tensor("w2b", [128, 4096], F32, kind="ExternalInput").ap()
    d_out = nc.dram_tensor("out", [2048, 1024], F32, kind="ExternalOutput").ap()

    d_ccinA = nc.dram_tensor("cc_inA", [64, 1040], F32)
    d_ccoutA = nc.dram_tensor("cc_outA", [64, 1040], F32)
    d_ccinB = nc.dram_tensor("cc_inB", [64, 1040], F32)
    d_ccoutB = nc.dram_tensor("cc_outB", [64, 1040], F32)
    d_gate = nc.dram_tensor("gate_scr", [8, 1024], F32)

    dbg = {}
    if debug:
        for name, shape in [
            ("ln1_0", [128, 1024]), ("q_str", [128, 16384]), ("k_0", [128, 1024]),
            ("v_0", [128, 1024]), ("cc_sb", [64, 2080]), ("kv_bd", [128, 2048]),
            ("sk", [128, 16]), ("recip0", [128, 16]), ("attn0", [128, 1024]),
            ("y1_0", [128, 1024]), ("xr_mid", [128, 16384]), ("t_sb", [128, 4096]),
            ("u_sb", [128, 4096]), ("wrep", [128, 4096]), ("wexp", [8, 1024]),
            ("qfm", [128, 1024]), ("ksbd", [128, 256]),
        ]:
            dbg[name] = nc.dram_tensor("dbg_" + name, shape, F32, kind="ExternalOutput").ap()

    with ExitStack() as ctx:
        sb = lambda name, shape, dt: ctx.enter_context(nc.sbuf_tensor(name, shape, dt))

        x_res = sb("x_res", [128, NT, 1024], F32)
        arena1 = sb("arena1", [128, 12288], F32)
        arena2 = sb("arena2", [128, 8192], F32)
        wout_sb = sb("wout_sb", [128, DC, 1024], BF16)
        w1aT = sb("w1aT", [128, DC, 512], BF16)
        h_sb = sb("h_sb", [128, 4, 1024], BF16)
        arena4 = sb("arena4", [128, 4096], BF16)
        arena3 = sb("arena3", [128, 2080], F32)
        ksum_bd = sb("ksum_bd", [128, 2, DC, 16], BF16)
        sk_sum = sb("sk_sum", [128, 16], F32)
        ident = sb("ident", [128, 128], BF16)
        ln1_tm = sb("ln1_tm", [128, 1024], BF16)
        ln1_fm = sb("ln1_fm", [128, 2, DC, 128], BF16)
        k_tile = sb("k_tile", [128, 1024], BF16)
        v_tile = sb("v_tile", [128, 1024], BF16)
        kn_rep = sb("kn_rep", [128, 1024], BF16)
        qn_fm = sb("qn_fm", [128, 8], F32)
        bo_bf = sb("bo_bf", [1, 1024], BF16)
        ones_r = sb("ones_r", [1, 128], BF16)
        ones_c = sb("ones_c", [128, 1], BF16)
        recips = sb("recips", [128, 32], F32)
        stats = sb("stats", [128, 16], F32)
        pc8 = sb("pc8", [8, 1], F32)
        cst = sb("cst", [128, 4], F32)

        ps = ctx.enter_context(nc.psum_tensor("ps", [128, 8, 512], F32))

        a1_bf = arena1.ap().bitcast(BF16)
        vw_qkv = a1_bf.rearrange("p (k f) -> p k f", k=DC)
        vw_w1b = a1_bf[:, 0:16384].rearrange("p (c f) -> p c f", c=4)
        vw_w2b = a1_bf[:, 16384:20480].rearrange("p (c f) -> p c f", c=4)
        vw_t = a1_bf[:, 20480:24576].rearrange("p (c f) -> p c f", c=4)

        a2_bf = arena2.ap().bitcast(BF16)
        vw_qstr = a2_bf.rearrange("p (t f) -> p t f", t=NT)
        vw_u = a2_bf[:, 0:4096].rearrange("p (c f) -> p c f", c=4)
        vw_sA = a2_bf[:, 4096:8192].rearrange("p (s c f) -> p s c f", s=2, c=4)

        vw_w1a_stage = h_sb.ap()

        vw_kvbd = arena4.ap()[:, 0:2048].rearrange("p (b f) -> p b f", b=2)
        vw_attn_tm = arena4.ap()[:, 2048:3072]
        vw_attn_fm = arena4.ap()[:, 3072:4096].rearrange("p (c f) -> p c f", c=DC)
        vw_wrep = arena4.ap().rearrange("p (c f) -> p c f", c=4)

        vw_cc = arena3.ap()[0:64, 0:2080]
        vw_y1 = arena3.ap()[:, 0:1024]
        vw_tmp = arena3.ap()[:, 1024:2048]
        vw_ewb = arena2.ap()[0:8, 2048:4096]
        vw_eib = arena2.ap()[0:8, 4096:6144].bitcast(I32)
        vw_eif = arena2.ap()[0:8, 6144:8192]
        vw_wexp = arena2.ap()[0:8, 4096:6144]

        q_fm = ln1_fm.ap()[:, 1, :, :]

        vw_w2aT = (x_res.ap()[:, 8:16, :].rearrange("p t f -> p (t f)")
                   .bitcast(BF16).rearrange("p (c f) -> p c f", c=32))

        def bank(i):
            return ps[:, i, :]

        def bankpair(i):
            return ps[:, i:i + 2, :].rearrange("p b f -> p (b f)")

        b0_bf = ps[:, 0, :].bitcast(BF16)

        semnames = ["tensor", "vector", "scalar", "gpsimd", "cc",
                    "dx0", "dx1", "dx2", "dW", "dW2", "dF0", "dF1", "dA0", "dA1", "dO", "dCC", "dM",
                    "dCA", "dCB", "dSB", "ccB"]
        sems = {n: ctx.enter_context(nc.semaphore(n)) for n in semnames}

        P = Prog()
        do = P.do

        out_t = d_out.rearrange("(t p) f -> t p f", p=128)
        x_t = d_x.rearrange("p (t f) -> p t f", t=NT)

        def emit_ln_stats2(x_ap, w_sq=(), s0=0, scratch=None):
            if scratch is None:
                scratch = vw_tmp
            t_sq = do("scalar", lambda e: e.activation(
                scratch, x_ap, AF.Square, accum_out=stats[:, s0:s0 + 1]), waits=list(w_sq))
            t3 = do("scalar", lambda e: e.activation(stats[:, s0 + 2:s0 + 3], stats[:, s0 + 1:s0 + 2],
                                                     AF.Copy, scale=cst[:, 2:3]))
            t4 = do("scalar", lambda e: e.activation(stats[:, s0 + 3:s0 + 4], stats[:, s0 + 2:s0 + 3], AF.Square))
            t5 = do("vector", lambda e: e.tensor_scalar(
                stats[:, s0 + 4:s0 + 5], stats[:, s0:s0 + 1], cst[:, 2:3], stats[:, s0 + 3:s0 + 4],
                ALU.mult, ALU.subtract), waits=[t4])
            t6 = do("scalar", lambda e: e.activation(stats[:, s0 + 5:s0 + 6], stats[:, s0 + 4:s0 + 5],
                                                     AF.Sqrt, bias=cst[:, 0:1]), waits=[t5])
            t7 = do("vector", lambda e: e.reciprocal(stats[:, s0 + 6:s0 + 7], stats[:, s0 + 5:s0 + 6]),
                    waits=[t6])
            return t7

        def emit_ln_stats(x_ap, scratch, w_first=()):
            t1 = do("vector", lambda e: e.scalar_tensor_tensor(
                scratch, x_ap, 1.0, x_ap, ALU.mult, ALU.mult,
                accum_out=stats[:, 0:1]), waits=list(w_first))
            t2 = do("vector", lambda e: e.tensor_reduce(stats[:, 1:2], x_ap, AX.X, ALU.add))
            t3 = do("scalar", lambda e: e.activation(stats[:, 2:3], stats[:, 1:2], AF.Copy,
                                                     scale=cst[:, 2:3]), waits=[t2])
            t4 = do("scalar", lambda e: e.activation(stats[:, 3:4], stats[:, 2:3], AF.Square))
            t5 = do("vector", lambda e: e.tensor_scalar(
                stats[:, 4:5], stats[:, 0:1], cst[:, 2:3], stats[:, 3:4],
                ALU.mult, ALU.subtract), waits=[t4, t1])
            t6 = do("scalar", lambda e: e.activation(stats[:, 5:6], stats[:, 4:5], AF.Sqrt,
                                                     bias=cst[:, 0:1]), waits=[t5])
            t7 = do("vector", lambda e: e.reciprocal(stats[:, 6:7], stats[:, 5:6]), waits=[t6])
            return t7

        t_x0 = do("sync", lambda e: e.dma_start(out=x_res[:, 0:1, :], in_=x_t[:, 0:1, :]),
                  sem="dx0", inc=16)
        t_x0b = do("sync", lambda e: e.dma_start(out=x_res[:, 1:8, :], in_=x_t[:, 1:8, :]),
                   sem="dx2", inc=16)
        t_x1 = do("sync", lambda e: e.dma_start(out=x_res[:, 8:16, :], in_=x_t[:, 8:16, :]),
                  sem="dx1", inc=16)

        do("gpsimd", lambda e: e.dma_start(out=kn_rep[:, :], in_=d_kn.unsqueeze(0).broadcast_to([128, 1024])),
           sem="dM", inc=16)
        do("gpsimd", lambda e: e.dma_start(out=bo_bf[0:1, :], in_=d_bo.unsqueeze(0)),
           sem="dM", inc=16)
        do("gpsimd", lambda e: e.dma_start(out=vw_eib, in_=d_ei.rearrange("n k -> (n k)").unsqueeze(0).broadcast_to([8, 2048])),
           sem="dM", inc=16)
        do("gpsimd", lambda e: e.dma_start(out=qn_fm[:, :], in_=d_qn.rearrange("(c p) -> p c", p=128), allow_slow_non_contiguous=True),
           sem="dM", inc=16)
        do("gpsimd", lambda e: e.dma_start(out=vw_ewb, in_=d_ew.rearrange("n k -> (n k)").unsqueeze(0).broadcast_to([8, 2048])),
           sem="dM", inc=16)
        t_misc = P.tok("dM")
        t_w1a_l = do("gpsimd", lambda e: e.dma_start(out=vw_w1a_stage, in_=d_w1a.rearrange("p (c f) -> p c f", c=4)),
                     sem="dA0", inc=16)
        wq_src = d_wqkv.rearrange("p (k f) -> p k f", k=DC)
        t_wqkv1 = do("gpsimd", lambda e: e.dma_start(out=vw_qkv[:, :, 1024:2048], in_=wq_src[:, :, 1024:2048]),
                     sem="dW", inc=16)
        do("gpsimd", lambda e: e.dma_start(out=vw_qkv[:, :, 0:1024], in_=wq_src[:, :, 0:1024]),
           sem="dW2", inc=16)
        t_wqkv2 = do("gpsimd", lambda e: e.dma_start(out=vw_qkv[:, :, 2048:3072], in_=wq_src[:, :, 2048:3072]),
                     sem="dW2", inc=16)
        do("gpsimd", lambda e: e.dma_start(out=wout_sb[:, :, :], in_=d_wout.rearrange("p (k f) -> p k f", k=DC)),
           sem="dF0", inc=16)
        t_f0_all = P.tok("dF0")
        t_qn = t_misc
        t_kn = t_misc

        do("gpsimd", lambda e: e.memset(ident[:, :], 0.0))
        t_ident = do("gpsimd", lambda e: e.affine_select(
            out=ident[:, :], in_=ident[:, :], compare_op=ALU.not_equal, fill=1.0,
            base=0, pattern=[[-1, 128]], channel_multiplier=1))
        do("gpsimd", lambda e: e.memset(ones_r[0:1, :], 1.0))
        t_ones = do("gpsimd", lambda e: e.memset(ones_c[:, :], 1.0))
        t_pc = do("gpsimd", lambda e: e.iota(pc8[:, :], [[0, 1]], channel_multiplier=1,
                                             allow_small_or_imprecise_dtypes=True))
        t_ksbd0 = do("gpsimd", lambda e: e.memset(ksum_bd.ap().rearrange("p a b c -> p (a b c)"), 0.0))
        t_kvbd0 = do("gpsimd", lambda e: e.memset(arena4.ap()[:, 0:2048], 0.0))
        do("gpsimd", lambda e: e.memset(cst[:, 0:1], EPS_LN))
        do("gpsimd", lambda e: e.memset(cst[:, 1:2], EPS_RMS))
        t_cst = do("gpsimd", lambda e: e.memset(cst[:, 2:3], 1.0 / 1024.0))
        t_cc0 = do("vector", lambda e: e.memset(vw_cc, 0.0))

        t_eif = do("vector", lambda e: e.tensor_copy(vw_eif, vw_eib), waits=[t_misc])
        t_m = do("vector", lambda e: e.tensor_single_scalar(vw_eif, vw_eif, pc8[:, 0:1], ALU.is_equal),
                 waits=[t_eif, t_pc])
        t_mw = do("vector", lambda e: e.tensor_tensor(vw_wexp, vw_eif, vw_ewb, ALU.mult),
                  waits=[t_m])
        t_wexp = do("vector", lambda e: e.tensor_reduce(
            vw_wexp[:, 0:1024], vw_wexp.rearrange("p (n k) -> p n k", k=2), AX.X, ALU.add),
            waits=[t_mw])
        t_gate_st = do("gpsimd", lambda e: e.dma_start(out=d_gate.ap(), in_=vw_wexp[:, 0:1024]),
                       sem="dF1", inc=16, waits=[t_wexp])
        if debug:
            do("gpsimd", lambda e: e.dma_start(out=dbg["wexp"], in_=vw_wexp[:, 0:1024]),
               sem="dO", inc=16, waits=[t_wexp])

        blocks = [(rc, fc) for rc in range(4) for fc in range(8)]
        t_w1aT = None
        for grp in range(4):
            chunk = blocks[grp * 8:(grp + 1) * 8]
            for idx, (rc, fc) in enumerate(chunk):
                w = [t_w1a_l, t_ident]
                if t_w1aT is not None:
                    w.append(t_w1aT)
                do("tensor", lambda e, rc=rc, fc=fc, idx=idx: e.transpose(
                    b0_bf[:, idx * 128:(idx + 1) * 128],
                    vw_w1a_stage[:, rc, fc * 128:(fc + 1) * 128], ident[:, :]),
                    waits=w if idx == 0 else (), mode="T")
            t_tp = P.tok("tensor")
            for idx, (rc, fc) in enumerate(chunk):
                do("vector", lambda e, rc=rc, fc=fc, idx=idx: e.tensor_copy(
                    w1aT[:, fc, rc * 128:(rc + 1) * 128], b0_bf[:, idx * 128:(idx + 1) * 128]),
                    waits=[t_tp] if idx == 0 else ())
            t_w1aT = P.tok("vector")

        GRP_COL = [1024, 1536, 0, 512, 2048, 2560]
        evac_qkv = {}
        tT_evac = {}
        last_qkv = {}
        kv_consumed = None
        kv_add = {}
        sq_done = None
        q_scaled = {}
        ln1_used = {}
        apply_tok = {}

        def emit_ln_statsA(i):
            t_xl = t_x0 if i == 0 else (t_x0b if i < 8 else t_x1)
            t1 = do("vector", lambda e, i=i: e.scalar_tensor_tensor(
                vw_attn_tm, x_res[:, i, :], 1.0, x_res[:, i, :], ALU.mult, ALU.mult,
                accum_out=stats[:, 0:1]), waits=[t_xl, t_cst])
            t2 = do("vector", lambda e, i=i: e.tensor_reduce(
                stats[:, 1:2], x_res[:, i, :], AX.X, ALU.add))
            t3 = do("scalar", lambda e: e.activation(stats[:, 2:3], stats[:, 1:2], AF.Copy,
                                                     scale=cst[:, 2:3]), waits=[t2])
            t4 = do("scalar", lambda e: e.activation(stats[:, 3:4], stats[:, 2:3], AF.Square))
            return t1, t4

        def emit_ln_statsB(t1, t4):
            t5 = do("vector", lambda e: e.tensor_scalar(
                stats[:, 4:5], stats[:, 0:1], cst[:, 2:3], stats[:, 3:4],
                ALU.mult, ALU.subtract), waits=[t4, t1])
            t6 = do("scalar", lambda e: e.activation(stats[:, 5:6], stats[:, 4:5], AF.Sqrt,
                                                     bias=cst[:, 0:1]), waits=[t5])
            return t6

        def emit_ln_apply(i, t6):
            t7 = do("vector", lambda e: e.reciprocal(stats[:, 6:7], stats[:, 5:6]), waits=[t6])
            w_ap = [t7]
            if i >= 1:
                w_ap.append(ln1_used[i - 1])
            return do("vector", lambda e, i=i: e.tensor_scalar(
                ln1_tm[:, :], x_res[:, i, :], stats[:, 2:3], stats[:, 6:7],
                ALU.subtract, ALU.mult), waits=w_ap)

        _t1, _t4 = emit_ln_statsA(0)
        _t6 = emit_ln_statsB(_t1, _t4)
        apply_tok[0] = emit_ln_apply(0, _t6)

        for i in range(NT):
            wT = [apply_tok[i], t_ident]
            if i == 0:
                wT.append(t_w1aT)
            else:
                wT.append(tT_evac[i - 1])
            for j in range(DC):
                do("tensor", lambda e, j=j: e.transpose(
                    b0_bf[:, j * 128:(j + 1) * 128], ln1_tm[:, j * 128:(j + 1) * 128], ident[:, :]),
                    waits=wT if j == 0 else (), mode="T")
            t_tp = P.tok("tensor")
            ln1_used[i] = t_tp
            wev = [t_tp]
            if i >= 2:
                wev.append(last_qkv[i - 2])
            t_ev = do("vector", lambda e, i=i: e.tensor_copy(
                ln1_fm[:, i % 2, :, :].rearrange("p c f -> p (c f)"), b0_bf[:, :]), waits=wev)
            tT_evac[i] = t_ev
            if debug and i == NT - 1:
                do("gpsimd", lambda e: e.dma_start(out=dbg["ln1_0"], in_=ln1_tm[:, :]),
                   sem="dO", inc=16, waits=[apply_tok[i]])

            ln_state = None
            for g in range(6):
                bk = 1 + (g % 2)
                col = GRP_COL[g]
                wq = [t_ev, t_wqkv1 if g < 2 else t_wqkv2]
                prev = (i, g - 2) if g >= 2 else ((i - 1, g + 4) if i >= 1 else None)
                if prev is not None:
                    wq.append(evac_qkv[prev])
                for k in range(DC):
                    do("tensor", lambda e, k=k, col=col, bk=bk, i=i: e.matmul(
                        bank(bk), ln1_fm[:, i % 2, k, :], vw_qkv[:, k, col:col + 512],
                        start=(k == 0), stop=(k == DC - 1)),
                        waits=wq if k == 0 else (), mode="P")
                t_mm = P.tok("tensor")
                if g == 5:
                    last_qkv[i] = t_mm
                if g < 2:
                    dest = k_tile[:, (g % 2) * 512:(g % 2) * 512 + 512]
                    wv = [t_mm]
                    if kv_consumed is not None and g == 0:
                        wv.append(kv_consumed)
                elif g < 4:
                    dest = vw_qstr[:, i, (g % 2) * 512:(g % 2) * 512 + 512]
                    wv = [t_mm]
                    if i in (8, 9) and g == 2:
                        wv.append(t_gate_st)
                    if i >= 1 and g == 2:
                        wv.append(q_scaled[i - 1])
                else:
                    dest = v_tile[:, (g % 2) * 512:(g % 2) * 512 + 512]
                    wv = [t_mm]
                    if kv_consumed is not None and g == 4:
                        wv.append(kv_consumed)
                t_e = do("vector", lambda e, dest=dest, bk=bk: e.tensor_copy(dest, bank(bk)), waits=wv)
                evac_qkv[(i, g)] = t_e

                if g == 1:
                    t_sqk = do("scalar", lambda e: e.activation(
                        vw_attn_fm.rearrange("p c f -> p (c f)"), k_tile[:, :], AF.Square,
                        accum_out=recips[:, 19:20]),
                        waits=[evac_qkv[(i, 0)], evac_qkv[(i, 1)]])
                    sq_done = t_sqk
                    t_rk0 = do("scalar", lambda e: e.activation(
                        recips[:, 20:21], recips[:, 19:20], AF.Sqrt,
                        scale=cst[:, 2:3], bias=cst[:, 1:2]))
                    t_kkn = do("gpsimd", lambda e: e.tensor_tensor(
                        k_tile[:, :], k_tile[:, :], kn_rep[:, :], ALU.mult),
                        waits=[t_sqk, t_kn])
                    t_rk = do("vector", lambda e: e.reciprocal(recips[:, 21:22], recips[:, 20:21]),
                              waits=[t_rk0])
                    t_krelu = do("scalar", lambda e: e.activation(
                        k_tile[:, :], k_tile[:, :], AF.Relu, scale=recips[:, 21:22]),
                        waits=[t_rk, t_kkn])
                if g == 2 and i < NT - 1:
                    ln_state = emit_ln_statsA(i + 1)
                if g == 3:
                    do("scalar", lambda e, i=i: e.activation(
                        vw_attn_fm.rearrange("p c f -> p (c f)"), vw_qstr[:, i, :], AF.Square,
                        accum_out=recips[:, 16:17]),
                        waits=[evac_qkv[(i, 2)], evac_qkv[(i, 3)]])
                    t_rq0 = do("scalar", lambda e: e.activation(
                        recips[:, 17:18], recips[:, 16:17], AF.Sqrt,
                        scale=cst[:, 2:3], bias=cst[:, 1:2]))
                    t_rq = do("vector", lambda e: e.reciprocal(recips[:, 18:19], recips[:, 17:18]),
                              waits=[t_rq0])
                    q_scaled[i] = do("scalar", lambda e, i=i: e.activation(
                        vw_qstr[:, i, :], vw_qstr[:, i, :], AF.Copy, scale=recips[:, 18:19]),
                        waits=[t_rq])
                if g == 4 and ln_state is not None:
                    ln_state = ln_state + (emit_ln_statsB(*ln_state),)

            if debug and i == NT - 1:
                do("gpsimd", lambda e: e.dma_start(out=dbg["k_0"], in_=k_tile[:, :]),
                   sem="dO", inc=16, waits=[t_krelu])
                do("gpsimd", lambda e: e.dma_start(out=dbg["v_0"], in_=v_tile[:, :]),
                   sem="dO", inc=16, waits=[evac_qkv[(i, 5)]])

            b = i // 8
            kvp = 3 if i % 2 == 0 else 5
            ksc = 16 * (i % 2)
            wkv = [t_krelu, evac_qkv[(i, 4)], evac_qkv[(i, 5)], t_ones]
            if i >= 1:
                wkv.append(kv_add[i - 1])
            for h in range(H):
                do("tensor", lambda e, h=h, kvp=kvp: e.matmul(
                    ps[0:64, kvp + h // 8, (h % 8) * 64:(h % 8) * 64 + 64],
                    k_tile[:, h * 64:(h + 1) * 64], v_tile[:, h * 64:(h + 1) * 64],
                    start=True, stop=True),
                    waits=wkv if h == 0 else (), mode="P")
            for h in range(H):
                do("tensor", lambda e, h=h, ksc=ksc: e.matmul(
                    ps[0:64, 7, ksc + h:ksc + h + 1],
                    k_tile[:, h * 64:(h + 1) * 64], ones_c[:, 0:1],
                    start=True, stop=True), mode="P")
            kv_consumed = P.tok("tensor")
            if ln_state is not None:
                apply_tok[i + 1] = emit_ln_apply(i + 1, ln_state[2])
            do("vector", lambda e, b=b, kvp=kvp: e.tensor_tensor(
                vw_cc[:, b * 1040:b * 1040 + 1024],
                ps[0:64, kvp:kvp + 2, :].rearrange("p b2 f -> p (b2 f)"),
                vw_cc[:, b * 1040:b * 1040 + 1024], ALU.add),
                waits=[kv_consumed, t_cc0])
            t_a2 = do("vector", lambda e, b=b, ksc=ksc: e.tensor_tensor(
                vw_cc[:, b * 1040 + 1024: b * 1040 + 1040],
                ps[0:64, 7, ksc:ksc + 16],
                vw_cc[:, b * 1040 + 1024: b * 1040 + 1040], ALU.add))
            kv_add[i] = t_a2

            if i == 7:
                t_ccstA = do("sync", lambda e: e.dma_start(out=d_ccinA.ap(), in_=vw_cc[:, 0:1040]),
                             sem="dCC", inc=16, waits=[kv_add[7]])
            if i == 8:
                t_ccA = do("gpsimd", lambda e: e.collective_compute(
                    "AllReduce", ALU.add,
                    replica_groups=[[0, 1], [2, 3], [4, 5], [6, 7]],
                    ins=[d_ccinA.ap().opt()], outs=[d_ccoutA.ap().opt()]),
                    sem="cc", inc=1, waits=[t_ccstA])
            if i == 9:
                cc_ksA = d_ccoutA.ap()[0:64, 1024:1040].rearrange("p (j two) -> p j two", two=2)
                do("sync", lambda e: e.dma_start(out=sk_sum[0:64, 0:8], in_=cc_ksA[:, :, 0],
                                                 allow_slow_non_contiguous=True),
                   sem="dCA", inc=16, waits=[t_ccA])
                do("sync", lambda e: e.dma_start(out=sk_sum[64:128, 0:8], in_=cc_ksA[:, :, 1],
                                                 allow_slow_non_contiguous=True),
                   sem="dCA", inc=16)

        qkv_done = last_qkv[NT - 1]

        if debug:
            do("gpsimd", lambda e: e.dma_start(out=dbg["cc_sb"], in_=vw_cc),
               sem="dO", inc=16, waits=[kv_add[NT - 1]])
            do("gpsimd", lambda e: e.dma_start(out=dbg["q_str"], in_=vw_qstr.rearrange("p t f -> p (t f)")),
               sem="dO", inc=16, waits=[q_scaled[NT - 1]])

        t_ccstB = do("sync", lambda e: e.dma_start(out=d_ccinB.ap(), in_=vw_cc[:, 1040:2080]),
                     sem="dSB", inc=16, waits=[kv_add[NT - 1]])
        t_ccB = do("gpsimd", lambda e: e.collective_compute(
            "AllReduce", ALU.add,
            replica_groups=[[0, 1], [2, 3], [4, 5], [6, 7]],
            ins=[d_ccinB.ap().opt()], outs=[d_ccoutB.ap().opt()]),
            sem="ccB", inc=1, waits=[t_ccstB])
        kvbd_even = arena4.ap()[0:64, 0:2048].rearrange("p (b2 j f) -> p b2 j f", b2=2, j=8)
        kvbd_odd = arena4.ap()[64:128, 0:2048].rearrange("p (b2 j f) -> p b2 j f", b2=2, j=8)
        cc_kvA = d_ccoutA.ap()[0:64, 0:1024].rearrange("p (j two f) -> p j two f", j=8, two=2)
        cc_kvB = d_ccoutB.ap()[0:64, 0:1024].rearrange("p (j two f) -> p j two f", j=8, two=2)
        do("gpsimd", lambda e: e.dma_start(out=kvbd_even[:, 0, :, 0:64], in_=cc_kvA[:, :, 0, :]),
           sem="dCA", inc=16, waits=[t_ccA, t_kvbd0])
        do("gpsimd", lambda e: e.dma_start(out=kvbd_odd[:, 0, :, 64:128], in_=cc_kvA[:, :, 1, :]),
           sem="dCA", inc=16)
        t_rbA = P.tok("dCA")
        cc_ksB = d_ccoutB.ap()[0:64, 1024:1040].rearrange("p (j two) -> p j two", two=2)
        do("sync", lambda e: e.dma_start(out=sk_sum[0:64, 8:16], in_=cc_ksB[:, :, 0],
                                         allow_slow_non_contiguous=True),
           sem="dCB", inc=16, waits=[t_ccB])
        do("sync", lambda e: e.dma_start(out=sk_sum[64:128, 8:16], in_=cc_ksB[:, :, 1],
                                         allow_slow_non_contiguous=True),
           sem="dCB", inc=16)
        do("gpsimd", lambda e: e.dma_start(out=kvbd_even[:, 1, :, 0:64], in_=cc_kvB[:, :, 0, :]),
           sem="dCB", inc=16, waits=[t_ccB])
        do("gpsimd", lambda e: e.dma_start(out=kvbd_odd[:, 1, :, 64:128], in_=cc_kvB[:, :, 1, :]),
           sem="dCB", inc=16)
        t_rbB = P.tok("dCB")
        for j in range(8):
            do("vector", lambda e, j=j: e.tensor_copy(
                ksum_bd[0:64, 0, j, 2 * j: 2 * j + 1], sk_sum[0:64, j: j + 1]),
                waits=[t_rbA, t_ksbd0] if j == 0 else ())
            do("vector", lambda e, j=j: e.tensor_copy(
                ksum_bd[64:128, 0, j, 2 * j + 1: 2 * j + 2], sk_sum[64:128, j: j + 1]))
        t_ksbdA = P.tok("vector")
        t_ksbdB = None

        b0_free = tT_evac[NT - 1]
        qfm_used = None
        attn_evac = None
        t_evac_t = None
        wout_evac = {}
        norm_read = None
        t_store = {}
        attn_div_prev = None
        last_wout = None
        attn_all_done = None
        pend = None

        P4O = [0, 1, 2, 3, 8, 9, 10, 11, 12, 13, 14, 15, 4, 5, 6, 7]
        l2T_done = None

        def flush_pend():
            nonlocal pend, b0_free, t_evac_t, l2T_done
            pi, t_ln2p = pend
            for j in range(DC):
                do("tensor", lambda e, j=j: e.transpose(
                    b0_bf[:, j * 128:(j + 1) * 128], ln1_tm[:, j * 128:(j + 1) * 128], ident[:, :]),
                    waits=[t_ln2p, b0_free] if j == 0 else (), mode="T")
            t_l2T = P.tok("tensor")
            l2T_done = t_l2T
            t_l2fm = do("vector", lambda e: e.tensor_copy(
                ln1_fm[:, 0, :, :].rearrange("p c f -> p (c f)"), b0_bf[:, :]), waits=[t_l2T])
            b0_free = t_l2fm
            wt_ = [t_l2fm, attn_evac, t_w1aT]
            for m in range(4):
                for k in range(DC):
                    do("tensor", lambda e, m=m, k=k: e.matmul(
                        bankpair(1)[:, m * 128:(m + 1) * 128], w1aT[:, k, m * 128:(m + 1) * 128],
                        ln1_fm[:, 0, k, :], start=(k == 0), stop=(k == DC - 1)),
                        waits=wt_ if (m == 0 and k == 0) else (), mode="P")
            t_tmm = P.tok("tensor")
            wev = [t_tmm]
            if pi == 0:
                wev.append(qkv_done)
            t_evac_t = do("vector", lambda e, pi=pi: e.tensor_copy(
                vw_t[:, :, pi * 128:(pi + 1) * 128],
                bankpair(1)[:, 0:512].rearrange("p (m n) -> p m n", n=128)), waits=wev)
            pend = None

        sA2 = h_sb.ap().rearrange("p a f -> p (a f)").rearrange("p (s g) -> p s g", s=2)
        w2a_src = d_w2a.rearrange("p (c f) -> p c f", c=4)
        t_sA = {}
        sA_used = {}

        def emit_sA_load(c8):
            slot = c8 % 2
            wL = [t_w1aT] if c8 < 2 else [sA_used[c8 - 2]]
            t_sA[c8] = do("gpsimd", lambda e, c8=c8, slot=slot: e.dma_start(
                out=sA2[:, slot, :].rearrange("p (rc f) -> p rc f", rc=4),
                in_=w2a_src[:, :, c8 * 512:(c8 + 1) * 512]),
                sem="dA0" if slot == 0 else "dA1", inc=16, waits=wL)

        emit_sA_load(0)
        emit_sA_load(1)

        for pos in range(NT):
            i = P4O[pos]
            b = i // 8
            if pos == 3:
                for j in range(8):
                    do("vector", lambda e, j=j: e.tensor_copy(
                        ksum_bd[0:64, 1, j, 2 * j: 2 * j + 1], sk_sum[0:64, 8 + j: 9 + j]),
                        waits=[t_rbB] if j == 0 else ())
                    do("vector", lambda e, j=j: e.tensor_copy(
                        ksum_bd[64:128, 1, j, 2 * j + 1: 2 * j + 2], sk_sum[64:128, 8 + j: 9 + j]))
                t_ksbdB = P.tok("vector")
                if debug:
                    do("gpsimd", lambda e: e.dma_start(out=dbg["kv_bd"], in_=vw_kvbd.rearrange("p b f -> p (b f)")),
                       sem="dO", inc=16, waits=[t_rbA, t_rbB])
                    do("gpsimd", lambda e: e.dma_start(out=dbg["sk"], in_=sk_sum[:, :]),
                       sem="dO", inc=16, waits=[t_rbB])
                    do("gpsimd", lambda e: e.dma_start(out=dbg["ksbd"], in_=ksum_bd.ap().rearrange("p a b c -> p (a b c)")),
                       sem="dO", inc=16, waits=[t_ksbdB])
            t_kvbd_b = t_rbA if b == 0 else t_rbB
            t_ksbd_b = t_ksbdA if b == 0 else t_ksbdB
            wT = [q_scaled[i], b0_free, t_kvbd_b, t_ksbd_b]
            if pos == 0:
                wT.append(qkv_done)
            if qfm_used is not None:
                wT.append(qfm_used)
            for j in range(DC):
                do("tensor", lambda e, i=i, j=j: e.transpose(
                    b0_bf[:, j * 128:(j + 1) * 128], vw_qstr[:, i, j * 128:(j + 1) * 128], ident[:, :]),
                    waits=wT if j == 0 else (), mode="T")
            t_qT = P.tok("tensor")
            for j in range(DC):
                do("vector", lambda e, j=j: e.tensor_scalar(
                    q_fm[:, j, :], b0_bf[:, j * 128:(j + 1) * 128],
                    qn_fm[:, j:j + 1], 0.0, ALU.mult, ALU.max),
                    waits=[t_qT, t_qn] if j == 0 else ())
            t_qfm = P.tok("vector")
            b0_free = t_qfm
            if debug and pos == NT - 1:
                do("gpsimd", lambda e: e.dma_start(out=dbg["qfm"], in_=q_fm.rearrange("p c f -> p (c f)")),
                   sem="dO", inc=16, waits=[t_qfm])

            wN = [t_qfm, t_ksbd_b]
            if norm_read is not None:
                wN.append(norm_read)
            if pos == 0:
                wN.append(kv_add[NT - 1])
            for j in range(DC):
                do("tensor", lambda e, j=j, b=b: e.matmul(
                    bank(7)[:, 0:16], q_fm[:, j, :], ksum_bd[:, b, j, :],
                    start=(j == 0), stop=(j == DC - 1)),
                    waits=wN if j == 0 else (), mode="P")
            t_norm = P.tok("tensor")
            wA = [t_kvbd_b]
            if t_evac_t is not None:
                wA.append(t_evac_t)
            if attn_evac is not None:
                wA.append(attn_evac)
            if pos == 0:
                wA += [evac_qkv[(NT - 1, 4)], evac_qkv[(NT - 1, 5)]]
            for j in range(DC):
                do("tensor", lambda e, j=j, b=b: e.matmul(
                    bankpair(1)[:, j * 128:(j + 1) * 128], q_fm[:, j, :], vw_kvbd[:, b, j * 128:(j + 1) * 128],
                    start=True, stop=True),
                    waits=wA if j == 0 else (), mode="P")
            t_attnmm = P.tok("tensor")
            qfm_used = t_attnmm
            if pos == NT - 1:
                attn_all_done = t_attnmm

            t_eps = do("scalar", lambda e: e.activation(
                recips[:, 16:32], bank(7)[:, 0:16], AF.Identity, bias=cst[:, 1:2]),
                waits=[t_norm])
            norm_read = t_eps
            t_rec = do("vector", lambda e: e.reciprocal(recips[:, 0:16], recips[:, 16:32]),
                       waits=[t_eps])
            if debug and pos == NT - 1:
                do("gpsimd", lambda e: e.dma_start(out=dbg["recip0"], in_=recips[:, 0:16]),
                   sem="dO", inc=16, waits=[t_rec])

            wD = [t_attnmm, t_rec]
            if pos == 0:
                wD.append(sq_done)
            if attn_div_prev is not None:
                wD.append(attn_div_prev)
            t_div = do("vector", lambda e: e.tensor_tensor(
                vw_attn_tm.rearrange("p (h d) -> p h d", d=64),
                bankpair(1).rearrange("p (h d) -> p h d", d=64),
                recips[:, 0:16].unsqueeze(-1).broadcast_to([128, 16, 64]),
                ALU.mult), waits=wD)
            attn_evac = t_div
            if debug and pos == NT - 1:
                do("gpsimd", lambda e: e.dma_start(out=dbg["attn0"], in_=vw_attn_tm),
                   sem="dO", inc=16, waits=[t_div])

            for j in range(DC):
                do("tensor", lambda e, j=j: e.transpose(
                    b0_bf[:, j * 128:(j + 1) * 128], vw_attn_tm[:, j * 128:(j + 1) * 128], ident[:, :]),
                    waits=[t_div, b0_free] if j == 0 else (), mode="T")
            t_aT = P.tok("tensor")
            t_afm = do("vector", lambda e: e.tensor_copy(
                vw_attn_fm.rearrange("p c f -> p (c f)"), b0_bf[:, :]),
                waits=[t_aT] + ([sq_done] if i == 0 else []))
            attn_div_prev = t_afm
            b0_free = t_afm

            wb = 3 if pos % 2 == 0 else 5
            wW = [t_afm, t_f0_all, t_misc, t_ones]
            if pos < 2:
                wW.append(kv_add[NT - 1])
            elif (pos - 2) in wout_evac:
                wW.append(wout_evac[pos - 2])
            for half in range(2):
                for j in range(DC):
                    do("tensor", lambda e, j=j, half=half, wb=wb: e.matmul(
                        bank(wb + half), vw_attn_fm[:, j, :], wout_sb[:, j, half * 512:(half + 1) * 512],
                        start=(j == 0), stop=False),
                        waits=wW if (j == 0 and half == 0) else (), mode="P")
                do("tensor", lambda e, half=half, wb=wb: e.matmul(
                    bank(wb + half), ones_r[0:1, :], bo_bf[0:1, half * 512:(half + 1) * 512],
                    start=False, stop=True), mode="P")
            t_wout = P.tok("tensor")
            last_wout = t_wout

            if pend is not None:
                flush_pend()

            wY = [t_wout]
            if pos == 0:
                wY += [t_ccstA, t_ccstB]
            t_y1 = do("vector", lambda e, wb=wb: e.tensor_scalar(
                vw_y1, bankpair(wb), 1.0, 0.0, ALU.mult, ALU.add,
                accum_out=stats[:, 1:2]), waits=wY)
            wout_evac[pos] = t_y1
            if debug and i == NT - 1:
                do("gpsimd", lambda e: e.dma_start(out=dbg["y1_0"], in_=vw_y1),
                   sem="dO", inc=16, waits=[t_y1])
            t_rs = emit_ln_stats2(vw_y1, [t_y1])
            t_lnA = do("vector", lambda e: e.tensor_scalar(
                vw_tmp, vw_y1, stats[:, 2:3], stats[:, 6:7], ALU.subtract, ALU.mult),
                waits=[t_rs])
            t_res = do("vector", lambda e, i=i: e.scalar_tensor_tensor(
                x_res[:, i, :], x_res[:, i, :], 1.0, vw_tmp, ALU.mult, ALU.add,
                accum_out=stats[:, 9:10]), waits=[t_lnA])

            if i >= 8:
                t_store[i] = do("sync", lambda e, i=i: e.dma_start(out=out_t[i], in_=x_res[:, i, :]),
                                sem="dO", inc=16, waits=[t_res])
            else:
                t_rs2 = emit_ln_stats2(x_res[:, i, :], [t_res], s0=8)
                w_l2 = [t_rs2]
                if pos == 0:
                    w_l2.append(ln1_used[NT - 1])
                else:
                    w_l2.append(l2T_done)
                t_ln2 = do("vector", lambda e, i=i: e.tensor_scalar(
                    ln1_tm[:, :], x_res[:, i, :], stats[:, 10:11], stats[:, 14:15], ALU.subtract, ALU.mult),
                    waits=w_l2)
                pend = (i, t_ln2)

            if pos >= 8:
                c8 = pos - 8
                slot = c8 % 2
                sAv = sA2[:, slot, :].rearrange("p (rc f) -> p rc f", rc=4)
                for halfg in range(2):
                    for idx in range(8):
                        n = halfg * 8 + idx
                        rc, fc = n // 4, n % 4
                        do("tensor", lambda e, sAv=sAv, rc=rc, fc=fc, idx=idx: e.transpose(
                            b0_bf[:, idx * 128:(idx + 1) * 128],
                            sAv[:, rc, fc * 128:(fc + 1) * 128], ident[:, :]),
                            waits=[t_sA[c8], b0_free] if idx == 0 else (), mode="T")
                    t_tp2 = P.tok("tensor")
                    wv2 = [t_tp2]
                    if halfg == 0:
                        wv2.append(P.tok("dO"))
                    for idx in range(8):
                        n = halfg * 8 + idx
                        rc, fc = n // 4, n % 4
                        do("vector", lambda e, c8=c8, rc=rc, fc=fc, idx=idx: e.tensor_copy(
                            vw_w2aT[:, c8 * 4 + fc, rc * 128:(rc + 1) * 128],
                            b0_bf[:, idx * 128:(idx + 1) * 128]),
                            waits=wv2 if idx == 0 else ())
                    b0_free = P.tok("vector")
                sA_used[c8] = b0_free
                if c8 + 2 < 8:
                    emit_sA_load(c8 + 2)

        if pend is not None:
            flush_pend()

        t_t_all = t_evac_t
        if debug:
            do("gpsimd", lambda e: e.dma_start(
                out=dbg["xr_mid"], in_=x_res.ap().rearrange("p t f -> p (t f)")),
                sem="dO", inc=16, waits=[P.tok("vector")])

        do("gpsimd", lambda e: e.dma_start(
            out=vw_w1b, in_=d_w1b.rearrange("p (c f) -> p c f", c=4)),
            sem="dF1", inc=16, waits=[qkv_done])
        do("gpsimd", lambda e: e.dma_start(
            out=vw_w2b, in_=d_w2b.rearrange("p (c f) -> p c f", c=4)),
            sem="dF1", inc=16)
        for cpair in range(4):
            for e2 in range(2):
                do("gpsimd", lambda e, cpair=cpair, e2=e2: e.dma_start(
                    out=vw_wrep[e2 * 64:(e2 + 1) * 64, cpair, :],
                    in_=d_gate.ap()[2 * cpair + e2: 2 * cpair + e2 + 1, :].broadcast_to([64, 1024])),
                    sem="dF1", inc=16,
                    waits=([P.tok("dF1"), last_wout, attn_all_done]
                           + ([P.tok("dO")] if debug else [])) if (cpair == 0 and e2 == 0) else ())
        t_f1_all = P.tok("dF1")
        if debug:
            do("gpsimd", lambda e: e.dma_start(out=dbg["wrep"], in_=vw_wrep.rearrange("p c f -> p (c f)")),
               sem="dO", inc=16, waits=[t_f1_all])
            do("gpsimd", lambda e: e.dma_start(out=dbg["t_sb"], in_=vw_t.rearrange("p c f -> p (c f)")),
               sem="dO", inc=16, waits=[t_t_all])

        t_w2aT = sA_used[7]

        u_evac = {}
        gel = {}
        umm = {}
        u_free = None
        first_h = True
        seq = [(c, nb, Fc) for c in range(4) for nb in range(2) for Fc in range(32)]

        def emit_u(c, nb, Fc):
            w = [gel[(c, nb, Fc)]]
            if Fc == 0:
                w.append(u_free if u_free is not None else wout_evac[15])
            u_start = (Fc == 0)
            u_stop = (Fc == 31)
            do("tensor", lambda e, c=c, Fc=Fc, s=u_start, st=u_stop: e.matmul(
                ps[0:64, 5, :], vw_w2aT[:, Fc, (2 * c) * 64:(2 * c) * 64 + 64],
                h_sb[:, Fc % 4, 0:512],
                start=s, stop=st, tile_position=(0, 0)),
                waits=w, mode="C")
            do("tensor", lambda e, c=c, Fc=Fc, s=u_start, st=u_stop: e.matmul(
                ps[64:128, 5, :], vw_w2aT[:, Fc, (2 * c + 1) * 64:(2 * c + 1) * 64 + 64],
                h_sb[:, Fc % 4, 512:1024],
                start=s, stop=st, tile_position=(0, 64)),
                mode="C")
            umm[(c, nb, Fc)] = P.tok("tensor")

        def emit_u_evac(c, nb):
            nonlocal_u = [umm[(c, nb, 31)], t_f1_all, qfm_used]
            u_evac[(c, nb)] = do("vector", lambda e, c=c, nb=nb: e.tensor_tensor(
                vw_u[:, c, nb * 512:(nb + 1) * 512], bank(5),
                vw_wrep[:, c, nb * 512:(nb + 1) * 512], ALU.mult), waits=nonlocal_u)
            return u_evac[(c, nb)]

        for s, (c, nb, Fc) in enumerate(seq):
            hp = 3 if Fc % 2 == 0 else 6
            w = [t_f1_all, t_w2aT]
            if first_h:
                w += [t_t_all, wout_evac[14], wout_evac[15], norm_read]
                first_h = False
            if (c, nb, Fc - 2) in gel:
                w.append(gel[(c, nb, Fc - 2)])
            elif Fc <= 1 and (c, nb) != (0, 0):
                prev = (c, nb - 1) if nb == 1 else (c - 1, 1)
                w.append(gel[prev + (30 + Fc,)])
            do("tensor", lambda e, c=c, nb=nb, Fc=Fc, hp=hp: e.matmul(
                bank(hp), vw_w1b[0:64, c, Fc * 128:(Fc + 1) * 128],
                vw_t[0:64, c, nb * 512:(nb + 1) * 512],
                start=True, stop=True, tile_position=(0, 0)),
                waits=w, mode="R")
            do("tensor", lambda e, c=c, nb=nb, Fc=Fc, hp=hp: e.matmul(
                bank(hp + 1), vw_w1b[64:128, c, Fc * 128:(Fc + 1) * 128],
                vw_t[64:128, c, nb * 512:(nb + 1) * 512],
                start=True, stop=True, tile_position=(64, 0)),
                mode="R")
            t_h = P.tok("tensor")
            wg = [t_h]
            if (c, nb, Fc - 4) in umm:
                wg.append(umm[(c, nb, Fc - 4)])
            gel[(c, nb, Fc)] = do("scalar", lambda e, Fc=Fc, hp=hp: e.activation(
                h_sb[:, Fc % 4, :], bankpair(hp), AF.Gelu), waits=wg)
            if s > 0:
                pc_, pn_, pf_ = seq[s - 1]
                emit_u(pc_, pn_, pf_)
                if pf_ == 31:
                    u_free = emit_u_evac(pc_, pn_)
        emit_u(*seq[-1])
        u_free = emit_u_evac(3, 1)
        t_u_all = P.tok("vector")
        if debug:
            do("gpsimd", lambda e: e.dma_start(out=dbg["u_sb"], in_=vw_u.rearrange("p c f -> p (c f)")),
               sem="dO", inc=16, waits=[t_u_all])

        y_prev = None
        for nt in range(NA):
            wy = [t_u_all]
            if y_prev is not None:
                wy.append(y_prev)
            for fh in range(2):
                if not (nt == 0 and fh == 0):
                    P.drain()
                for e_ in range(E):
                    half = e_ % 2
                    bk = (1 + fh) if half == 0 else (3 + fh)
                    s = (e_ < 2)
                    st = (e_ >= E - 2)
                    do("tensor", lambda en, e_=e_, fh=fh, bk=bk, nt=nt, s=s, st=st: en.matmul(
                        bank(bk), vw_u[(e_ % 2) * 64:(e_ % 2) * 64 + 64, e_ // 2, nt * 128:(nt + 1) * 128],
                        vw_w2b[(e_ % 2) * 64:(e_ % 2) * 64 + 64, e_ // 2, fh * 512:(fh + 1) * 512],
                        start=s, stop=st, tile_position=((e_ % 2) * 64, 0)),
                        waits=wy if (e_ == 0 and fh == 0) else (), mode="R")
            t_ymm = P.tok("tensor")
            do("vector", lambda e: e.tensor_copy(vw_y1, bankpair(1)), waits=[t_ymm])
            t_c2 = do("vector", lambda e: e.scalar_tensor_tensor(
                vw_y1, bankpair(3), 1.0, vw_y1, ALU.mult, ALU.add,
                accum_out=stats[:, 1:2]))
            y_prev = t_c2
            t_rs = emit_ln_stats2(vw_y1, [t_c2])
            t_lnA = do("vector", lambda e: e.tensor_scalar(
                vw_tmp, vw_y1, stats[:, 2:3], stats[:, 6:7], ALU.subtract, ALU.mult),
                waits=[t_rs])
            t_res = do("vector", lambda e, nt=nt: e.scalar_tensor_tensor(
                x_res[:, nt, :], x_res[:, nt, :], 1.0, vw_tmp, ALU.mult, ALU.add), waits=[t_lnA])
            t_store[nt] = do("sync", lambda e, nt=nt: e.dma_start(out=out_t[nt], in_=x_res[:, nt, :]),
                             sem="dO", inc=16, waits=[t_res])

        P.wait_only("sync", [P.tok("dO")])

        block = ctx.enter_context(nc.Block())
        for eng_name in ENGINES:
            ops = P.stream[eng_name]
            if not ops:
                continue

            def body(eng, ops=ops, eng_name=eng_name):
                seen = {}
                last_mode = None
                for waits, fn, sem, inc, mode in ops:
                    if eng_name == "tensor" and mode is not None:
                        if last_mode is not None and mode != last_mode:
                            eng.drain()
                        last_mode = mode
                    for (ws, wv) in waits:
                        if ws == eng_name or wv <= 0:
                            continue
                        if seen.get(ws, 0) >= wv:
                            continue
                        eng.wait_ge(sems[ws], wv)
                        seen[ws] = wv
                    if fn is None:
                        continue
                    inst = fn(eng)
                    if inst is not None and sem is not None:
                        inst.then_inc(sems[sem], inc)

            getattr(block, eng_name)(body)

    return nc


_CACHE = {}


def _get_nc(debug=False):
    key = bool(debug)
    if key not in _CACHE:
        _CACHE[key] = build_nc(debug=key)
    return _CACHE[key]


def kernel(L0_attn_x, L0_attn_W_qkv, L0_attn_qn_w, L0_attn_kn_w, L0_attn_W_out,
           L0_attn_b_out, L0_ffn_expert_weights, L0_ffn_expert_indices,
           L0_ffn_w1_A, L0_ffn_w1_B, L0_ffn_w2_A, L0_ffn_w2_B, _debug=False):
    def pmajor(a):
        a = np.asarray(a, dtype=np.float32)
        r = a.shape[0] // 128
        return np.ascontiguousarray(
            a.reshape(r, 128, a.shape[1]).transpose(1, 0, 2).reshape(128, r * a.shape[1]))

    x = np.ascontiguousarray(np.asarray(L0_attn_x, dtype=np.float32))
    shared = dict(
        wqkv=pmajor(L0_attn_W_qkv),
        qn=np.ascontiguousarray(np.asarray(L0_attn_qn_w, dtype=np.float32)),
        kn=np.ascontiguousarray(np.asarray(L0_attn_kn_w, dtype=np.float32)),
        wout=pmajor(L0_attn_W_out),
        bo=np.ascontiguousarray(np.asarray(L0_attn_b_out, dtype=np.float32)),
        w1a=pmajor(L0_ffn_w1_A),
        w1b=pmajor(L0_ffn_w1_B),
        w2a=pmajor(L0_ffn_w2_A),
        w2b=pmajor(L0_ffn_w2_B),
    )
    ew = np.ascontiguousarray(np.asarray(L0_ffn_expert_weights, dtype=np.float32))
    ei = np.ascontiguousarray(np.asarray(L0_ffn_expert_indices, dtype=np.int32))

    in_maps = []
    for c in range(8):
        bA = c // 2
        bB = 4 + c // 2
        r0 = (c % 2) * 1024
        xc = np.concatenate([x[bA, r0:r0 + 1024], x[bB, r0:r0 + 1024]], axis=0)
        g0 = bA * 2048 + r0
        m = dict(shared)
        m["x"] = pmajor(xc)
        m["ew"] = np.ascontiguousarray(ew[g0:g0 + 1024])
        m["ei"] = np.ascontiguousarray(ei[g0:g0 + 1024])
        in_maps.append(m)

    nc = _get_nc(debug=_debug)
    res = run_bass_kernel_spmd(nc, in_maps, core_ids=list(range(8)))

    y = np.empty((8, 2048, 1024), dtype=np.float32)
    for c in range(8):
        o = np.asarray(res.results[c]["out"], dtype=np.float32)
        bA = c // 2
        bB = 4 + c // 2
        r0 = (c % 2) * 1024
        y[bA, r0:r0 + 1024] = o[0:1024]
        y[bB, r0:r0 + 1024] = o[1024:2048]
    if _debug:
        return y, [res.results[c] for c in range(8)]
    return y


# version 13
# speedup vs baseline: 1.1361x; 1.0640x over previous
import numpy as np
from contextlib import ExitStack

import concourse.bass as bass
import concourse.mybir as mybir
from concourse.bass_utils import run_bass_kernel_spmd

F32 = mybir.dt.float32
BF16 = mybir.dt.bfloat16
I32 = mybir.dt.int32
AF = mybir.ActivationFunctionType
ALU = mybir.AluOpType
AX = mybir.AxisListType

NT = 16
NA = 8
D = 1024
DC = 8
H = 16
FF = 4096
E = 8
EPS_LN = 1e-5
EPS_RMS = 1e-6
EPS_DEN = 1e-6

ENGINES = ("tensor", "vector", "scalar", "gpsimd", "sync")


class Prog:

    DMA_SEMS = ("dx0", "dx1", "dx2", "dW", "dW2", "dF0", "dF1", "dA0", "dA1", "dO", "dCC", "dM", "dCA", "dCB", "dSB")

    def __init__(self):
        self.stream = {e: [] for e in ENGINES}
        self.cnt = {}

    def do(self, eng, fn, waits=(), sem=None, inc=1, mode=None):
        for (ws, wv) in waits:
            if ws in self.DMA_SEMS:
                assert wv == self.cnt.get(ws, 0), \
                    f"prefix wait on dma sem {ws}: {wv} != {self.cnt.get(ws, 0)}"
        if sem is None:
            sem = eng
            inc = 1
        c = self.cnt.get(sem, 0) + inc
        self.cnt[sem] = c
        self.stream[eng].append((tuple(waits), fn, sem, inc, mode))
        return (sem, c)

    def drain(self):
        self.stream["tensor"].append(((), lambda e: e.drain(), None, 0, None))

    def wait_only(self, eng, waits):
        self.stream[eng].append((tuple(waits), None, None, 0, None))

    def tok(self, sem):
        return (sem, self.cnt.get(sem, 0))


def build_nc(debug=False):
    nc = bass.Bass("TRN2", target_bir_lowering=False)

    d_x = nc.dram_tensor("x", [128, 16384], F32, kind="ExternalInput").ap()
    d_wqkv = nc.dram_tensor("wqkv", [128, 24576], F32, kind="ExternalInput").ap()
    d_qn = nc.dram_tensor("qn", [1024], F32, kind="ExternalInput").ap()
    d_kn = nc.dram_tensor("kn", [1024], F32, kind="ExternalInput").ap()
    d_wout = nc.dram_tensor("wout", [128, 8192], F32, kind="ExternalInput").ap()
    d_bo = nc.dram_tensor("bo", [1024], F32, kind="ExternalInput").ap()
    d_ew = nc.dram_tensor("ew", [1024, 2], F32, kind="ExternalInput").ap()
    d_ei = nc.dram_tensor("ei", [1024, 2], I32, kind="ExternalInput").ap()
    d_w1a = nc.dram_tensor("w1a", [128, 4096], F32, kind="ExternalInput").ap()
    d_w1b = nc.dram_tensor("w1b", [128, 16384], F32, kind="ExternalInput").ap()
    d_w2a = nc.dram_tensor("w2a", [128, 16384], F32, kind="ExternalInput").ap()
    d_w2b = nc.dram_tensor("w2b", [128, 4096], F32, kind="ExternalInput").ap()
    d_out = nc.dram_tensor("out", [2048, 1024], F32, kind="ExternalOutput").ap()

    d_ccinA = nc.dram_tensor("cc_inA", [64, 1040], F32)
    d_ccoutA = nc.dram_tensor("cc_outA", [64, 1040], F32)
    d_ccinB = nc.dram_tensor("cc_inB", [64, 1040], F32)
    d_ccoutB = nc.dram_tensor("cc_outB", [64, 1040], F32)
    d_gate = nc.dram_tensor("gate_scr", [8, 1024], F32)

    dbg = {}
    if debug:
        for name, shape in [
            ("ln1_0", [128, 1024]), ("q_str", [128, 16384]), ("k_0", [128, 1024]),
            ("v_0", [128, 1024]), ("cc_sb", [64, 2080]), ("kv_bd", [128, 2048]),
            ("sk", [128, 16]), ("recip0", [128, 16]), ("attn0", [128, 1024]),
            ("y1_0", [128, 1024]), ("xr_mid", [128, 16384]), ("t_sb", [128, 4096]),
            ("u_sb", [128, 4096]), ("wrep", [128, 4096]), ("wexp", [8, 1024]),
            ("qfm", [128, 1024]), ("ksbd", [128, 256]),
        ]:
            dbg[name] = nc.dram_tensor("dbg_" + name, shape, F32, kind="ExternalOutput").ap()

    with ExitStack() as ctx:
        sb = lambda name, shape, dt: ctx.enter_context(nc.sbuf_tensor(name, shape, dt))

        x_res = sb("x_res", [128, NT, 1024], F32)
        arena1 = sb("arena1", [128, 12288], F32)
        arena2 = sb("arena2", [128, 8192], F32)
        wout_sb = sb("wout_sb", [128, DC, 1024], BF16)
        w1aT = sb("w1aT", [128, DC, 512], BF16)
        h_sb = sb("h_sb", [128, 4, 1024], BF16)
        arena4 = sb("arena4", [128, 4096], BF16)
        arena3 = sb("arena3", [128, 2080], F32)
        ksum_bd = sb("ksum_bd", [128, 2, DC, 16], BF16)
        sk_sum = sb("sk_sum", [128, 16], F32)
        ident = sb("ident", [128, 128], BF16)
        ln1_tm = sb("ln1_tm", [128, 1024], BF16)
        ln1_fm = sb("ln1_fm", [128, 2, DC, 128], BF16)
        k_tile = sb("k_tile", [128, 1024], BF16)
        v_tile = sb("v_tile", [128, 1024], BF16)
        kn_rep = sb("kn_rep", [128, 1024], BF16)
        qn_fm = sb("qn_fm", [128, 8], F32)
        bo_bf = sb("bo_bf", [1, 1024], BF16)
        ones_r = sb("ones_r", [1, 128], BF16)
        ones_c = sb("ones_c", [128, 1], BF16)
        recips = sb("recips", [128, 32], F32)
        stats = sb("stats", [128, 16], F32)
        pc8 = sb("pc8", [8, 1], F32)
        cst = sb("cst", [128, 4], F32)

        ps = ctx.enter_context(nc.psum_tensor("ps", [128, 8, 512], F32))

        a1_bf = arena1.ap().bitcast(BF16)
        vw_qkv = a1_bf.rearrange("p (k f) -> p k f", k=DC)
        vw_w1b = a1_bf[:, 0:16384].rearrange("p (c f) -> p c f", c=4)
        vw_w2b = a1_bf[:, 16384:20480].rearrange("p (c f) -> p c f", c=4)
        vw_t = a1_bf[:, 20480:24576].rearrange("p (c f) -> p c f", c=4)

        a2_bf = arena2.ap().bitcast(BF16)
        vw_qstr = a2_bf.rearrange("p (t f) -> p t f", t=NT)
        vw_u = a2_bf[:, 0:4096].rearrange("p (c f) -> p c f", c=4)
        vw_sA = a2_bf[:, 4096:8192].rearrange("p (s c f) -> p s c f", s=2, c=4)

        vw_w1a_stage = h_sb.ap()

        vw_kvbd = arena4.ap()[:, 0:2048].rearrange("p (b f) -> p b f", b=2)
        vw_attn_tm = arena4.ap()[:, 2048:3072]
        vw_attn_fm = arena4.ap()[:, 3072:4096].rearrange("p (c f) -> p c f", c=DC)
        vw_wrep = arena4.ap().rearrange("p (c f) -> p c f", c=4)

        vw_cc = arena3.ap()[0:64, 0:2080]
        vw_y1 = arena3.ap()[:, 0:1024]
        vw_tmp = arena3.ap()[:, 1024:2048]
        vw_ewb = arena2.ap()[0:8, 2048:4096]
        vw_eib = arena2.ap()[0:8, 4096:6144].bitcast(I32)
        vw_eif = arena2.ap()[0:8, 6144:8192]
        vw_wexp = arena2.ap()[0:8, 4096:6144]

        q_fm = ln1_fm.ap()[:, 1, :, :]

        vw_w2aT = (x_res.ap()[:, 8:16, :].rearrange("p t f -> p (t f)")
                   .bitcast(BF16).rearrange("p (c f) -> p c f", c=32))

        def bank(i):
            return ps[:, i, :]

        def bankpair(i):
            return ps[:, i:i + 2, :].rearrange("p b f -> p (b f)")

        b0_bf = ps[:, 0, :].bitcast(BF16)

        semnames = ["tensor", "vector", "scalar", "gpsimd", "cc",
                    "dx0", "dx1", "dx2", "dW", "dW2", "dF0", "dF1", "dA0", "dA1", "dO", "dCC", "dM",
                    "dCA", "dCB", "dSB", "ccB"]
        sems = {n: ctx.enter_context(nc.semaphore(n)) for n in semnames}

        P = Prog()
        do = P.do

        out_t = d_out.rearrange("(t p) f -> t p f", p=128)
        x_t = d_x.rearrange("p (t f) -> p t f", t=NT)

        def emit_ln_stats2(x_ap, w_sq=(), s0=0, scratch=None):
            if scratch is None:
                scratch = vw_tmp
            t_sq = do("scalar", lambda e: e.activation(
                scratch, x_ap, AF.Square, accum_out=stats[:, s0:s0 + 1]), waits=list(w_sq))
            t3 = do("scalar", lambda e: e.activation(stats[:, s0 + 2:s0 + 3], stats[:, s0 + 1:s0 + 2],
                                                     AF.Copy, scale=cst[:, 2:3]))
            t4 = do("scalar", lambda e: e.activation(stats[:, s0 + 3:s0 + 4], stats[:, s0 + 2:s0 + 3], AF.Square))
            t5 = do("vector", lambda e: e.tensor_scalar(
                stats[:, s0 + 4:s0 + 5], stats[:, s0:s0 + 1], cst[:, 2:3], stats[:, s0 + 3:s0 + 4],
                ALU.mult, ALU.subtract), waits=[t4])
            t6 = do("scalar", lambda e: e.activation(stats[:, s0 + 5:s0 + 6], stats[:, s0 + 4:s0 + 5],
                                                     AF.Sqrt, bias=cst[:, 0:1]), waits=[t5])
            t7 = do("vector", lambda e: e.reciprocal(stats[:, s0 + 6:s0 + 7], stats[:, s0 + 5:s0 + 6]),
                    waits=[t6])
            return t7

        def emit_ln_stats(x_ap, scratch, w_first=()):
            t1 = do("vector", lambda e: e.scalar_tensor_tensor(
                scratch, x_ap, 1.0, x_ap, ALU.mult, ALU.mult,
                accum_out=stats[:, 0:1]), waits=list(w_first))
            t2 = do("vector", lambda e: e.tensor_reduce(stats[:, 1:2], x_ap, AX.X, ALU.add))
            t3 = do("scalar", lambda e: e.activation(stats[:, 2:3], stats[:, 1:2], AF.Copy,
                                                     scale=cst[:, 2:3]), waits=[t2])
            t4 = do("scalar", lambda e: e.activation(stats[:, 3:4], stats[:, 2:3], AF.Square))
            t5 = do("vector", lambda e: e.tensor_scalar(
                stats[:, 4:5], stats[:, 0:1], cst[:, 2:3], stats[:, 3:4],
                ALU.mult, ALU.subtract), waits=[t4, t1])
            t6 = do("scalar", lambda e: e.activation(stats[:, 5:6], stats[:, 4:5], AF.Sqrt,
                                                     bias=cst[:, 0:1]), waits=[t5])
            t7 = do("vector", lambda e: e.reciprocal(stats[:, 6:7], stats[:, 5:6]), waits=[t6])
            return t7

        t_x0 = do("sync", lambda e: e.dma_start(out=x_res[:, 0:1, :], in_=x_t[:, 0:1, :]),
                  sem="dx0", inc=16)
        t_x0b = do("sync", lambda e: e.dma_start(out=x_res[:, 1:8, :], in_=x_t[:, 1:8, :]),
                   sem="dx2", inc=16)
        t_x1 = do("sync", lambda e: e.dma_start(out=x_res[:, 8:16, :], in_=x_t[:, 8:16, :]),
                  sem="dx1", inc=16)

        do("gpsimd", lambda e: e.dma_start(out=kn_rep[:, :], in_=d_kn.unsqueeze(0).broadcast_to([128, 1024])),
           sem="dM", inc=16)
        do("gpsimd", lambda e: e.dma_start(out=bo_bf[0:1, :], in_=d_bo.unsqueeze(0)),
           sem="dM", inc=16)
        do("gpsimd", lambda e: e.dma_start(out=vw_eib, in_=d_ei.rearrange("n k -> (n k)").unsqueeze(0).broadcast_to([8, 2048])),
           sem="dM", inc=16)
        do("gpsimd", lambda e: e.dma_start(out=qn_fm[:, :], in_=d_qn.rearrange("(c p) -> p c", p=128), allow_slow_non_contiguous=True),
           sem="dM", inc=16)
        do("gpsimd", lambda e: e.dma_start(out=vw_ewb, in_=d_ew.rearrange("n k -> (n k)").unsqueeze(0).broadcast_to([8, 2048])),
           sem="dM", inc=16)
        t_misc = P.tok("dM")
        t_w1a_l = do("gpsimd", lambda e: e.dma_start(out=vw_w1a_stage, in_=d_w1a.rearrange("p (c f) -> p c f", c=4)),
                     sem="dA0", inc=16)
        wq_src = d_wqkv.rearrange("p (k f) -> p k f", k=DC)
        t_wqkv1 = do("gpsimd", lambda e: e.dma_start(out=vw_qkv[:, :, 1024:2048], in_=wq_src[:, :, 1024:2048]),
                     sem="dW", inc=16)
        do("gpsimd", lambda e: e.dma_start(out=vw_qkv[:, :, 0:1024], in_=wq_src[:, :, 0:1024]),
           sem="dW2", inc=16)
        t_wqkv2 = do("gpsimd", lambda e: e.dma_start(out=vw_qkv[:, :, 2048:3072], in_=wq_src[:, :, 2048:3072]),
                     sem="dW2", inc=16)
        do("gpsimd", lambda e: e.dma_start(out=wout_sb[:, :, :], in_=d_wout.rearrange("p (k f) -> p k f", k=DC)),
           sem="dF0", inc=16)
        t_f0_all = P.tok("dF0")
        t_qn = t_misc
        t_kn = t_misc

        do("gpsimd", lambda e: e.memset(ident[:, :], 0.0))
        t_ident = do("gpsimd", lambda e: e.affine_select(
            out=ident[:, :], in_=ident[:, :], compare_op=ALU.not_equal, fill=1.0,
            base=0, pattern=[[-1, 128]], channel_multiplier=1))
        do("gpsimd", lambda e: e.memset(ones_r[0:1, :], 1.0))
        t_ones = do("gpsimd", lambda e: e.memset(ones_c[:, :], 1.0))
        t_pc = do("gpsimd", lambda e: e.iota(pc8[:, :], [[0, 1]], channel_multiplier=1,
                                             allow_small_or_imprecise_dtypes=True))
        t_ksbd0 = do("gpsimd", lambda e: e.memset(ksum_bd.ap().rearrange("p a b c -> p (a b c)"), 0.0))
        t_kvbd0 = do("gpsimd", lambda e: e.memset(arena4.ap()[:, 0:2048], 0.0))
        do("gpsimd", lambda e: e.memset(cst[:, 0:1], EPS_LN))
        do("gpsimd", lambda e: e.memset(cst[:, 1:2], EPS_RMS))
        t_cst = do("gpsimd", lambda e: e.memset(cst[:, 2:3], 1.0 / 1024.0))
        t_cc0 = do("vector", lambda e: e.memset(vw_cc, 0.0))

        t_eif = do("vector", lambda e: e.tensor_copy(vw_eif, vw_eib), waits=[t_misc])
        t_m = do("vector", lambda e: e.tensor_single_scalar(vw_eif, vw_eif, pc8[:, 0:1], ALU.is_equal),
                 waits=[t_eif, t_pc])
        t_mw = do("vector", lambda e: e.tensor_tensor(vw_wexp, vw_eif, vw_ewb, ALU.mult),
                  waits=[t_m])
        t_wexp = do("vector", lambda e: e.tensor_reduce(
            vw_wexp[:, 0:1024], vw_wexp.rearrange("p (n k) -> p n k", k=2), AX.X, ALU.add),
            waits=[t_mw])
        t_gate_st = do("gpsimd", lambda e: e.dma_start(out=d_gate.ap(), in_=vw_wexp[:, 0:1024]),
                       sem="dF1", inc=16, waits=[t_wexp])
        if debug:
            do("gpsimd", lambda e: e.dma_start(out=dbg["wexp"], in_=vw_wexp[:, 0:1024]),
               sem="dO", inc=16, waits=[t_wexp])

        blocks = [(rc, fc) for rc in range(4) for fc in range(8)]
        t_w1aT = None
        for grp in range(4):
            chunk = blocks[grp * 8:(grp + 1) * 8]
            for idx, (rc, fc) in enumerate(chunk):
                w = [t_w1a_l, t_ident]
                if t_w1aT is not None:
                    w.append(t_w1aT)
                do("tensor", lambda e, rc=rc, fc=fc, idx=idx: e.transpose(
                    b0_bf[:, idx * 128:(idx + 1) * 128],
                    vw_w1a_stage[:, rc, fc * 128:(fc + 1) * 128], ident[:, :]),
                    waits=w if idx == 0 else (), mode="T")
            t_tp = P.tok("tensor")
            for idx, (rc, fc) in enumerate(chunk):
                do("vector", lambda e, rc=rc, fc=fc, idx=idx: e.tensor_copy(
                    w1aT[:, fc, rc * 128:(rc + 1) * 128], b0_bf[:, idx * 128:(idx + 1) * 128]),
                    waits=[t_tp] if idx == 0 else ())
            t_w1aT = P.tok("vector")

        GRP_COL = [1024, 1536, 0, 512, 2048, 2560]
        evac_qkv = {}
        tT_evac = {}
        last_qkv = {}
        kv_consumed = None
        kv_add = {}
        sq_done = None
        q_scaled = {}
        ln1_used = {}
        apply_tok = {}

        def emit_ln_statsA(i):
            t_xl = t_x0 if i == 0 else (t_x0b if i < 8 else t_x1)
            t1 = do("vector", lambda e, i=i: e.scalar_tensor_tensor(
                vw_attn_tm, x_res[:, i, :], 1.0, x_res[:, i, :], ALU.mult, ALU.mult,
                accum_out=stats[:, 0:1]), waits=[t_xl, t_cst])
            t2 = do("vector", lambda e, i=i: e.tensor_reduce(
                stats[:, 1:2], x_res[:, i, :], AX.X, ALU.add))
            t3 = do("scalar", lambda e: e.activation(stats[:, 2:3], stats[:, 1:2], AF.Copy,
                                                     scale=cst[:, 2:3]), waits=[t2])
            t4 = do("scalar", lambda e: e.activation(stats[:, 3:4], stats[:, 2:3], AF.Square))
            return t1, t4

        def emit_ln_statsB(t1, t4):
            t5 = do("vector", lambda e: e.tensor_scalar(
                stats[:, 4:5], stats[:, 0:1], cst[:, 2:3], stats[:, 3:4],
                ALU.mult, ALU.subtract), waits=[t4, t1])
            t6 = do("scalar", lambda e: e.activation(stats[:, 5:6], stats[:, 4:5], AF.Sqrt,
                                                     bias=cst[:, 0:1]), waits=[t5])
            return t6

        def emit_ln_apply(i, t6):
            t7 = do("vector", lambda e: e.reciprocal(stats[:, 6:7], stats[:, 5:6]), waits=[t6])
            w_ap = [t7]
            if i >= 1:
                w_ap.append(ln1_used[i - 1])
            return do("vector", lambda e, i=i: e.tensor_scalar(
                ln1_tm[:, :], x_res[:, i, :], stats[:, 2:3], stats[:, 6:7],
                ALU.subtract, ALU.mult), waits=w_ap)

        _t1, _t4 = emit_ln_statsA(0)
        _t6 = emit_ln_statsB(_t1, _t4)
        apply_tok[0] = emit_ln_apply(0, _t6)

        for i in range(NT):
            wT = [apply_tok[i], t_ident]
            if i == 0:
                wT.append(t_w1aT)
            else:
                wT.append(tT_evac[i - 1])
            for j in range(DC):
                do("tensor", lambda e, j=j: e.transpose(
                    b0_bf[:, j * 128:(j + 1) * 128], ln1_tm[:, j * 128:(j + 1) * 128], ident[:, :]),
                    waits=wT if j == 0 else (), mode="T")
            t_tp = P.tok("tensor")
            ln1_used[i] = t_tp
            wev = [t_tp]
            if i >= 2:
                wev.append(last_qkv[i - 2])
            t_ev = do("vector", lambda e, i=i: e.tensor_copy(
                ln1_fm[:, i % 2, :, :].rearrange("p c f -> p (c f)"), b0_bf[:, :]), waits=wev)
            tT_evac[i] = t_ev
            if debug and i == NT - 1:
                do("gpsimd", lambda e: e.dma_start(out=dbg["ln1_0"], in_=ln1_tm[:, :]),
                   sem="dO", inc=16, waits=[apply_tok[i]])

            ln_state = None
            for g in range(6):
                bk = 1 + (g % 2)
                col = GRP_COL[g]
                wq = [t_ev, t_wqkv1 if g < 2 else t_wqkv2]
                prev = (i, g - 2) if g >= 2 else ((i - 1, g + 4) if i >= 1 else None)
                if prev is not None:
                    wq.append(evac_qkv[prev])
                for k in range(DC):
                    do("tensor", lambda e, k=k, col=col, bk=bk, i=i: e.matmul(
                        bank(bk), ln1_fm[:, i % 2, k, :], vw_qkv[:, k, col:col + 512],
                        start=(k == 0), stop=(k == DC - 1)),
                        waits=wq if k == 0 else (), mode="P")
                t_mm = P.tok("tensor")
                if g == 5:
                    last_qkv[i] = t_mm
                if g < 2:
                    dest = k_tile[:, (g % 2) * 512:(g % 2) * 512 + 512]
                    wv = [t_mm]
                    if kv_consumed is not None and g == 0:
                        wv.append(kv_consumed)
                elif g < 4:
                    dest = vw_qstr[:, i, (g % 2) * 512:(g % 2) * 512 + 512]
                    wv = [t_mm]
                    if i in (8, 9) and g == 2:
                        wv.append(t_gate_st)
                    if i >= 1 and g == 2:
                        wv.append(q_scaled[i - 1])
                else:
                    dest = v_tile[:, (g % 2) * 512:(g % 2) * 512 + 512]
                    wv = [t_mm]
                    if kv_consumed is not None and g == 4:
                        wv.append(kv_consumed)
                t_e = do("vector", lambda e, dest=dest, bk=bk: e.tensor_copy(dest, bank(bk)), waits=wv)
                evac_qkv[(i, g)] = t_e

                if g == 1:
                    t_sqk = do("scalar", lambda e: e.activation(
                        vw_attn_fm.rearrange("p c f -> p (c f)"), k_tile[:, :], AF.Square,
                        accum_out=recips[:, 19:20]),
                        waits=[evac_qkv[(i, 0)], evac_qkv[(i, 1)]])
                    sq_done = t_sqk
                    t_rk0 = do("scalar", lambda e: e.activation(
                        recips[:, 20:21], recips[:, 19:20], AF.Sqrt,
                        scale=cst[:, 2:3], bias=cst[:, 1:2]))
                    t_kkn = do("gpsimd", lambda e: e.tensor_tensor(
                        k_tile[:, :], k_tile[:, :], kn_rep[:, :], ALU.mult),
                        waits=[t_sqk, t_kn])
                    t_rk = do("vector", lambda e: e.reciprocal(recips[:, 21:22], recips[:, 20:21]),
                              waits=[t_rk0])
                    t_krelu = do("scalar", lambda e: e.activation(
                        k_tile[:, :], k_tile[:, :], AF.Relu, scale=recips[:, 21:22]),
                        waits=[t_rk, t_kkn])
                if g == 2 and i < NT - 1:
                    ln_state = emit_ln_statsA(i + 1)
                if g == 3:
                    do("scalar", lambda e, i=i: e.activation(
                        vw_attn_fm.rearrange("p c f -> p (c f)"), vw_qstr[:, i, :], AF.Square,
                        accum_out=recips[:, 16:17]),
                        waits=[evac_qkv[(i, 2)], evac_qkv[(i, 3)]])
                    t_rq0 = do("scalar", lambda e: e.activation(
                        recips[:, 17:18], recips[:, 16:17], AF.Sqrt,
                        scale=cst[:, 2:3], bias=cst[:, 1:2]))
                    t_rq = do("vector", lambda e: e.reciprocal(recips[:, 18:19], recips[:, 17:18]),
                              waits=[t_rq0])
                    q_scaled[i] = do("scalar", lambda e, i=i: e.activation(
                        vw_qstr[:, i, :], vw_qstr[:, i, :], AF.Copy, scale=recips[:, 18:19]),
                        waits=[t_rq])
                if g == 4 and ln_state is not None:
                    ln_state = ln_state + (emit_ln_statsB(*ln_state),)

            if debug and i == NT - 1:
                do("gpsimd", lambda e: e.dma_start(out=dbg["k_0"], in_=k_tile[:, :]),
                   sem="dO", inc=16, waits=[t_krelu])
                do("gpsimd", lambda e: e.dma_start(out=dbg["v_0"], in_=v_tile[:, :]),
                   sem="dO", inc=16, waits=[evac_qkv[(i, 5)]])

            b = i // 8
            kvp = 3 if i % 2 == 0 else 5
            ksc = 16 * (i % 2)
            wkv = [t_krelu, evac_qkv[(i, 4)], evac_qkv[(i, 5)], t_ones]
            if i >= 1:
                wkv.append(kv_add[i - 1])
            for h in range(H):
                do("tensor", lambda e, h=h, kvp=kvp: e.matmul(
                    ps[0:64, kvp + h // 8, (h % 8) * 64:(h % 8) * 64 + 64],
                    k_tile[:, h * 64:(h + 1) * 64], v_tile[:, h * 64:(h + 1) * 64],
                    start=True, stop=True),
                    waits=wkv if h == 0 else (), mode="P")
            for h in range(H):
                do("tensor", lambda e, h=h, ksc=ksc: e.matmul(
                    ps[0:64, 7, ksc + h:ksc + h + 1],
                    k_tile[:, h * 64:(h + 1) * 64], ones_c[:, 0:1],
                    start=True, stop=True), mode="P")
            kv_consumed = P.tok("tensor")
            if ln_state is not None:
                apply_tok[i + 1] = emit_ln_apply(i + 1, ln_state[2])
            do("vector", lambda e, b=b, kvp=kvp: e.tensor_tensor(
                vw_cc[:, b * 1040:b * 1040 + 1024],
                ps[0:64, kvp:kvp + 2, :].rearrange("p b2 f -> p (b2 f)"),
                vw_cc[:, b * 1040:b * 1040 + 1024], ALU.add),
                waits=[kv_consumed, t_cc0])
            t_a2 = do("vector", lambda e, b=b, ksc=ksc: e.tensor_tensor(
                vw_cc[:, b * 1040 + 1024: b * 1040 + 1040],
                ps[0:64, 7, ksc:ksc + 16],
                vw_cc[:, b * 1040 + 1024: b * 1040 + 1040], ALU.add))
            kv_add[i] = t_a2

            if i == 7:
                t_ccstA = do("sync", lambda e: e.dma_start(out=d_ccinA.ap(), in_=vw_cc[:, 0:1040]),
                             sem="dCC", inc=16, waits=[kv_add[7]])
            if i == 8:
                t_ccA = do("gpsimd", lambda e: e.collective_compute(
                    "AllReduce", ALU.add,
                    replica_groups=[[0, 1], [2, 3], [4, 5], [6, 7]],
                    ins=[d_ccinA.ap().opt()], outs=[d_ccoutA.ap().opt()]),
                    sem="cc", inc=1, waits=[t_ccstA])
            if i == 9:
                cc_ksA = d_ccoutA.ap()[0:64, 1024:1040].rearrange("p (j two) -> p j two", two=2)
                do("sync", lambda e: e.dma_start(out=sk_sum[0:64, 0:8], in_=cc_ksA[:, :, 0],
                                                 allow_slow_non_contiguous=True),
                   sem="dCA", inc=16, waits=[t_ccA])
                do("sync", lambda e: e.dma_start(out=sk_sum[64:128, 0:8], in_=cc_ksA[:, :, 1],
                                                 allow_slow_non_contiguous=True),
                   sem="dCA", inc=16)

        qkv_done = last_qkv[NT - 1]

        if debug:
            do("gpsimd", lambda e: e.dma_start(out=dbg["cc_sb"], in_=vw_cc),
               sem="dO", inc=16, waits=[kv_add[NT - 1]])
            do("gpsimd", lambda e: e.dma_start(out=dbg["q_str"], in_=vw_qstr.rearrange("p t f -> p (t f)")),
               sem="dO", inc=16, waits=[q_scaled[NT - 1]])

        t_ccstB = do("sync", lambda e: e.dma_start(out=d_ccinB.ap(), in_=vw_cc[:, 1040:2080]),
                     sem="dSB", inc=16, waits=[kv_add[NT - 1]])
        t_ccB = do("gpsimd", lambda e: e.collective_compute(
            "AllReduce", ALU.add,
            replica_groups=[[0, 1], [2, 3], [4, 5], [6, 7]],
            ins=[d_ccinB.ap().opt()], outs=[d_ccoutB.ap().opt()]),
            sem="ccB", inc=1, waits=[t_ccstB])
        kvbd_even = arena4.ap()[0:64, 0:2048].rearrange("p (b2 j f) -> p b2 j f", b2=2, j=8)
        kvbd_odd = arena4.ap()[64:128, 0:2048].rearrange("p (b2 j f) -> p b2 j f", b2=2, j=8)
        cc_kvA = d_ccoutA.ap()[0:64, 0:1024].rearrange("p (j two f) -> p j two f", j=8, two=2)
        cc_kvB = d_ccoutB.ap()[0:64, 0:1024].rearrange("p (j two f) -> p j two f", j=8, two=2)
        do("gpsimd", lambda e: e.dma_start(out=kvbd_even[:, 0, :, 0:64], in_=cc_kvA[:, :, 0, :]),
           sem="dCA", inc=16, waits=[t_ccA, t_kvbd0])
        do("gpsimd", lambda e: e.dma_start(out=kvbd_odd[:, 0, :, 64:128], in_=cc_kvA[:, :, 1, :]),
           sem="dCA", inc=16)
        t_rbA = P.tok("dCA")
        cc_ksB = d_ccoutB.ap()[0:64, 1024:1040].rearrange("p (j two) -> p j two", two=2)
        do("sync", lambda e: e.dma_start(out=sk_sum[0:64, 8:16], in_=cc_ksB[:, :, 0],
                                         allow_slow_non_contiguous=True),
           sem="dCB", inc=16, waits=[t_ccB])
        do("sync", lambda e: e.dma_start(out=sk_sum[64:128, 8:16], in_=cc_ksB[:, :, 1],
                                         allow_slow_non_contiguous=True),
           sem="dCB", inc=16)
        do("gpsimd", lambda e: e.dma_start(out=kvbd_even[:, 1, :, 0:64], in_=cc_kvB[:, :, 0, :]),
           sem="dCB", inc=16, waits=[t_ccB])
        do("gpsimd", lambda e: e.dma_start(out=kvbd_odd[:, 1, :, 64:128], in_=cc_kvB[:, :, 1, :]),
           sem="dCB", inc=16)
        t_rbB = P.tok("dCB")
        for j in range(8):
            do("vector", lambda e, j=j: e.tensor_copy(
                ksum_bd[0:64, 0, j, 2 * j: 2 * j + 1], sk_sum[0:64, j: j + 1]),
                waits=[t_rbA, t_ksbd0] if j == 0 else ())
            do("vector", lambda e, j=j: e.tensor_copy(
                ksum_bd[64:128, 0, j, 2 * j + 1: 2 * j + 2], sk_sum[64:128, j: j + 1]))
        t_ksbdA = P.tok("vector")
        t_ksbdB = None

        b0_free = tT_evac[NT - 1]
        qfm_used = None
        attn_evac = None
        t_evac_t = None
        wout_evac = {}
        norm_read = None
        t_store = {}
        attn_div_prev = None
        last_wout = None
        attn_all_done = None
        pend = None

        P4O = [0, 1, 2, 3, 8, 9, 10, 11, 12, 13, 14, 15, 4, 5, 6, 7]
        l2T_done = None

        def flush_pend():
            nonlocal pend, b0_free, t_evac_t, l2T_done
            pi, t_ln2p = pend
            for j in range(DC):
                do("tensor", lambda e, j=j: e.transpose(
                    b0_bf[:, j * 128:(j + 1) * 128], ln1_tm[:, j * 128:(j + 1) * 128], ident[:, :]),
                    waits=[t_ln2p, b0_free] if j == 0 else (), mode="T")
            t_l2T = P.tok("tensor")
            l2T_done = t_l2T
            t_l2fm = do("vector", lambda e: e.tensor_copy(
                ln1_fm[:, 0, :, :].rearrange("p c f -> p (c f)"), b0_bf[:, :]), waits=[t_l2T])
            b0_free = t_l2fm
            wt_ = [t_l2fm, attn_evac, t_w1aT]
            for m in range(4):
                for k in range(DC):
                    do("tensor", lambda e, m=m, k=k: e.matmul(
                        bankpair(1)[:, m * 128:(m + 1) * 128], w1aT[:, k, m * 128:(m + 1) * 128],
                        ln1_fm[:, 0, k, :], start=(k == 0), stop=(k == DC - 1)),
                        waits=wt_ if (m == 0 and k == 0) else (), mode="P")
            t_tmm = P.tok("tensor")
            wev = [t_tmm]
            if pi == 0:
                wev.append(qkv_done)
            t_evac_t = do("vector", lambda e, pi=pi: e.tensor_copy(
                vw_t[:, :, pi * 128:(pi + 1) * 128],
                bankpair(1)[:, 0:512].rearrange("p (m n) -> p m n", n=128)), waits=wev)
            pend = None

        sA2 = h_sb.ap().rearrange("p a f -> p (a f)").rearrange("p (s g) -> p s g", s=2)
        w2a_src = d_w2a.rearrange("p (c f) -> p c f", c=4)
        t_sA = {}
        sA_used = {}

        def emit_sA_load(c8):
            slot = c8 % 2
            wL = [t_w1aT] if c8 < 2 else [sA_used[c8 - 2]]
            t_sA[c8] = do("gpsimd", lambda e, c8=c8, slot=slot: e.dma_start(
                out=sA2[:, slot, :].rearrange("p (rc f) -> p rc f", rc=4),
                in_=w2a_src[:, :, c8 * 512:(c8 + 1) * 512]),
                sem="dA0" if slot == 0 else "dA1", inc=16, waits=wL)

        emit_sA_load(0)
        emit_sA_load(1)

        for pos in range(NT):
            i = P4O[pos]
            b = i // 8
            if pos == 3:
                for j in range(8):
                    do("vector", lambda e, j=j: e.tensor_copy(
                        ksum_bd[0:64, 1, j, 2 * j: 2 * j + 1], sk_sum[0:64, 8 + j: 9 + j]),
                        waits=[t_rbB] if j == 0 else ())
                    do("vector", lambda e, j=j: e.tensor_copy(
                        ksum_bd[64:128, 1, j, 2 * j + 1: 2 * j + 2], sk_sum[64:128, 8 + j: 9 + j]))
                t_ksbdB = P.tok("vector")
                if debug:
                    do("gpsimd", lambda e: e.dma_start(out=dbg["kv_bd"], in_=vw_kvbd.rearrange("p b f -> p (b f)")),
                       sem="dO", inc=16, waits=[t_rbA, t_rbB])
                    do("gpsimd", lambda e: e.dma_start(out=dbg["sk"], in_=sk_sum[:, :]),
                       sem="dO", inc=16, waits=[t_rbB])
                    do("gpsimd", lambda e: e.dma_start(out=dbg["ksbd"], in_=ksum_bd.ap().rearrange("p a b c -> p (a b c)")),
                       sem="dO", inc=16, waits=[t_ksbdB])
            t_kvbd_b = t_rbA if b == 0 else t_rbB
            t_ksbd_b = t_ksbdA if b == 0 else t_ksbdB
            wT = [q_scaled[i], b0_free, t_kvbd_b, t_ksbd_b]
            if pos == 0:
                wT.append(qkv_done)
            if qfm_used is not None:
                wT.append(qfm_used)
            for j in range(DC):
                do("tensor", lambda e, i=i, j=j: e.transpose(
                    b0_bf[:, j * 128:(j + 1) * 128], vw_qstr[:, i, j * 128:(j + 1) * 128], ident[:, :]),
                    waits=wT if j == 0 else (), mode="T")
            t_qT = P.tok("tensor")
            for j in range(DC):
                do("vector", lambda e, j=j: e.tensor_scalar(
                    q_fm[:, j, :], b0_bf[:, j * 128:(j + 1) * 128],
                    qn_fm[:, j:j + 1], 0.0, ALU.mult, ALU.max),
                    waits=[t_qT, t_qn] if j == 0 else ())
            t_qfm = P.tok("vector")
            b0_free = t_qfm
            if debug and pos == NT - 1:
                do("gpsimd", lambda e: e.dma_start(out=dbg["qfm"], in_=q_fm.rearrange("p c f -> p (c f)")),
                   sem="dO", inc=16, waits=[t_qfm])

            wN = [t_qfm, t_ksbd_b]
            if norm_read is not None:
                wN.append(norm_read)
            if pos == 0:
                wN.append(kv_add[NT - 1])
            for j in range(DC):
                do("tensor", lambda e, j=j, b=b: e.matmul(
                    bank(7)[:, 0:16], q_fm[:, j, :], ksum_bd[:, b, j, :],
                    start=(j == 0), stop=(j == DC - 1)),
                    waits=wN if j == 0 else (), mode="P")
            t_norm = P.tok("tensor")
            wA = [t_kvbd_b]
            if t_evac_t is not None:
                wA.append(t_evac_t)
            if attn_evac is not None:
                wA.append(attn_evac)
            if pos == 0:
                wA += [evac_qkv[(NT - 1, 4)], evac_qkv[(NT - 1, 5)]]
            for j in range(DC):
                do("tensor", lambda e, j=j, b=b: e.matmul(
                    bankpair(1)[:, j * 128:(j + 1) * 128], q_fm[:, j, :], vw_kvbd[:, b, j * 128:(j + 1) * 128],
                    start=True, stop=True),
                    waits=wA if j == 0 else (), mode="P")
            t_attnmm = P.tok("tensor")
            qfm_used = t_attnmm
            if pos == NT - 1:
                attn_all_done = t_attnmm

            t_eps = do("scalar", lambda e: e.activation(
                recips[:, 16:32], bank(7)[:, 0:16], AF.Identity, bias=cst[:, 1:2]),
                waits=[t_norm])
            norm_read = t_eps
            t_rec = do("vector", lambda e: e.reciprocal(recips[:, 0:16], recips[:, 16:32]),
                       waits=[t_eps])
            if debug and pos == NT - 1:
                do("gpsimd", lambda e: e.dma_start(out=dbg["recip0"], in_=recips[:, 0:16]),
                   sem="dO", inc=16, waits=[t_rec])

            wD = [t_attnmm, t_rec]
            if pos == 0:
                wD.append(sq_done)
            if attn_div_prev is not None:
                wD.append(attn_div_prev)
            t_div = do("vector", lambda e: e.tensor_tensor(
                vw_attn_tm.rearrange("p (h d) -> p h d", d=64),
                bankpair(1).rearrange("p (h d) -> p h d", d=64),
                recips[:, 0:16].unsqueeze(-1).broadcast_to([128, 16, 64]),
                ALU.mult), waits=wD)
            attn_evac = t_div
            if debug and pos == NT - 1:
                do("gpsimd", lambda e: e.dma_start(out=dbg["attn0"], in_=vw_attn_tm),
                   sem="dO", inc=16, waits=[t_div])

            for j in range(DC):
                do("tensor", lambda e, j=j: e.transpose(
                    b0_bf[:, j * 128:(j + 1) * 128], vw_attn_tm[:, j * 128:(j + 1) * 128], ident[:, :]),
                    waits=[t_div, b0_free] if j == 0 else (), mode="T")
            t_aT = P.tok("tensor")
            t_afm = do("vector", lambda e: e.tensor_copy(
                vw_attn_fm.rearrange("p c f -> p (c f)"), b0_bf[:, :]),
                waits=[t_aT] + ([sq_done] if i == 0 else []))
            attn_div_prev = t_afm
            b0_free = t_afm

            wb = 3 if pos % 2 == 0 else 5
            wW = [t_afm, t_f0_all, t_misc, t_ones]
            if pos < 2:
                wW.append(kv_add[NT - 1])
            elif (pos - 2) in wout_evac:
                wW.append(wout_evac[pos - 2])
            for half in range(2):
                for j in range(DC):
                    do("tensor", lambda e, j=j, half=half, wb=wb: e.matmul(
                        bank(wb + half), vw_attn_fm[:, j, :], wout_sb[:, j, half * 512:(half + 1) * 512],
                        start=(j == 0), stop=False),
                        waits=wW if (j == 0 and half == 0) else (), mode="P")
                do("tensor", lambda e, half=half, wb=wb: e.matmul(
                    bank(wb + half), ones_r[0:1, :], bo_bf[0:1, half * 512:(half + 1) * 512],
                    start=False, stop=True), mode="P")
            t_wout = P.tok("tensor")
            last_wout = t_wout

            if pend is not None:
                flush_pend()

            wY = [t_wout]
            if pos == 0:
                wY += [t_ccstA, t_ccstB]
            t_y1 = do("vector", lambda e, wb=wb: e.tensor_scalar(
                vw_y1, bankpair(wb), 1.0, 0.0, ALU.mult, ALU.add,
                accum_out=stats[:, 1:2]), waits=wY)
            wout_evac[pos] = t_y1
            if debug and i == NT - 1:
                do("gpsimd", lambda e: e.dma_start(out=dbg["y1_0"], in_=vw_y1),
                   sem="dO", inc=16, waits=[t_y1])
            t_rs = emit_ln_stats2(vw_y1, [t_y1])
            t_lnA = do("vector", lambda e: e.tensor_scalar(
                vw_tmp, vw_y1, stats[:, 2:3], stats[:, 6:7], ALU.subtract, ALU.mult),
                waits=[t_rs])
            t_res = do("vector", lambda e, i=i: e.scalar_tensor_tensor(
                x_res[:, i, :], x_res[:, i, :], 1.0, vw_tmp, ALU.mult, ALU.add,
                accum_out=stats[:, 9:10]), waits=[t_lnA])

            if i >= 8:
                t_store[i] = do("sync", lambda e, i=i: e.dma_start(out=out_t[i], in_=x_res[:, i, :]),
                                sem="dO", inc=16, waits=[t_res])
            else:
                t_rs2 = emit_ln_stats2(x_res[:, i, :], [t_res], s0=8)
                w_l2 = [t_rs2]
                if pos == 0:
                    w_l2.append(ln1_used[NT - 1])
                else:
                    w_l2.append(l2T_done)
                t_ln2 = do("vector", lambda e, i=i: e.tensor_scalar(
                    ln1_tm[:, :], x_res[:, i, :], stats[:, 10:11], stats[:, 14:15], ALU.subtract, ALU.mult),
                    waits=w_l2)
                pend = (i, t_ln2)

            if pos >= 8:
                c8 = pos - 8
                slot = c8 % 2
                sAv = sA2[:, slot, :].rearrange("p (rc f) -> p rc f", rc=4)
                for halfg in range(2):
                    for idx in range(8):
                        n = halfg * 8 + idx
                        rc, fc = n // 4, n % 4
                        do("tensor", lambda e, sAv=sAv, rc=rc, fc=fc, idx=idx: e.transpose(
                            b0_bf[:, idx * 128:(idx + 1) * 128],
                            sAv[:, rc, fc * 128:(fc + 1) * 128], ident[:, :]),
                            waits=[t_sA[c8], b0_free] if idx == 0 else (), mode="T")
                    t_tp2 = P.tok("tensor")
                    wv2 = [t_tp2]
                    if halfg == 0:
                        wv2.append(P.tok("dO"))
                    for idx in range(8):
                        n = halfg * 8 + idx
                        rc, fc = n // 4, n % 4
                        do("vector", lambda e, c8=c8, rc=rc, fc=fc, idx=idx: e.tensor_copy(
                            vw_w2aT[:, c8 * 4 + fc, rc * 128:(rc + 1) * 128],
                            b0_bf[:, idx * 128:(idx + 1) * 128]),
                            waits=wv2 if idx == 0 else ())
                    b0_free = P.tok("vector")
                sA_used[c8] = b0_free
                if c8 + 2 < 8:
                    emit_sA_load(c8 + 2)

        if pend is not None:
            flush_pend()

        t_t_all = t_evac_t
        if debug:
            do("gpsimd", lambda e: e.dma_start(
                out=dbg["xr_mid"], in_=x_res.ap().rearrange("p t f -> p (t f)")),
                sem="dO", inc=16, waits=[P.tok("vector")])

        do("gpsimd", lambda e: e.dma_start(
            out=vw_w1b, in_=d_w1b.rearrange("p (c f) -> p c f", c=4)),
            sem="dF1", inc=16, waits=[qkv_done])
        do("gpsimd", lambda e: e.dma_start(
            out=vw_w2b, in_=d_w2b.rearrange("p (c f) -> p c f", c=4)),
            sem="dF1", inc=16)
        for cpair in range(4):
            for e2 in range(2):
                do("gpsimd", lambda e, cpair=cpair, e2=e2: e.dma_start(
                    out=vw_wrep[e2 * 64:(e2 + 1) * 64, cpair, :],
                    in_=d_gate.ap()[2 * cpair + e2: 2 * cpair + e2 + 1, :].broadcast_to([64, 1024])),
                    sem="dF1", inc=16,
                    waits=([P.tok("dF1"), last_wout, attn_all_done]
                           + ([P.tok("dO")] if debug else [])) if (cpair == 0 and e2 == 0) else ())
        t_f1_all = P.tok("dF1")
        if debug:
            do("gpsimd", lambda e: e.dma_start(out=dbg["wrep"], in_=vw_wrep.rearrange("p c f -> p (c f)")),
               sem="dO", inc=16, waits=[t_f1_all])
            do("gpsimd", lambda e: e.dma_start(out=dbg["t_sb"], in_=vw_t.rearrange("p c f -> p (c f)")),
               sem="dO", inc=16, waits=[t_t_all])

        t_w2aT = sA_used[7]

        u_evac = {}
        gel = {}
        umm = {}
        u_free = None
        first_h = True
        seq = [(c, nb, Fc) for c in range(4) for nb in range(2) for Fc in range(32)]

        def emit_u(c, nb, Fc):
            w = [gel[(c, nb, Fc)]]
            if Fc == 0:
                w.append(u_free if u_free is not None else wout_evac[15])
            u_start = (Fc == 0)
            u_stop = (Fc == 31)
            do("tensor", lambda e, c=c, Fc=Fc, s=u_start, st=u_stop: e.matmul(
                ps[0:64, 5, :], vw_w2aT[:, Fc, (2 * c) * 64:(2 * c) * 64 + 64],
                h_sb[:, Fc % 4, 0:512],
                start=s, stop=st),
                waits=w, mode="P")
            do("tensor", lambda e, c=c, Fc=Fc, s=u_start, st=u_stop: e.matmul(
                ps[64:128, 5, :], vw_w2aT[:, Fc, (2 * c + 1) * 64:(2 * c + 1) * 64 + 64],
                h_sb[:, Fc % 4, 512:1024],
                start=s, stop=st),
                mode="P")
            umm[(c, nb, Fc)] = P.tok("tensor")

        def emit_u_evac(c, nb):
            nonlocal_u = [umm[(c, nb, 31)], t_f1_all, qfm_used]
            u_evac[(c, nb)] = do("vector", lambda e, c=c, nb=nb: e.tensor_tensor(
                vw_u[:, c, nb * 512:(nb + 1) * 512], bank(5),
                vw_wrep[:, c, nb * 512:(nb + 1) * 512], ALU.mult), waits=nonlocal_u)
            return u_evac[(c, nb)]

        for s, (c, nb, Fc) in enumerate(seq):
            hp = 3 if Fc % 2 == 0 else 6
            w = [t_f1_all, t_w2aT]
            if first_h:
                w += [t_t_all, wout_evac[14], wout_evac[15], norm_read]
                first_h = False
            if (c, nb, Fc - 2) in gel:
                w.append(gel[(c, nb, Fc - 2)])
            elif Fc <= 1 and (c, nb) != (0, 0):
                prev = (c, nb - 1) if nb == 1 else (c - 1, 1)
                w.append(gel[prev + (30 + Fc,)])
            do("tensor", lambda e, c=c, nb=nb, Fc=Fc, hp=hp: e.matmul(
                bank(hp), vw_w1b[0:64, c, Fc * 128:(Fc + 1) * 128],
                vw_t[0:64, c, nb * 512:(nb + 1) * 512],
                start=True, stop=True),
                waits=w, mode="P")
            do("tensor", lambda e, c=c, nb=nb, Fc=Fc, hp=hp: e.matmul(
                bank(hp + 1), vw_w1b[64:128, c, Fc * 128:(Fc + 1) * 128],
                vw_t[64:128, c, nb * 512:(nb + 1) * 512],
                start=True, stop=True),
                mode="P")
            t_h = P.tok("tensor")
            wg = [t_h]
            if (c, nb, Fc - 4) in umm:
                wg.append(umm[(c, nb, Fc - 4)])
            gel[(c, nb, Fc)] = do("scalar", lambda e, Fc=Fc, hp=hp: e.activation(
                h_sb[:, Fc % 4, :], bankpair(hp), AF.Gelu), waits=wg)
            if s > 0:
                pc_, pn_, pf_ = seq[s - 1]
                emit_u(pc_, pn_, pf_)
                if pf_ == 31:
                    u_free = emit_u_evac(pc_, pn_)
        emit_u(*seq[-1])
        u_free = emit_u_evac(3, 1)
        t_u_all = P.tok("vector")
        if debug:
            do("gpsimd", lambda e: e.dma_start(out=dbg["u_sb"], in_=vw_u.rearrange("p c f -> p (c f)")),
               sem="dO", inc=16, waits=[t_u_all])

        y_prev = None
        for nt in range(NA):
            wy = [t_u_all]
            if y_prev is not None:
                wy.append(y_prev)
            for fh in range(2):
                for e_ in range(E):
                    half = e_ % 2
                    bk = (1 + fh) if half == 0 else (3 + fh)
                    s = (e_ < 2)
                    st = (e_ >= E - 2)
                    do("tensor", lambda en, e_=e_, fh=fh, bk=bk, nt=nt, s=s, st=st: en.matmul(
                        bank(bk), vw_u[(e_ % 2) * 64:(e_ % 2) * 64 + 64, e_ // 2, nt * 128:(nt + 1) * 128],
                        vw_w2b[(e_ % 2) * 64:(e_ % 2) * 64 + 64, e_ // 2, fh * 512:(fh + 1) * 512],
                        start=s, stop=st),
                        waits=wy if (e_ == 0 and fh == 0) else (), mode="P")
            t_ymm = P.tok("tensor")
            do("vector", lambda e: e.tensor_copy(vw_y1, bankpair(1)), waits=[t_ymm])
            t_c2 = do("vector", lambda e: e.scalar_tensor_tensor(
                vw_y1, bankpair(3), 1.0, vw_y1, ALU.mult, ALU.add,
                accum_out=stats[:, 1:2]))
            y_prev = t_c2
            t_rs = emit_ln_stats2(vw_y1, [t_c2])
            t_lnA = do("vector", lambda e: e.tensor_scalar(
                vw_tmp, vw_y1, stats[:, 2:3], stats[:, 6:7], ALU.subtract, ALU.mult),
                waits=[t_rs])
            t_res = do("vector", lambda e, nt=nt: e.scalar_tensor_tensor(
                x_res[:, nt, :], x_res[:, nt, :], 1.0, vw_tmp, ALU.mult, ALU.add), waits=[t_lnA])
            t_store[nt] = do("sync", lambda e, nt=nt: e.dma_start(out=out_t[nt], in_=x_res[:, nt, :]),
                             sem="dO", inc=16, waits=[t_res])

        P.wait_only("sync", [P.tok("dO")])

        block = ctx.enter_context(nc.Block())
        for eng_name in ENGINES:
            ops = P.stream[eng_name]
            if not ops:
                continue

            def body(eng, ops=ops, eng_name=eng_name):
                seen = {}
                last_mode = None
                for waits, fn, sem, inc, mode in ops:
                    if eng_name == "tensor" and mode is not None:
                        if last_mode is not None and mode != last_mode:
                            eng.drain()
                        last_mode = mode
                    for (ws, wv) in waits:
                        if ws == eng_name or wv <= 0:
                            continue
                        if seen.get(ws, 0) >= wv:
                            continue
                        eng.wait_ge(sems[ws], wv)
                        seen[ws] = wv
                    if fn is None:
                        continue
                    inst = fn(eng)
                    if inst is not None and sem is not None:
                        inst.then_inc(sems[sem], inc)

            getattr(block, eng_name)(body)

    return nc


_CACHE = {}


def _get_nc(debug=False):
    key = bool(debug)
    if key not in _CACHE:
        _CACHE[key] = build_nc(debug=key)
    return _CACHE[key]


def kernel(L0_attn_x, L0_attn_W_qkv, L0_attn_qn_w, L0_attn_kn_w, L0_attn_W_out,
           L0_attn_b_out, L0_ffn_expert_weights, L0_ffn_expert_indices,
           L0_ffn_w1_A, L0_ffn_w1_B, L0_ffn_w2_A, L0_ffn_w2_B, _debug=False):
    def pmajor(a):
        a = np.asarray(a, dtype=np.float32)
        r = a.shape[0] // 128
        return np.ascontiguousarray(
            a.reshape(r, 128, a.shape[1]).transpose(1, 0, 2).reshape(128, r * a.shape[1]))

    x = np.ascontiguousarray(np.asarray(L0_attn_x, dtype=np.float32))
    shared = dict(
        wqkv=pmajor(L0_attn_W_qkv),
        qn=np.ascontiguousarray(np.asarray(L0_attn_qn_w, dtype=np.float32)),
        kn=np.ascontiguousarray(np.asarray(L0_attn_kn_w, dtype=np.float32)),
        wout=pmajor(L0_attn_W_out),
        bo=np.ascontiguousarray(np.asarray(L0_attn_b_out, dtype=np.float32)),
        w1a=pmajor(L0_ffn_w1_A),
        w1b=pmajor(L0_ffn_w1_B),
        w2a=pmajor(L0_ffn_w2_A),
        w2b=pmajor(L0_ffn_w2_B),
    )
    ew = np.ascontiguousarray(np.asarray(L0_ffn_expert_weights, dtype=np.float32))
    ei = np.ascontiguousarray(np.asarray(L0_ffn_expert_indices, dtype=np.int32))

    in_maps = []
    for c in range(8):
        bA = c // 2
        bB = 4 + c // 2
        r0 = (c % 2) * 1024
        xc = np.concatenate([x[bA, r0:r0 + 1024], x[bB, r0:r0 + 1024]], axis=0)
        g0 = bA * 2048 + r0
        m = dict(shared)
        m["x"] = pmajor(xc)
        m["ew"] = np.ascontiguousarray(ew[g0:g0 + 1024])
        m["ei"] = np.ascontiguousarray(ei[g0:g0 + 1024])
        in_maps.append(m)

    nc = _get_nc(debug=_debug)
    res = run_bass_kernel_spmd(nc, in_maps, core_ids=list(range(8)))

    y = np.empty((8, 2048, 1024), dtype=np.float32)
    for c in range(8):
        o = np.asarray(res.results[c]["out"], dtype=np.float32)
        bA = c // 2
        bB = 4 + c // 2
        r0 = (c % 2) * 1024
        y[bA, r0:r0 + 1024] = o[0:1024]
        y[bB, r0:r0 + 1024] = o[1024:2048]
    if _debug:
        return y, [res.results[c] for c in range(8)]
    return y


# version 14
# speedup vs baseline: 1.1817x; 1.0401x over previous
import numpy as np
from contextlib import ExitStack

import concourse.bass as bass
import concourse.mybir as mybir
from concourse.bass_utils import run_bass_kernel_spmd

F32 = mybir.dt.float32
BF16 = mybir.dt.bfloat16
I32 = mybir.dt.int32
AF = mybir.ActivationFunctionType
ALU = mybir.AluOpType
AX = mybir.AxisListType

NT = 16
NA = 8
D = 1024
DC = 8
H = 16
FF = 4096
E = 8
EPS_LN = 1e-5
EPS_RMS = 1e-6
EPS_DEN = 1e-6

ENGINES = ("tensor", "vector", "scalar", "gpsimd", "sync")


class Prog:

    DMA_SEMS = ("dx0", "dx1", "dx2", "dW", "dW2", "dF0", "dF1", "dA0", "dA1", "dO", "dCC", "dM", "dCA", "dCB", "dSB")

    def __init__(self):
        self.stream = {e: [] for e in ENGINES}
        self.cnt = {}

    def do(self, eng, fn, waits=(), sem=None, inc=1, mode=None):
        for (ws, wv) in waits:
            if ws in self.DMA_SEMS:
                assert wv == self.cnt.get(ws, 0), \
                    f"prefix wait on dma sem {ws}: {wv} != {self.cnt.get(ws, 0)}"
        if sem is None:
            sem = eng
            inc = 1
        c = self.cnt.get(sem, 0) + inc
        self.cnt[sem] = c
        self.stream[eng].append((tuple(waits), fn, sem, inc, mode))
        return (sem, c)

    def drain(self):
        self.stream["tensor"].append(((), lambda e: e.drain(), None, 0, None))

    def wait_only(self, eng, waits):
        self.stream[eng].append((tuple(waits), None, None, 0, None))

    def tok(self, sem):
        return (sem, self.cnt.get(sem, 0))


def build_nc(debug=False):
    nc = bass.Bass("TRN2", target_bir_lowering=False)

    d_x = nc.dram_tensor("x", [128, 16384], F32, kind="ExternalInput").ap()
    d_wqkv = nc.dram_tensor("wqkv", [128, 24576], F32, kind="ExternalInput").ap()
    d_qn = nc.dram_tensor("qn", [1024], F32, kind="ExternalInput").ap()
    d_kn = nc.dram_tensor("kn", [1024], F32, kind="ExternalInput").ap()
    d_wout = nc.dram_tensor("wout", [128, 8192], F32, kind="ExternalInput").ap()
    d_bo = nc.dram_tensor("bo", [1024], F32, kind="ExternalInput").ap()
    d_ew = nc.dram_tensor("ew", [1024, 2], F32, kind="ExternalInput").ap()
    d_ei = nc.dram_tensor("ei", [1024, 2], I32, kind="ExternalInput").ap()
    d_w1a = nc.dram_tensor("w1a", [128, 4096], F32, kind="ExternalInput").ap()
    d_w1b = nc.dram_tensor("w1b", [128, 16384], F32, kind="ExternalInput").ap()
    d_w2a = nc.dram_tensor("w2a", [128, 16384], F32, kind="ExternalInput").ap()
    d_w2b = nc.dram_tensor("w2b", [128, 4096], F32, kind="ExternalInput").ap()
    d_out = nc.dram_tensor("out", [2048, 1024], F32, kind="ExternalOutput").ap()

    d_ccinA = nc.dram_tensor("cc_inA", [64, 1040], F32)
    d_ccoutA = nc.dram_tensor("cc_outA", [64, 1040], F32)
    d_ccinB = nc.dram_tensor("cc_inB", [64, 1040], F32)
    d_ccoutB = nc.dram_tensor("cc_outB", [64, 1040], F32)
    d_gate = nc.dram_tensor("gate_scr", [8, 1024], F32)

    dbg = {}
    if debug:
        for name, shape in [
            ("ln1_0", [128, 1024]), ("q_str", [128, 16384]), ("k_0", [128, 1024]),
            ("v_0", [128, 1024]), ("cc_sb", [64, 2080]), ("kv_bd", [128, 2048]),
            ("sk", [128, 16]), ("recip0", [128, 16]), ("attn0", [128, 1024]),
            ("y1_0", [128, 1024]), ("xr_mid", [128, 16384]), ("t_sb", [128, 4096]),
            ("u_sb", [128, 4096]), ("wrep", [128, 4096]), ("wexp", [8, 1024]),
            ("qfm", [128, 1024]), ("ksbd", [128, 256]),
        ]:
            dbg[name] = nc.dram_tensor("dbg_" + name, shape, F32, kind="ExternalOutput").ap()

    with ExitStack() as ctx:
        sb = lambda name, shape, dt: ctx.enter_context(nc.sbuf_tensor(name, shape, dt))

        x_res = sb("x_res", [128, NT, 1024], F32)
        arena1 = sb("arena1", [128, 12288], F32)
        arena2 = sb("arena2", [128, 8192], F32)
        wout_sb = sb("wout_sb", [128, DC, 1024], BF16)
        w1aT = sb("w1aT", [128, DC, 512], BF16)
        h_sb = sb("h_sb", [128, 4, 1024], BF16)
        arena4 = sb("arena4", [128, 4096], BF16)
        arena3 = sb("arena3", [128, 2080], F32)
        ksum_bd = sb("ksum_bd", [128, 2, DC, 16], BF16)
        sk_sum = sb("sk_sum", [128, 16], F32)
        ident = sb("ident", [128, 128], BF16)
        ln1_tm = sb("ln1_tm", [128, 1024], BF16)
        ln1_fm = sb("ln1_fm", [128, 2, DC, 128], BF16)
        k_tile = sb("k_tile", [128, 1024], BF16)
        v_tile = sb("v_tile", [128, 1024], BF16)
        kn_rep = sb("kn_rep", [128, 1024], BF16)
        qn_fm = sb("qn_fm", [128, 8], F32)
        bo_bf = sb("bo_bf", [1, 1024], BF16)
        ones_r = sb("ones_r", [1, 128], BF16)
        ones_c = sb("ones_c", [128, 1], BF16)
        recips = sb("recips", [128, 32], F32)
        stats = sb("stats", [128, 16], F32)
        pc8 = sb("pc8", [8, 1], F32)
        cst = sb("cst", [128, 4], F32)

        ps = ctx.enter_context(nc.psum_tensor("ps", [128, 8, 512], F32))

        a1_bf = arena1.ap().bitcast(BF16)
        vw_qkv = a1_bf.rearrange("p (k f) -> p k f", k=DC)
        vw_w1b = a1_bf[:, 0:16384].rearrange("p (c f) -> p c f", c=4)
        vw_w2b = a1_bf[:, 16384:20480].rearrange("p (c f) -> p c f", c=4)
        vw_t = a1_bf[:, 20480:24576].rearrange("p (c f) -> p c f", c=4)

        a2_bf = arena2.ap().bitcast(BF16)
        vw_qstr = a2_bf.rearrange("p (t f) -> p t f", t=NT)
        vw_u = a2_bf[:, 0:4096].rearrange("p (c f) -> p c f", c=4)
        vw_sA = a2_bf[:, 4096:8192].rearrange("p (s c f) -> p s c f", s=2, c=4)

        vw_w1a_stage = h_sb.ap()

        vw_kvbd = arena4.ap()[:, 0:2048].rearrange("p (b f) -> p b f", b=2)
        vw_attn_tm = arena4.ap()[:, 2048:3072]
        vw_attn_fm = arena4.ap()[:, 3072:4096].rearrange("p (c f) -> p c f", c=DC)
        vw_wrep = arena4.ap().rearrange("p (c f) -> p c f", c=4)

        vw_cc = arena3.ap()[0:64, 0:2080]
        vw_y1 = arena3.ap()[:, 0:1024]
        vw_tmp = arena3.ap()[:, 1024:2048]
        vw_ewb = arena2.ap()[0:8, 2048:4096]
        vw_eib = arena2.ap()[0:8, 4096:6144].bitcast(I32)
        vw_eif = arena2.ap()[0:8, 6144:8192]
        vw_wexp = arena2.ap()[0:8, 4096:6144]

        q_fm = ln1_fm.ap()[:, 1, :, :]

        vw_w2aT = (x_res.ap()[:, 8:16, :].rearrange("p t f -> p (t f)")
                   .bitcast(BF16).rearrange("p (c f) -> p c f", c=32))

        def bank(i):
            return ps[:, i, :]

        def bankpair(i):
            return ps[:, i:i + 2, :].rearrange("p b f -> p (b f)")

        b0_bf = ps[:, 0, :].bitcast(BF16)

        semnames = ["tensor", "vector", "scalar", "gpsimd", "cc",
                    "dx0", "dx1", "dx2", "dW", "dW2", "dF0", "dF1", "dA0", "dA1", "dO", "dCC", "dM",
                    "dCA", "dCB", "dSB", "ccB"]
        sems = {n: ctx.enter_context(nc.semaphore(n)) for n in semnames}

        P = Prog()
        do = P.do

        out_t = d_out.rearrange("(t p) f -> t p f", p=128)
        x_t = d_x.rearrange("p (t f) -> p t f", t=NT)

        def emit_ln_stats2(x_ap, w_sq=(), s0=0, scratch=None):
            if scratch is None:
                scratch = vw_tmp
            t_sq = do("scalar", lambda e: e.activation(
                scratch, x_ap, AF.Square, accum_out=stats[:, s0:s0 + 1]), waits=list(w_sq))
            t3 = do("scalar", lambda e: e.activation(stats[:, s0 + 2:s0 + 3], stats[:, s0 + 1:s0 + 2],
                                                     AF.Copy, scale=cst[:, 2:3]))
            t4 = do("scalar", lambda e: e.activation(stats[:, s0 + 3:s0 + 4], stats[:, s0 + 2:s0 + 3], AF.Square))
            t5 = do("vector", lambda e: e.tensor_scalar(
                stats[:, s0 + 4:s0 + 5], stats[:, s0:s0 + 1], cst[:, 2:3], stats[:, s0 + 3:s0 + 4],
                ALU.mult, ALU.subtract), waits=[t4])
            t6 = do("scalar", lambda e: e.activation(stats[:, s0 + 5:s0 + 6], stats[:, s0 + 4:s0 + 5],
                                                     AF.Sqrt, bias=cst[:, 0:1]), waits=[t5])
            t7 = do("vector", lambda e: e.reciprocal(stats[:, s0 + 6:s0 + 7], stats[:, s0 + 5:s0 + 6]),
                    waits=[t6])
            return t7

        def emit_ln_stats(x_ap, scratch, w_first=()):
            t1 = do("vector", lambda e: e.scalar_tensor_tensor(
                scratch, x_ap, 1.0, x_ap, ALU.mult, ALU.mult,
                accum_out=stats[:, 0:1]), waits=list(w_first))
            t2 = do("vector", lambda e: e.tensor_reduce(stats[:, 1:2], x_ap, AX.X, ALU.add))
            t3 = do("scalar", lambda e: e.activation(stats[:, 2:3], stats[:, 1:2], AF.Copy,
                                                     scale=cst[:, 2:3]), waits=[t2])
            t4 = do("scalar", lambda e: e.activation(stats[:, 3:4], stats[:, 2:3], AF.Square))
            t5 = do("vector", lambda e: e.tensor_scalar(
                stats[:, 4:5], stats[:, 0:1], cst[:, 2:3], stats[:, 3:4],
                ALU.mult, ALU.subtract), waits=[t4, t1])
            t6 = do("scalar", lambda e: e.activation(stats[:, 5:6], stats[:, 4:5], AF.Sqrt,
                                                     bias=cst[:, 0:1]), waits=[t5])
            t7 = do("vector", lambda e: e.reciprocal(stats[:, 6:7], stats[:, 5:6]), waits=[t6])
            return t7

        t_x0 = do("sync", lambda e: e.dma_start(out=x_res[:, 0:1, :], in_=x_t[:, 0:1, :]),
                  sem="dx0", inc=16)
        t_x0b = do("sync", lambda e: e.dma_start(out=x_res[:, 1:8, :], in_=x_t[:, 1:8, :]),
                   sem="dx2", inc=16)
        t_x1 = do("sync", lambda e: e.dma_start(out=x_res[:, 8:16, :], in_=x_t[:, 8:16, :]),
                  sem="dx1", inc=16)

        do("gpsimd", lambda e: e.dma_start(out=kn_rep[:, :], in_=d_kn.unsqueeze(0).broadcast_to([128, 1024])),
           sem="dM", inc=16)
        do("gpsimd", lambda e: e.dma_start(out=bo_bf[0:1, :], in_=d_bo.unsqueeze(0)),
           sem="dM", inc=16)
        do("gpsimd", lambda e: e.dma_start(out=vw_eib, in_=d_ei.rearrange("n k -> (n k)").unsqueeze(0).broadcast_to([8, 2048])),
           sem="dM", inc=16)
        do("gpsimd", lambda e: e.dma_start(out=qn_fm[:, :], in_=d_qn.rearrange("(c p) -> p c", p=128), allow_slow_non_contiguous=True),
           sem="dM", inc=16)
        do("gpsimd", lambda e: e.dma_start(out=vw_ewb, in_=d_ew.rearrange("n k -> (n k)").unsqueeze(0).broadcast_to([8, 2048])),
           sem="dM", inc=16)
        t_misc = P.tok("dM")
        t_w1a_l = do("gpsimd", lambda e: e.dma_start(out=vw_w1a_stage, in_=d_w1a.rearrange("p (c f) -> p c f", c=4)),
                     sem="dA0", inc=16)
        wq_src = d_wqkv.rearrange("p (k f) -> p k f", k=DC)
        t_wqkv1 = do("gpsimd", lambda e: e.dma_start(out=vw_qkv[:, :, 1024:2048], in_=wq_src[:, :, 1024:2048]),
                     sem="dW", inc=16)
        do("gpsimd", lambda e: e.dma_start(out=vw_qkv[:, :, 0:1024], in_=wq_src[:, :, 0:1024]),
           sem="dW2", inc=16)
        t_wqkv2 = do("gpsimd", lambda e: e.dma_start(out=vw_qkv[:, :, 2048:3072], in_=wq_src[:, :, 2048:3072]),
                     sem="dW2", inc=16)
        do("gpsimd", lambda e: e.dma_start(out=wout_sb[:, :, :], in_=d_wout.rearrange("p (k f) -> p k f", k=DC)),
           sem="dF0", inc=16)
        t_f0_all = P.tok("dF0")
        t_qn = t_misc
        t_kn = t_misc

        do("gpsimd", lambda e: e.memset(ident[:, :], 0.0))
        t_ident = do("gpsimd", lambda e: e.affine_select(
            out=ident[:, :], in_=ident[:, :], compare_op=ALU.not_equal, fill=1.0,
            base=0, pattern=[[-1, 128]], channel_multiplier=1))
        do("gpsimd", lambda e: e.memset(ones_r[0:1, :], 1.0))
        t_ones = do("gpsimd", lambda e: e.memset(ones_c[:, :], 1.0))
        t_pc = do("gpsimd", lambda e: e.iota(pc8[:, :], [[0, 1]], channel_multiplier=1,
                                             allow_small_or_imprecise_dtypes=True))
        t_ksbd0 = do("gpsimd", lambda e: e.memset(ksum_bd.ap().rearrange("p a b c -> p (a b c)"), 0.0))
        t_kvbd0 = do("gpsimd", lambda e: e.memset(arena4.ap()[:, 0:2048], 0.0))
        do("gpsimd", lambda e: e.memset(cst[:, 0:1], EPS_LN))
        do("gpsimd", lambda e: e.memset(cst[:, 1:2], EPS_RMS))
        t_cst = do("gpsimd", lambda e: e.memset(cst[:, 2:3], 1.0 / 1024.0))
        t_cc0 = do("vector", lambda e: e.memset(vw_cc, 0.0))

        t_eif = do("vector", lambda e: e.tensor_copy(vw_eif, vw_eib), waits=[t_misc])
        t_m = do("vector", lambda e: e.tensor_single_scalar(vw_eif, vw_eif, pc8[:, 0:1], ALU.is_equal),
                 waits=[t_eif, t_pc])
        t_mw = do("vector", lambda e: e.tensor_tensor(vw_wexp, vw_eif, vw_ewb, ALU.mult),
                  waits=[t_m])
        t_wexp = do("vector", lambda e: e.tensor_reduce(
            vw_wexp[:, 0:1024], vw_wexp.rearrange("p (n k) -> p n k", k=2), AX.X, ALU.add),
            waits=[t_mw])
        t_gate_st = do("gpsimd", lambda e: e.dma_start(out=d_gate.ap(), in_=vw_wexp[:, 0:1024]),
                       sem="dF1", inc=16, waits=[t_wexp])
        if debug:
            do("gpsimd", lambda e: e.dma_start(out=dbg["wexp"], in_=vw_wexp[:, 0:1024]),
               sem="dO", inc=16, waits=[t_wexp])

        blocks = [(rc, fc) for rc in range(4) for fc in range(8)]
        t_w1aT = None
        for grp in range(4):
            chunk = blocks[grp * 8:(grp + 1) * 8]
            for idx, (rc, fc) in enumerate(chunk):
                w = [t_w1a_l, t_ident]
                if t_w1aT is not None:
                    w.append(t_w1aT)
                do("tensor", lambda e, rc=rc, fc=fc, idx=idx: e.transpose(
                    b0_bf[:, idx * 128:(idx + 1) * 128],
                    vw_w1a_stage[:, rc, fc * 128:(fc + 1) * 128], ident[:, :]),
                    waits=w if idx == 0 else (), mode="T")
            t_tp = P.tok("tensor")
            for idx, (rc, fc) in enumerate(chunk):
                do("vector", lambda e, rc=rc, fc=fc, idx=idx: e.tensor_copy(
                    w1aT[:, fc, rc * 128:(rc + 1) * 128], b0_bf[:, idx * 128:(idx + 1) * 128]),
                    waits=[t_tp] if idx == 0 else ())
            t_w1aT = P.tok("vector")

        GRP_COL = [1024, 1536, 0, 512, 2048, 2560]
        evac_qkv = {}
        tT_evac = {}
        last_qkv = {}
        kv_consumed = None
        kv_add = {}
        sq_done = None
        q_scaled = {}
        ln1_used = {}
        apply_tok = {}

        def emit_ln_statsA(i):
            t_xl = t_x0 if i == 0 else (t_x0b if i < 8 else t_x1)
            t1 = do("vector", lambda e, i=i: e.scalar_tensor_tensor(
                vw_attn_tm, x_res[:, i, :], 1.0, x_res[:, i, :], ALU.mult, ALU.mult,
                accum_out=stats[:, 0:1]), waits=[t_xl, t_cst])
            t2 = do("vector", lambda e, i=i: e.tensor_reduce(
                stats[:, 1:2], x_res[:, i, :], AX.X, ALU.add))
            t3 = do("scalar", lambda e: e.activation(stats[:, 2:3], stats[:, 1:2], AF.Copy,
                                                     scale=cst[:, 2:3]), waits=[t2])
            t4 = do("scalar", lambda e: e.activation(stats[:, 3:4], stats[:, 2:3], AF.Square))
            return t1, t4

        def emit_ln_statsB(t1, t4):
            t5 = do("vector", lambda e: e.tensor_scalar(
                stats[:, 4:5], stats[:, 0:1], cst[:, 2:3], stats[:, 3:4],
                ALU.mult, ALU.subtract), waits=[t4, t1])
            t6 = do("scalar", lambda e: e.activation(stats[:, 5:6], stats[:, 4:5], AF.Sqrt,
                                                     bias=cst[:, 0:1]), waits=[t5])
            return t6

        def emit_ln_apply(i, t6):
            t7 = do("vector", lambda e: e.reciprocal(stats[:, 6:7], stats[:, 5:6]), waits=[t6])
            w_ap = [t7]
            if i >= 1:
                w_ap.append(ln1_used[i - 1])
            return do("vector", lambda e, i=i: e.tensor_scalar(
                ln1_tm[:, :], x_res[:, i, :], stats[:, 2:3], stats[:, 6:7],
                ALU.subtract, ALU.mult), waits=w_ap)

        _t1, _t4 = emit_ln_statsA(0)
        _t6 = emit_ln_statsB(_t1, _t4)
        apply_tok[0] = emit_ln_apply(0, _t6)

        for i in range(NT):
            wT = [apply_tok[i], t_ident]
            if i == 0:
                wT.append(t_w1aT)
            else:
                wT.append(tT_evac[i - 1])
            for j in range(DC):
                do("tensor", lambda e, j=j: e.transpose(
                    b0_bf[:, j * 128:(j + 1) * 128], ln1_tm[:, j * 128:(j + 1) * 128], ident[:, :]),
                    waits=wT if j == 0 else (), mode="T")
            t_tp = P.tok("tensor")
            ln1_used[i] = t_tp
            wev = [t_tp]
            if i >= 2:
                wev.append(last_qkv[i - 2])
            t_ev = do("vector", lambda e, i=i: e.tensor_copy(
                ln1_fm[:, i % 2, :, :].rearrange("p c f -> p (c f)"), b0_bf[:, :]), waits=wev)
            tT_evac[i] = t_ev
            if debug and i == NT - 1:
                do("gpsimd", lambda e: e.dma_start(out=dbg["ln1_0"], in_=ln1_tm[:, :]),
                   sem="dO", inc=16, waits=[apply_tok[i]])

            ln_state = None
            for g in range(6):
                bk = 1 + (g % 2)
                col = GRP_COL[g]
                wq = [t_ev, t_wqkv1 if g < 2 else t_wqkv2]
                prev = (i, g - 2) if g >= 2 else ((i - 1, g + 4) if i >= 1 else None)
                if prev is not None:
                    wq.append(evac_qkv[prev])
                for k in range(DC):
                    do("tensor", lambda e, k=k, col=col, bk=bk, i=i: e.matmul(
                        bank(bk), ln1_fm[:, i % 2, k, :], vw_qkv[:, k, col:col + 512],
                        start=(k == 0), stop=(k == DC - 1)),
                        waits=wq if k == 0 else (), mode="P")
                t_mm = P.tok("tensor")
                if g == 5:
                    last_qkv[i] = t_mm
                if g < 2:
                    dest = k_tile[:, (g % 2) * 512:(g % 2) * 512 + 512]
                    wv = [t_mm]
                    if kv_consumed is not None and g == 0:
                        wv.append(kv_consumed)
                elif g < 4:
                    dest = vw_qstr[:, i, (g % 2) * 512:(g % 2) * 512 + 512]
                    wv = [t_mm]
                    if i in (8, 9) and g == 2:
                        wv.append(t_gate_st)
                    if i >= 1 and g == 2:
                        wv.append(q_scaled[i - 1])
                else:
                    dest = v_tile[:, (g % 2) * 512:(g % 2) * 512 + 512]
                    wv = [t_mm]
                    if kv_consumed is not None and g == 4:
                        wv.append(kv_consumed)
                t_e = do("vector", lambda e, dest=dest, bk=bk: e.tensor_copy(dest, bank(bk)), waits=wv)
                evac_qkv[(i, g)] = t_e

                if g == 1:
                    t_sqk = do("scalar", lambda e: e.activation(
                        vw_attn_fm.rearrange("p c f -> p (c f)"), k_tile[:, :], AF.Square,
                        accum_out=recips[:, 19:20]),
                        waits=[evac_qkv[(i, 0)], evac_qkv[(i, 1)]])
                    sq_done = t_sqk
                    t_rk0 = do("scalar", lambda e: e.activation(
                        recips[:, 20:21], recips[:, 19:20], AF.Sqrt,
                        scale=cst[:, 2:3], bias=cst[:, 1:2]))
                    t_kkn = do("gpsimd", lambda e: e.tensor_tensor(
                        k_tile[:, :], k_tile[:, :], kn_rep[:, :], ALU.mult),
                        waits=[t_sqk, t_kn])
                    t_rk = do("vector", lambda e: e.reciprocal(recips[:, 21:22], recips[:, 20:21]),
                              waits=[t_rk0])
                    t_krelu = do("scalar", lambda e: e.activation(
                        k_tile[:, :], k_tile[:, :], AF.Relu, scale=recips[:, 21:22]),
                        waits=[t_rk, t_kkn])
                if g == 2 and i < NT - 1:
                    ln_state = emit_ln_statsA(i + 1)
                if g == 3:
                    do("scalar", lambda e, i=i: e.activation(
                        vw_attn_fm.rearrange("p c f -> p (c f)"), vw_qstr[:, i, :], AF.Square,
                        accum_out=recips[:, 16:17]),
                        waits=[evac_qkv[(i, 2)], evac_qkv[(i, 3)]])
                    t_rq0 = do("scalar", lambda e: e.activation(
                        recips[:, 17:18], recips[:, 16:17], AF.Sqrt,
                        scale=cst[:, 2:3], bias=cst[:, 1:2]))
                    t_rq = do("vector", lambda e: e.reciprocal(recips[:, 18:19], recips[:, 17:18]),
                              waits=[t_rq0])
                    q_scaled[i] = do("scalar", lambda e, i=i: e.activation(
                        vw_qstr[:, i, :], vw_qstr[:, i, :], AF.Copy, scale=recips[:, 18:19]),
                        waits=[t_rq])
                if g == 4 and ln_state is not None:
                    ln_state = ln_state + (emit_ln_statsB(*ln_state),)

            if debug and i == NT - 1:
                do("gpsimd", lambda e: e.dma_start(out=dbg["k_0"], in_=k_tile[:, :]),
                   sem="dO", inc=16, waits=[t_krelu])
                do("gpsimd", lambda e: e.dma_start(out=dbg["v_0"], in_=v_tile[:, :]),
                   sem="dO", inc=16, waits=[evac_qkv[(i, 5)]])

            b = i // 8
            kvp = 3 if i % 2 == 0 else 5
            ksc = 16 * (i % 2)
            wkv = [t_krelu, evac_qkv[(i, 4)], evac_qkv[(i, 5)], t_ones]
            if i >= 1:
                wkv.append(kv_add[i - 1])
            for h in range(H):
                do("tensor", lambda e, h=h, kvp=kvp: e.matmul(
                    ps[0:64, kvp + h // 8, (h % 8) * 64:(h % 8) * 64 + 64],
                    k_tile[:, h * 64:(h + 1) * 64], v_tile[:, h * 64:(h + 1) * 64],
                    start=True, stop=True),
                    waits=wkv if h == 0 else (), mode="P")
            for h in range(H):
                do("tensor", lambda e, h=h, ksc=ksc: e.matmul(
                    ps[0:64, 7, ksc + h:ksc + h + 1],
                    k_tile[:, h * 64:(h + 1) * 64], ones_c[:, 0:1],
                    start=True, stop=True), mode="P")
            kv_consumed = P.tok("tensor")
            if ln_state is not None:
                apply_tok[i + 1] = emit_ln_apply(i + 1, ln_state[2])
            do("vector", lambda e, b=b, kvp=kvp: e.tensor_tensor(
                vw_cc[:, b * 1040:b * 1040 + 1024],
                ps[0:64, kvp:kvp + 2, :].rearrange("p b2 f -> p (b2 f)"),
                vw_cc[:, b * 1040:b * 1040 + 1024], ALU.add),
                waits=[kv_consumed, t_cc0])
            t_a2 = do("vector", lambda e, b=b, ksc=ksc: e.tensor_tensor(
                vw_cc[:, b * 1040 + 1024: b * 1040 + 1040],
                ps[0:64, 7, ksc:ksc + 16],
                vw_cc[:, b * 1040 + 1024: b * 1040 + 1040], ALU.add))
            kv_add[i] = t_a2

            if i == 7:
                t_ccstA = do("sync", lambda e: e.dma_start(out=d_ccinA.ap(), in_=vw_cc[:, 0:1040]),
                             sem="dCC", inc=16, waits=[kv_add[7]])
            if i == 8:
                t_ccA = do("gpsimd", lambda e: e.collective_compute(
                    "AllReduce", ALU.add,
                    replica_groups=[[0, 1], [2, 3], [4, 5], [6, 7]],
                    ins=[d_ccinA.ap().opt()], outs=[d_ccoutA.ap().opt()]),
                    sem="cc", inc=1, waits=[t_ccstA])
            if i == 9:
                cc_ksA = d_ccoutA.ap()[0:64, 1024:1040].rearrange("p (j two) -> p j two", two=2)
                do("sync", lambda e: e.dma_start(out=sk_sum[0:64, 0:8], in_=cc_ksA[:, :, 0],
                                                 allow_slow_non_contiguous=True),
                   sem="dCA", inc=16, waits=[t_ccA])
                do("sync", lambda e: e.dma_start(out=sk_sum[64:128, 0:8], in_=cc_ksA[:, :, 1],
                                                 allow_slow_non_contiguous=True),
                   sem="dCA", inc=16)

        qkv_done = last_qkv[NT - 1]

        if debug:
            do("gpsimd", lambda e: e.dma_start(out=dbg["cc_sb"], in_=vw_cc),
               sem="dO", inc=16, waits=[kv_add[NT - 1]])
            do("gpsimd", lambda e: e.dma_start(out=dbg["q_str"], in_=vw_qstr.rearrange("p t f -> p (t f)")),
               sem="dO", inc=16, waits=[q_scaled[NT - 1]])

        t_ccstB = do("sync", lambda e: e.dma_start(out=d_ccinB.ap(), in_=vw_cc[:, 1040:2080]),
                     sem="dSB", inc=16, waits=[kv_add[NT - 1]])
        t_ccB = do("gpsimd", lambda e: e.collective_compute(
            "AllReduce", ALU.add,
            replica_groups=[[0, 1], [2, 3], [4, 5], [6, 7]],
            ins=[d_ccinB.ap().opt()], outs=[d_ccoutB.ap().opt()]),
            sem="ccB", inc=1, waits=[t_ccstB])
        kvbd_even = arena4.ap()[0:64, 0:2048].rearrange("p (b2 j f) -> p b2 j f", b2=2, j=8)
        kvbd_odd = arena4.ap()[64:128, 0:2048].rearrange("p (b2 j f) -> p b2 j f", b2=2, j=8)
        cc_kvA = d_ccoutA.ap()[0:64, 0:1024].rearrange("p (j two f) -> p j two f", j=8, two=2)
        cc_kvB = d_ccoutB.ap()[0:64, 0:1024].rearrange("p (j two f) -> p j two f", j=8, two=2)
        do("gpsimd", lambda e: e.dma_start(out=kvbd_even[:, 0, :, 0:64], in_=cc_kvA[:, :, 0, :]),
           sem="dCA", inc=16, waits=[t_ccA, t_kvbd0])
        do("gpsimd", lambda e: e.dma_start(out=kvbd_odd[:, 0, :, 64:128], in_=cc_kvA[:, :, 1, :]),
           sem="dCA", inc=16)
        t_rbA = P.tok("dCA")
        cc_ksB = d_ccoutB.ap()[0:64, 1024:1040].rearrange("p (j two) -> p j two", two=2)
        do("sync", lambda e: e.dma_start(out=sk_sum[0:64, 8:16], in_=cc_ksB[:, :, 0],
                                         allow_slow_non_contiguous=True),
           sem="dCB", inc=16, waits=[t_ccB])
        do("sync", lambda e: e.dma_start(out=sk_sum[64:128, 8:16], in_=cc_ksB[:, :, 1],
                                         allow_slow_non_contiguous=True),
           sem="dCB", inc=16)
        do("gpsimd", lambda e: e.dma_start(out=kvbd_even[:, 1, :, 0:64], in_=cc_kvB[:, :, 0, :]),
           sem="dCB", inc=16, waits=[t_ccB])
        do("gpsimd", lambda e: e.dma_start(out=kvbd_odd[:, 1, :, 64:128], in_=cc_kvB[:, :, 1, :]),
           sem="dCB", inc=16)
        t_rbB = P.tok("dCB")
        for j in range(8):
            do("vector", lambda e, j=j: e.tensor_copy(
                ksum_bd[0:64, 0, j, 2 * j: 2 * j + 1], sk_sum[0:64, j: j + 1]),
                waits=[t_rbA, t_ksbd0] if j == 0 else ())
            do("vector", lambda e, j=j: e.tensor_copy(
                ksum_bd[64:128, 0, j, 2 * j + 1: 2 * j + 2], sk_sum[64:128, j: j + 1]))
        t_ksbdA = P.tok("vector")
        t_ksbdB = None

        b0_free = tT_evac[NT - 1]
        qfm_used = None
        attn_evac = None
        t_evac_t = None
        wout_evac = {}
        norm_read = None
        t_store = {}
        attn_div_prev = None
        last_wout = None
        attn_all_done = None
        pend = None

        P4O = [0, 1, 2, 3, 8, 9, 10, 11, 12, 13, 14, 15, 4, 5, 6, 7]
        l2T_done = None

        pend_mid = None

        def flush_pend_T():
            nonlocal b0_free, l2T_done, pend_mid
            pi, t_ln2p = pend
            for j in range(DC):
                do("tensor", lambda e, j=j: e.transpose(
                    b0_bf[:, j * 128:(j + 1) * 128], ln1_tm[:, j * 128:(j + 1) * 128], ident[:, :]),
                    waits=[t_ln2p, b0_free] if j == 0 else (), mode="T")
            t_l2T = P.tok("tensor")
            l2T_done = t_l2T
            t_l2fm = do("vector", lambda e: e.tensor_copy(
                ln1_fm[:, 0, :, :].rearrange("p c f -> p (c f)"), b0_bf[:, :]), waits=[t_l2T])
            b0_free = t_l2fm
            pend_mid = t_l2fm

        def flush_pend_P():
            nonlocal pend, t_evac_t
            pi, t_ln2p = pend
            wt_ = [pend_mid, attn_evac, t_w1aT]
            for m in range(4):
                for k in range(DC):
                    do("tensor", lambda e, m=m, k=k: e.matmul(
                        bankpair(1)[:, m * 128:(m + 1) * 128], w1aT[:, k, m * 128:(m + 1) * 128],
                        ln1_fm[:, 0, k, :], start=(k == 0), stop=(k == DC - 1)),
                        waits=wt_ if (m == 0 and k == 0) else (), mode="P")
            t_tmm = P.tok("tensor")
            wev = [t_tmm]
            if pi == 0:
                wev.append(qkv_done)
            t_evac_t = do("vector", lambda e, pi=pi: e.tensor_copy(
                vw_t[:, :, pi * 128:(pi + 1) * 128],
                bankpair(1)[:, 0:512].rearrange("p (m n) -> p m n", n=128)), waits=wev)
            pend = None

        sA2 = h_sb.ap().rearrange("p a f -> p (a f)").rearrange("p (s g) -> p s g", s=2)
        w2a_src = d_w2a.rearrange("p (c f) -> p c f", c=4)
        t_sA = {}
        sA_used = {}

        def emit_sA_load(c8):
            slot = c8 % 2
            wL = [t_w1aT] if c8 < 2 else [sA_used[c8 - 2]]
            t_sA[c8] = do("gpsimd", lambda e, c8=c8, slot=slot: e.dma_start(
                out=sA2[:, slot, :].rearrange("p (rc f) -> p rc f", rc=4),
                in_=w2a_src[:, :, c8 * 512:(c8 + 1) * 512]),
                sem="dA0" if slot == 0 else "dA1", inc=16, waits=wL)

        emit_sA_load(0)
        emit_sA_load(1)

        def emit_qT(p):
            nonlocal b0_free
            ii = P4O[p]
            bb = ii // 8
            wT = [q_scaled[ii], b0_free,
                  t_rbA if bb == 0 else t_rbB,
                  t_ksbdA if bb == 0 else t_ksbdB]
            if p == 0:
                wT.append(qkv_done)
            if qfm_used is not None:
                wT.append(qfm_used)
            for j in range(DC):
                do("tensor", lambda e, ii=ii, j=j: e.transpose(
                    b0_bf[:, j * 128:(j + 1) * 128], vw_qstr[:, ii, j * 128:(j + 1) * 128], ident[:, :]),
                    waits=wT if j == 0 else (), mode="T")
            t_qT = P.tok("tensor")
            for j in range(DC):
                do("vector", lambda e, j=j: e.tensor_scalar(
                    q_fm[:, j, :], b0_bf[:, j * 128:(j + 1) * 128],
                    qn_fm[:, j:j + 1], 0.0, ALU.mult, ALU.max),
                    waits=[t_qT, t_qn] if j == 0 else ())
            t_q = P.tok("vector")
            b0_free = t_q
            return t_q

        t_qfm = emit_qT(0)

        for pos in range(NT):
            i = P4O[pos]
            b = i // 8
            if pos == 3:
                for j in range(8):
                    do("vector", lambda e, j=j: e.tensor_copy(
                        ksum_bd[0:64, 1, j, 2 * j: 2 * j + 1], sk_sum[0:64, 8 + j: 9 + j]),
                        waits=[t_rbB] if j == 0 else ())
                    do("vector", lambda e, j=j: e.tensor_copy(
                        ksum_bd[64:128, 1, j, 2 * j + 1: 2 * j + 2], sk_sum[64:128, 8 + j: 9 + j]))
                t_ksbdB = P.tok("vector")
                if debug:
                    do("gpsimd", lambda e: e.dma_start(out=dbg["kv_bd"], in_=vw_kvbd.rearrange("p b f -> p (b f)")),
                       sem="dO", inc=16, waits=[t_rbA, t_rbB])
                    do("gpsimd", lambda e: e.dma_start(out=dbg["sk"], in_=sk_sum[:, :]),
                       sem="dO", inc=16, waits=[t_rbB])
                    do("gpsimd", lambda e: e.dma_start(out=dbg["ksbd"], in_=ksum_bd.ap().rearrange("p a b c -> p (a b c)")),
                       sem="dO", inc=16, waits=[t_ksbdB])
            t_kvbd_b = t_rbA if b == 0 else t_rbB
            t_ksbd_b = t_ksbdA if b == 0 else t_ksbdB
            if debug and pos == NT - 1:
                do("gpsimd", lambda e: e.dma_start(out=dbg["qfm"], in_=q_fm.rearrange("p c f -> p (c f)")),
                   sem="dO", inc=16, waits=[t_qfm])

            wN = [t_qfm, t_ksbd_b]
            if norm_read is not None:
                wN.append(norm_read)
            if pos == 0:
                wN.append(kv_add[NT - 1])
            for j in range(DC):
                do("tensor", lambda e, j=j, b=b: e.matmul(
                    bank(7)[:, 0:16], q_fm[:, j, :], ksum_bd[:, b, j, :],
                    start=(j == 0), stop=(j == DC - 1)),
                    waits=wN if j == 0 else (), mode="P")
            t_norm = P.tok("tensor")
            wA = [t_kvbd_b]
            if t_evac_t is not None:
                wA.append(t_evac_t)
            if attn_evac is not None:
                wA.append(attn_evac)
            if pos == 0:
                wA += [evac_qkv[(NT - 1, 4)], evac_qkv[(NT - 1, 5)]]
            for j in range(DC):
                do("tensor", lambda e, j=j, b=b: e.matmul(
                    bankpair(1)[:, j * 128:(j + 1) * 128], q_fm[:, j, :], vw_kvbd[:, b, j * 128:(j + 1) * 128],
                    start=True, stop=True),
                    waits=wA if j == 0 else (), mode="P")
            t_attnmm = P.tok("tensor")
            qfm_used = t_attnmm
            if pos == NT - 1:
                attn_all_done = t_attnmm

            t_eps = do("scalar", lambda e: e.activation(
                recips[:, 16:32], bank(7)[:, 0:16], AF.Identity, bias=cst[:, 1:2]),
                waits=[t_norm])
            norm_read = t_eps
            t_rec = do("vector", lambda e: e.reciprocal(recips[:, 0:16], recips[:, 16:32]),
                       waits=[t_eps])
            if debug and pos == NT - 1:
                do("gpsimd", lambda e: e.dma_start(out=dbg["recip0"], in_=recips[:, 0:16]),
                   sem="dO", inc=16, waits=[t_rec])

            wD = [t_attnmm, t_rec]
            if pos == 0:
                wD.append(sq_done)
            if attn_div_prev is not None:
                wD.append(attn_div_prev)
            t_div = do("vector", lambda e: e.tensor_tensor(
                vw_attn_tm.rearrange("p (h d) -> p h d", d=64),
                bankpair(1).rearrange("p (h d) -> p h d", d=64),
                recips[:, 0:16].unsqueeze(-1).broadcast_to([128, 16, 64]),
                ALU.mult), waits=wD)
            attn_evac = t_div
            if debug and pos == NT - 1:
                do("gpsimd", lambda e: e.dma_start(out=dbg["attn0"], in_=vw_attn_tm),
                   sem="dO", inc=16, waits=[t_div])

            for j in range(DC):
                do("tensor", lambda e, j=j: e.transpose(
                    b0_bf[:, j * 128:(j + 1) * 128], vw_attn_tm[:, j * 128:(j + 1) * 128], ident[:, :]),
                    waits=[t_div, b0_free] if j == 0 else (), mode="T")
            t_aT = P.tok("tensor")
            t_afm = do("vector", lambda e: e.tensor_copy(
                vw_attn_fm.rearrange("p c f -> p (c f)"), b0_bf[:, :]),
                waits=[t_aT] + ([sq_done] if i == 0 else []))
            attn_div_prev = t_afm
            b0_free = t_afm

            if pend is not None:
                flush_pend_T()
            if pos >= 8:
                c8 = pos - 8
                slot = c8 % 2
                sAv = sA2[:, slot, :].rearrange("p (rc f) -> p rc f", rc=4)
                for halfg in range(2):
                    for idx in range(8):
                        n = halfg * 8 + idx
                        rc, fc = n // 4, n % 4
                        do("tensor", lambda e, sAv=sAv, rc=rc, fc=fc, idx=idx: e.transpose(
                            b0_bf[:, idx * 128:(idx + 1) * 128],
                            sAv[:, rc, fc * 128:(fc + 1) * 128], ident[:, :]),
                            waits=[t_sA[c8], b0_free] if idx == 0 else (), mode="T")
                    t_tp2 = P.tok("tensor")
                    wv2 = [t_tp2]
                    if halfg == 0:
                        wv2.append(P.tok("dO"))
                    for idx in range(8):
                        n = halfg * 8 + idx
                        rc, fc = n // 4, n % 4
                        do("vector", lambda e, c8=c8, rc=rc, fc=fc, idx=idx: e.tensor_copy(
                            vw_w2aT[:, c8 * 4 + fc, rc * 128:(rc + 1) * 128],
                            b0_bf[:, idx * 128:(idx + 1) * 128]),
                            waits=wv2 if idx == 0 else ())
                    b0_free = P.tok("vector")
                sA_used[c8] = b0_free
                if c8 + 2 < 8:
                    emit_sA_load(c8 + 2)
            if pos < NT - 1:
                t_qfm = emit_qT(pos + 1)

            wb = 3 if pos % 2 == 0 else 5
            wW = [t_afm, t_f0_all, t_misc, t_ones]
            if pos < 2:
                wW.append(kv_add[NT - 1])
            elif (pos - 2) in wout_evac:
                wW.append(wout_evac[pos - 2])
            for half in range(2):
                for j in range(DC):
                    do("tensor", lambda e, j=j, half=half, wb=wb: e.matmul(
                        bank(wb + half), vw_attn_fm[:, j, :], wout_sb[:, j, half * 512:(half + 1) * 512],
                        start=(j == 0), stop=False),
                        waits=wW if (j == 0 and half == 0) else (), mode="P")
                do("tensor", lambda e, half=half, wb=wb: e.matmul(
                    bank(wb + half), ones_r[0:1, :], bo_bf[0:1, half * 512:(half + 1) * 512],
                    start=False, stop=True), mode="P")
            t_wout = P.tok("tensor")
            last_wout = t_wout

            if pend is not None:
                flush_pend_P()

            wY = [t_wout]
            if pos == 0:
                wY += [t_ccstA, t_ccstB]
            t_y1 = do("vector", lambda e, wb=wb: e.tensor_scalar(
                vw_y1, bankpair(wb), 1.0, 0.0, ALU.mult, ALU.add,
                accum_out=stats[:, 1:2]), waits=wY)
            wout_evac[pos] = t_y1
            if debug and i == NT - 1:
                do("gpsimd", lambda e: e.dma_start(out=dbg["y1_0"], in_=vw_y1),
                   sem="dO", inc=16, waits=[t_y1])
            t_rs = emit_ln_stats2(vw_y1, [t_y1])
            t_lnA = do("vector", lambda e: e.tensor_scalar(
                vw_tmp, vw_y1, stats[:, 2:3], stats[:, 6:7], ALU.subtract, ALU.mult),
                waits=[t_rs])
            t_res = do("vector", lambda e, i=i: e.scalar_tensor_tensor(
                x_res[:, i, :], x_res[:, i, :], 1.0, vw_tmp, ALU.mult, ALU.add,
                accum_out=stats[:, 9:10]), waits=[t_lnA])

            if i >= 8:
                t_store[i] = do("sync", lambda e, i=i: e.dma_start(out=out_t[i], in_=x_res[:, i, :]),
                                sem="dO", inc=16, waits=[t_res])
            else:
                t_rs2 = emit_ln_stats2(x_res[:, i, :], [t_res], s0=8)
                w_l2 = [t_rs2]
                if pos == 0:
                    w_l2.append(ln1_used[NT - 1])
                else:
                    w_l2.append(l2T_done)
                t_ln2 = do("vector", lambda e, i=i: e.tensor_scalar(
                    ln1_tm[:, :], x_res[:, i, :], stats[:, 10:11], stats[:, 14:15], ALU.subtract, ALU.mult),
                    waits=w_l2)
                pend = (i, t_ln2)

        if pend is not None:
            flush_pend_T()
            flush_pend_P()

        t_t_all = t_evac_t
        if debug:
            do("gpsimd", lambda e: e.dma_start(
                out=dbg["xr_mid"], in_=x_res.ap().rearrange("p t f -> p (t f)")),
                sem="dO", inc=16, waits=[P.tok("vector")])

        do("gpsimd", lambda e: e.dma_start(
            out=vw_w1b, in_=d_w1b.rearrange("p (c f) -> p c f", c=4)),
            sem="dF1", inc=16, waits=[qkv_done])
        do("gpsimd", lambda e: e.dma_start(
            out=vw_w2b, in_=d_w2b.rearrange("p (c f) -> p c f", c=4)),
            sem="dF1", inc=16)
        for cpair in range(4):
            for e2 in range(2):
                do("gpsimd", lambda e, cpair=cpair, e2=e2: e.dma_start(
                    out=vw_wrep[e2 * 64:(e2 + 1) * 64, cpair, :],
                    in_=d_gate.ap()[2 * cpair + e2: 2 * cpair + e2 + 1, :].broadcast_to([64, 1024])),
                    sem="dF1", inc=16,
                    waits=([P.tok("dF1"), last_wout, attn_all_done]
                           + ([P.tok("dO")] if debug else [])) if (cpair == 0 and e2 == 0) else ())
        t_f1_all = P.tok("dF1")
        if debug:
            do("gpsimd", lambda e: e.dma_start(out=dbg["wrep"], in_=vw_wrep.rearrange("p c f -> p (c f)")),
               sem="dO", inc=16, waits=[t_f1_all])
            do("gpsimd", lambda e: e.dma_start(out=dbg["t_sb"], in_=vw_t.rearrange("p c f -> p (c f)")),
               sem="dO", inc=16, waits=[t_t_all])

        t_w2aT = sA_used[7]

        u_evac = {}
        gel = {}
        umm = {}
        u_free = None
        first_h = True
        seq = [(c, nb, Fc) for c in range(4) for nb in range(2) for Fc in range(32)]

        def emit_u(c, nb, Fc):
            w = [gel[(c, nb, Fc)]]
            if Fc == 0:
                w.append(u_free if u_free is not None else wout_evac[15])
            u_start = (Fc == 0)
            u_stop = (Fc == 31)
            do("tensor", lambda e, c=c, Fc=Fc, s=u_start, st=u_stop: e.matmul(
                ps[0:64, 5, :], vw_w2aT[:, Fc, (2 * c) * 64:(2 * c) * 64 + 64],
                h_sb[:, Fc % 4, 0:512],
                start=s, stop=st),
                waits=w, mode="P")
            do("tensor", lambda e, c=c, Fc=Fc, s=u_start, st=u_stop: e.matmul(
                ps[64:128, 5, :], vw_w2aT[:, Fc, (2 * c + 1) * 64:(2 * c + 1) * 64 + 64],
                h_sb[:, Fc % 4, 512:1024],
                start=s, stop=st),
                mode="P")
            umm[(c, nb, Fc)] = P.tok("tensor")

        def emit_u_evac(c, nb):
            nonlocal_u = [umm[(c, nb, 31)], t_f1_all, qfm_used]
            u_evac[(c, nb)] = do("vector", lambda e, c=c, nb=nb: e.tensor_tensor(
                vw_u[:, c, nb * 512:(nb + 1) * 512], bank(5),
                vw_wrep[:, c, nb * 512:(nb + 1) * 512], ALU.mult), waits=nonlocal_u)
            return u_evac[(c, nb)]

        for s, (c, nb, Fc) in enumerate(seq):
            hp = 3 if Fc % 2 == 0 else 6
            w = [t_f1_all, t_w2aT]
            if first_h:
                w += [t_t_all, wout_evac[14], wout_evac[15], norm_read]
                first_h = False
            if (c, nb, Fc - 2) in gel:
                w.append(gel[(c, nb, Fc - 2)])
            elif Fc <= 1 and (c, nb) != (0, 0):
                prev = (c, nb - 1) if nb == 1 else (c - 1, 1)
                w.append(gel[prev + (30 + Fc,)])
            do("tensor", lambda e, c=c, nb=nb, Fc=Fc, hp=hp: e.matmul(
                bank(hp), vw_w1b[0:64, c, Fc * 128:(Fc + 1) * 128],
                vw_t[0:64, c, nb * 512:(nb + 1) * 512],
                start=True, stop=True),
                waits=w, mode="P")
            do("tensor", lambda e, c=c, nb=nb, Fc=Fc, hp=hp: e.matmul(
                bank(hp + 1), vw_w1b[64:128, c, Fc * 128:(Fc + 1) * 128],
                vw_t[64:128, c, nb * 512:(nb + 1) * 512],
                start=True, stop=True),
                mode="P")
            t_h = P.tok("tensor")
            wg = [t_h]
            if (c, nb, Fc - 4) in umm:
                wg.append(umm[(c, nb, Fc - 4)])
            gel[(c, nb, Fc)] = do("scalar", lambda e, Fc=Fc, hp=hp: e.activation(
                h_sb[:, Fc % 4, :], bankpair(hp), AF.Gelu), waits=wg)
            if s > 0:
                pc_, pn_, pf_ = seq[s - 1]
                emit_u(pc_, pn_, pf_)
                if pf_ == 31:
                    u_free = emit_u_evac(pc_, pn_)
        emit_u(*seq[-1])
        u_free = emit_u_evac(3, 1)
        t_u_all = P.tok("vector")
        if debug:
            do("gpsimd", lambda e: e.dma_start(out=dbg["u_sb"], in_=vw_u.rearrange("p c f -> p (c f)")),
               sem="dO", inc=16, waits=[t_u_all])

        y_prev = None
        for nt in range(NA):
            wy = [t_u_all]
            if y_prev is not None:
                wy.append(y_prev)
            for fh in range(2):
                for e_ in range(E):
                    half = e_ % 2
                    bk = (1 + fh) if half == 0 else (3 + fh)
                    s = (e_ < 2)
                    st = (e_ >= E - 2)
                    do("tensor", lambda en, e_=e_, fh=fh, bk=bk, nt=nt, s=s, st=st: en.matmul(
                        bank(bk), vw_u[(e_ % 2) * 64:(e_ % 2) * 64 + 64, e_ // 2, nt * 128:(nt + 1) * 128],
                        vw_w2b[(e_ % 2) * 64:(e_ % 2) * 64 + 64, e_ // 2, fh * 512:(fh + 1) * 512],
                        start=s, stop=st),
                        waits=wy if (e_ == 0 and fh == 0) else (), mode="P")
            t_ymm = P.tok("tensor")
            do("vector", lambda e: e.tensor_copy(vw_y1, bankpair(1)), waits=[t_ymm])
            t_c2 = do("vector", lambda e: e.scalar_tensor_tensor(
                vw_y1, bankpair(3), 1.0, vw_y1, ALU.mult, ALU.add,
                accum_out=stats[:, 1:2]))
            y_prev = t_c2
            t_rs = emit_ln_stats2(vw_y1, [t_c2])
            t_lnA = do("vector", lambda e: e.tensor_scalar(
                vw_tmp, vw_y1, stats[:, 2:3], stats[:, 6:7], ALU.subtract, ALU.mult),
                waits=[t_rs])
            t_res = do("vector", lambda e, nt=nt: e.scalar_tensor_tensor(
                x_res[:, nt, :], x_res[:, nt, :], 1.0, vw_tmp, ALU.mult, ALU.add), waits=[t_lnA])
            t_store[nt] = do("sync", lambda e, nt=nt: e.dma_start(out=out_t[nt], in_=x_res[:, nt, :]),
                             sem="dO", inc=16, waits=[t_res])

        P.wait_only("sync", [P.tok("dO")])

        block = ctx.enter_context(nc.Block())
        for eng_name in ENGINES:
            ops = P.stream[eng_name]
            if not ops:
                continue

            def body(eng, ops=ops, eng_name=eng_name):
                seen = {}
                last_mode = None
                for waits, fn, sem, inc, mode in ops:
                    if eng_name == "tensor" and mode is not None:
                        if last_mode is not None and mode != last_mode:
                            eng.drain()
                        last_mode = mode
                    for (ws, wv) in waits:
                        if ws == eng_name or wv <= 0:
                            continue
                        if seen.get(ws, 0) >= wv:
                            continue
                        eng.wait_ge(sems[ws], wv)
                        seen[ws] = wv
                    if fn is None:
                        continue
                    inst = fn(eng)
                    if inst is not None and sem is not None:
                        inst.then_inc(sems[sem], inc)

            getattr(block, eng_name)(body)

    return nc


_CACHE = {}


def _get_nc(debug=False):
    key = bool(debug)
    if key not in _CACHE:
        _CACHE[key] = build_nc(debug=key)
    return _CACHE[key]


def kernel(L0_attn_x, L0_attn_W_qkv, L0_attn_qn_w, L0_attn_kn_w, L0_attn_W_out,
           L0_attn_b_out, L0_ffn_expert_weights, L0_ffn_expert_indices,
           L0_ffn_w1_A, L0_ffn_w1_B, L0_ffn_w2_A, L0_ffn_w2_B, _debug=False):
    def pmajor(a):
        a = np.asarray(a, dtype=np.float32)
        r = a.shape[0] // 128
        return np.ascontiguousarray(
            a.reshape(r, 128, a.shape[1]).transpose(1, 0, 2).reshape(128, r * a.shape[1]))

    x = np.ascontiguousarray(np.asarray(L0_attn_x, dtype=np.float32))
    shared = dict(
        wqkv=pmajor(L0_attn_W_qkv),
        qn=np.ascontiguousarray(np.asarray(L0_attn_qn_w, dtype=np.float32)),
        kn=np.ascontiguousarray(np.asarray(L0_attn_kn_w, dtype=np.float32)),
        wout=pmajor(L0_attn_W_out),
        bo=np.ascontiguousarray(np.asarray(L0_attn_b_out, dtype=np.float32)),
        w1a=pmajor(L0_ffn_w1_A),
        w1b=pmajor(L0_ffn_w1_B),
        w2a=pmajor(L0_ffn_w2_A),
        w2b=pmajor(L0_ffn_w2_B),
    )
    ew = np.ascontiguousarray(np.asarray(L0_ffn_expert_weights, dtype=np.float32))
    ei = np.ascontiguousarray(np.asarray(L0_ffn_expert_indices, dtype=np.int32))

    in_maps = []
    for c in range(8):
        bA = c // 2
        bB = 4 + c // 2
        r0 = (c % 2) * 1024
        xc = np.concatenate([x[bA, r0:r0 + 1024], x[bB, r0:r0 + 1024]], axis=0)
        g0 = bA * 2048 + r0
        m = dict(shared)
        m["x"] = pmajor(xc)
        m["ew"] = np.ascontiguousarray(ew[g0:g0 + 1024])
        m["ei"] = np.ascontiguousarray(ei[g0:g0 + 1024])
        in_maps.append(m)

    nc = _get_nc(debug=_debug)
    res = run_bass_kernel_spmd(nc, in_maps, core_ids=list(range(8)))

    y = np.empty((8, 2048, 1024), dtype=np.float32)
    for c in range(8):
        o = np.asarray(res.results[c]["out"], dtype=np.float32)
        bA = c // 2
        bB = 4 + c // 2
        r0 = (c % 2) * 1024
        y[bA, r0:r0 + 1024] = o[0:1024]
        y[bB, r0:r0 + 1024] = o[1024:2048]
    if _debug:
        return y, [res.results[c] for c in range(8)]
    return y
